# version 15
# baseline (speedup 1.0000x reference)
import math
import numpy as np
import ml_dtypes
import concourse.bass as bass
import concourse.mybir as mybir
from concourse.bass_utils import run_bass_kernel_spmd

F32 = mybir.dt.float32
BF16 = mybir.dt.bfloat16
I32 = mybir.dt.int32
AF = mybir.ActivationFunctionType
ALU = mybir.AluOpType
AX = mybir.AxisListType
BF = ml_dtypes.bfloat16

NCORE = 8
B, T, D = 2, 4096, 2048
CH = 512
OWN = 1024
LOC = 2048
KCD = 16
H, HD = 16, 128
NEG = -1e30
NE, IMOE, ISH = 8, 1408, 5632
IFF = 8192


class Sem:
    __slots__ = ("h", "n", "name")

    def __init__(self, nc, name):
        self.h = nc.alloc_semaphore(name)
        self.n = 0
        self.name = name


class Bld:
    ENG = ("tensor", "vector", "scalar", "gpsimd", "sync")

    def __init__(self, nc):
        self.nc = nc
        self.q = {e: [] for e in self.ENG}
        self.sem = {e: Sem(nc, "s_" + e) for e in self.ENG}
        self.dsems = {}
        self.eng_dma = {e: {} for e in self.ENG}
        self.waited = {e: {} for e in self.ENG}
        self.nops = 0
        self.bank = {}
        self.bg_dmas = []

    def bg(self, mk):
        self.bg_dmas.append(mk)

    def drip(self):
        if self.bg_dmas:
            mk = self.bg_dmas.pop(0)
            return self.do("gpsimd", mk, dma=self.S("bgc"))
        return None

    def S(self, name):
        if name not in self.dsems:
            self.dsems[name] = Sem(self.nc, "d_" + name)
        return self.dsems[name]

    CHAIN = ("vector", "scalar", "gpsimd")

    def do(self, eng, mk, after=(), inc=False, dma=None, force_own=False):
        s = dma if dma is not None else self.sem[eng]
        es = self.sem[eng]
        waits = []
        wmap = self.waited[eng]
        after = list(after)
        if dma is None and (eng in self.CHAIN or force_own) and es.n > 0:
            after.insert(0, (es, es.n))
        for t in after:
            if t is None:
                continue
            ps, v = t
            if ps is es and not (dma is None and (eng in self.CHAIN or force_own)):
                continue
            w = wmap.get(ps.name)
            if w is None or w < v:
                waits.append((ps, v))
                wmap[ps.name] = v
        amt = 16 if dma is not None else 1
        if dma is None and eng in self.CHAIN:
            inc = True
        tok = None
        if inc or dma is not None:
            s.n += amt
            tok = (s, s.n)
        if dma is not None:
            self.eng_dma[eng][s.name] = (s, s.n)
        self.nops += 1 + len(waits)
        doinc = inc or dma is not None

        def f(eo, waits=waits, mk=mk, doinc=doinc, s=s, amt=amt):
            for ps, v in waits:
                eo.wait_ge(ps.h, v)
            ins = mk(eo)
            if doinc:
                ins.then_inc(s.h, amt)
        self.q[eng].append(f)
        return tok

    def barrier(self):
        toks = {}
        for e in self.ENG:
            aft = list(self.eng_dma[e].values())
            toks[e] = self.do(e, lambda eo: eo.nop(), after=aft, inc=True,
                              force_own=True)
            self.eng_dma[e] = {}
        for e in self.ENG:
            self.do(e, lambda eo: eo.nop(),
                    after=[toks[e2] for e2 in self.ENG if e2 != e])
        self.bank = {}
        return toks

    def finalize(self):
        with self.nc.Block() as block:
            for e in self.ENG:
                lst = self.q[e]
                getattr(block, e)(lambda eo, lst=lst: [f(eo) for f in lst])


def dap(t, offset, dims):
    return bass.AP(t, offset, [list(d) for d in dims])


class Ctx:
    pass


def alloc_common(nc, bld):
    c = Ctx()
    c.nc, c.b = nc, bld
    A = nc.alloc_sbuf_tensor
    c.x = A("x_fm", [128, 16 * 1024], F32)
    c.bfa = A("bfa", [128, 16 * 1024], BF16)
    c.bfb = A("bfb", [128, 16 * 1024], BF16)
    c.bfc = A("bfc", [128, 16 * 1024], BF16)
    c.bfd = A("bfd", [128, 8 * 1024], BF16)
    c.wb = A("wb", [128, 8 * 1024], BF16)
    c.fs = A("fs", [128, 2 * 1024], F32)
    c.cst = A("cst", [128, 16], F32)
    c.sc = A("sc", [128, 192], F32)
    P = nc.alloc_psum_tensor
    c.pA = P("pA", [128, 1024], F32)
    c.pB = P("pB", [128, 1024], F32)
    c.pC = P("pC", [128, 512], F32)
    c.pD = P("pD", [128, 512], F32)
    c.pE = P("pE", [128, 512], F32)
    c.pF = P("pF", [128, 512], F32)
    c.consts = {}
    return c


def reg_const(c, val):
    if val not in c.consts:
        i = len(c.consts)
        assert i < 16
        c.consts[val] = i
        c.nc.const_aps.register(F32, float(val), c.cst[:, i:i + 1])


def emit_const_init(c):
    b = c.b
    for val, i in list(c.consts.items()):
        b.do("vector", lambda eo, v=val, i=i: eo.memset(c.cst[:, i:i + 1], v))
    return b.do("vector", lambda eo: eo.engine_nop(), inc=True)


def puse(c, slot):
    return c.b.bank.get(slot)


def prel(c, slot, tok):
    c.b.bank[slot] = tok


def emit_p0(c, io):
    b = c.b
    stage = c.bfc[:].bitcast(F32)
    h_r = io.h_in[:].rearrange("(i p) f -> i p f", p=128)
    reg_const(c, 1e-5)
    slot_done = [None, None]
    prev = {}
    P0SC = [(c.sc[:, 64 + 16 * par + k:65 + 16 * par + k] for k in range(7))
            for par in range(2)]
    for i in range(16):
        par = i % 2
        sl = stage[:, par * 2048:par * 2048 + 2048]
        ld = b.do("sync", lambda eo, sl=sl, i=i: eo.dma_start(sl, h_r[i]),
                  after=[slot_done[par]], dma=b.S("st%d" % par))
        base = 64 + 16 * par
        s_col = c.sc[:, base:base + 1]
        q_col = c.sc[:, base + 1:base + 2]
        mu = c.sc[:, base + 2:base + 3]
        ex2 = c.sc[:, base + 3:base + 4]
        var = c.sc[:, base + 4:base + 5]
        r = c.sc[:, base + 5:base + 6]
        bb = c.sc[:, base + 6:base + 7]
        xsq = c.bfd[:, par * 4096:par * 4096 + 2048]
        xn_tm = c.bfd[:, par * 4096 + 2048:par * 4096 + 4096]
        t1 = b.do("scalar", lambda eo, sl=sl, xsq=xsq, q_col=q_col:
                  eo.activation(xsq, sl, AF.Square, accum_out=q_col),
                  after=(ld, prev.get("t3%d" % par)), inc=True)
        t3 = b.do("scalar", lambda eo, sl=sl, xsq=xsq, s_col=s_col:
                  eo.activation(xsq, sl, AF.Identity, accum_out=s_col),
                  after=(t1,), inc=True)
        prev["t3%d" % par] = t3
        b.do("vector", lambda eo, mu=mu, s_col=s_col: eo.tensor_scalar(
            mu, s_col, 1.0 / 2048.0, None, ALU.mult), after=(t3,))
        b.do("vector", lambda eo, ex2=ex2, q_col=q_col: eo.tensor_scalar(
            ex2, q_col, 1.0 / 2048.0, None, ALU.mult), after=(t1,))
        b.do("vector", lambda eo, var=var, mu=mu: eo.scalar_tensor_tensor(
            var, mu, -1.0, mu, ALU.mult, ALU.mult),
            after=[prev.get("t8%d" % par)])
        t7 = b.do("vector", lambda eo, var=var, ex2=ex2: eo.tensor_tensor(
            var, var, ex2, ALU.add), inc=True)
        t8 = b.do("scalar", lambda eo, r=r, var=var: eo.activation(
            r, var, AF.Sqrt, bias=1e-5), after=(t7, prev.get("t11%d" % par)),
            inc=True)
        prev["t8%d" % par] = t8
        t9 = b.do("vector", lambda eo, r=r: eo.reciprocal(r, r), after=(t8,),
                  inc=True)
        b.do("vector", lambda eo, bb=bb, mu=mu, r=r: eo.scalar_tensor_tensor(
            bb, mu, -1.0, r, ALU.mult, ALU.mult))
        t11 = b.do("vector", lambda eo, sl=sl, xn_tm=xn_tm, r=r, bb=bb:
                   eo.tensor_scalar(xn_tm, sl, r, bb, ALU.mult, ALU.add),
                   after=[prev.get("lastT%d" % par)], inc=True)
        prev["t11%d" % par] = t11
        tb = i // 4
        own = tb in (1, 3)
        if own:
            col = (i - 4) * 128 if tb == 1 else 512 + (i - 12) * 128
            dst = c.bfa
        else:
            col = i * 128 if tb == 0 else 512 + (i - 8) * 128
            dst = c.bfb
        lastT = None
        evs = []
        for kc in range(16):
            if par == 0:
                bslot = ("pC", "pD")[kc % 2]
                pt = getattr(c, bslot)[:].bitcast(BF16)[:, 0:128]
            else:
                bslot = "pAb" + str(kc % 2)
                pt = c.pA[:].bitcast(BF16)[:, (kc % 2) * 1024:
                                           (kc % 2) * 1024 + 128]
            tt = b.do("tensor", lambda eo, pt=pt, kc=kc, xn_tm=xn_tm: eo.transpose(
                pt, xn_tm[:, kc * 128:(kc + 1) * 128], io.identb),
                after=(t11, puse(c, bslot)), inc=True)
            lastT = tt
            ev = b.do("vector", lambda eo, pt=pt, dst=dst, kc=kc, col=col:
                      eo.tensor_copy(dst[:, kc * 1024 + col:kc * 1024 + col + 128], pt),
                      after=(tt,), inc=True)
            prel(c, bslot, ev)
            evs.append(ev)
            if own:
                if par == 0:
                    fslot = ("pE", "pF")[kc % 2]
                    pf = getattr(c, fslot)[:, 0:128]
                else:
                    fslot = "pBb" + str(kc % 2)
                    pf = c.pB[:, (kc % 2) * 512:(kc % 2) * 512 + 128]
                tf = b.do("tensor", lambda eo, pf=pf, kc=kc, sl=sl: eo.transpose(
                    pf, sl[:, kc * 128:(kc + 1) * 128], io.identf),
                    after=(t11, puse(c, fslot)), inc=True)
                lastT = tf
                ev2 = b.do("vector", lambda eo, pf=pf, kc=kc, col=col:
                           eo.tensor_copy(c.x[:, kc * 1024 + col:kc * 1024 + col + 128], pf),
                           after=(tf,), inc=True)
                prel(c, fslot, ev2)
                evs.append(ev2)
        prev["lastT%d" % par] = lastT
        slot_done[par] = evs[-1]
    b.barrier()


class IO:
    pass


def decl_common_inputs(nc, io):
    DT = nc.dram_tensor
    io.identb = _sb_const(nc, "identb", [128, 128], BF16)
    io.identf = _sb_const(nc, "identf", [128, 128], F32)


_SB_CONST_INPUTS = []


def _sb_const(nc, name, shape, dtype, sbuf_dtype=None):
    sd = sbuf_dtype or dtype
    d = nc.dram_tensor(name, shape, dtype, kind="ExternalInput")
    s = nc.alloc_sbuf_tensor("sb_" + name, shape, sd)
    _SB_CONST_INPUTS.append((d, s, sd != dtype))
    return s[:]


def emit_const_loads(c):
    b = c.b
    t = t2 = None
    for d, s, cast in _SB_CONST_INPUTS:
        if cast:
            t2 = b.do("gpsimd", lambda eo, d=d, s=s: eo.dma_start(s[:], d[:]),
                      dma=b.S("cl2"))
        else:
            t = b.do("sync", lambda eo, d=d, s=s: eo.dma_start(s[:], d[:]),
                     dma=b.S("cl"))
    if t is not None:
        b.do("sync", lambda eo: eo.nop(), after=(t,))
    if t2 is not None:
        b.do("gpsimd", lambda eo: eo.nop(), after=(t2,))
    _SB_CONST_INPUTS.clear()


def build_p0_test():
    nc = bass.Bass("TRN2", target_bir_lowering=False, debug=False)
    bld = Bld(nc)
    io = IO()
    io.h_in = nc.dram_tensor("h_in", [LOC, D], F32, kind="ExternalInput")
    decl_common_inputs(nc, io)
    o_x = nc.dram_tensor("o_x", [D, OWN], F32, kind="ExternalOutput")
    o_xn = nc.dram_tensor("o_xn", [D, OWN], BF16, kind="ExternalOutput")
    o_xh = nc.dram_tensor("o_xh", [D, OWN], BF16, kind="ExternalOutput")
    c = alloc_common(nc, bld)
    reg_const(c, 1e-5)
    emit_const_init(c)
    emit_const_loads(c)
    bld.barrier()
    emit_p0(c, io)
    so = bld.sem["sync"]
    b = bld
    for kc in range(16):
        b.do("sync", lambda eo, kc=kc: eo.dma_start(
            o_x[:].rearrange("(k p) t -> k p t", p=128)[kc],
            c.x[:, kc * 1024:(kc + 1) * 1024]), dma=b.S("o0"))
        b.do("sync", lambda eo, kc=kc: eo.dma_start(
            o_xn[:].rearrange("(k p) t -> k p t", p=128)[kc],
            c.bfa[:, kc * 1024:(kc + 1) * 1024]), dma=b.S("o0"))
        t = b.do("sync", lambda eo, kc=kc: eo.dma_start(
            o_xh[:].rearrange("(k p) t -> k p t", p=128)[kc],
            c.bfb[:, kc * 1024:(kc + 1) * 1024]), dma=b.S("o0"))
    b.do("sync", lambda eo, t=t: eo.nop(), after=(t,))
    bld.finalize()
    return nc


class DBuf:

    def __init__(self, c, name, nslots=2, queue="gpsimd"):
        self.c = c
        self.sems = [c.b.S("%s%d" % (name, i)) for i in range(nslots)]
        self.cons = [None] * nslots
        self.nslots = nslots
        self.q = queue
        self.i = 0

    def next(self):
        k = self.i % self.nslots
        self.i += 1
        return k

    def load(self, k, dst_ap, src_ap, after=()):
        return self.c.b.do(self.q, lambda eo: eo.dma_start(dst_ap, src_ap),
                           after=[self.cons[k]] + list(after), dma=self.sems[k])

    def done(self, k, tok):
        self.cons[k] = tok


def wslab_src(wdram, mc, ncols=128):
    r = wdram[:].rearrange("(kc p) m -> p kc m", p=128)
    return r[:, :, mc * ncols:(mc + 1) * ncols]


def wslab_dst(c, region_cols, kcn, ncols=128):
    a, bcol = region_cols
    return a[:, bcol:bcol + kcn * ncols].rearrange("p (kc m) -> p kc m", m=ncols)


def emit_l0_kvq(c, io):
    b = c.b
    reg_const(c, 1.0 / 128.0)
    reg_const(c, 1e-6)
    reg_const(c, 1.28e-4)
    wdb = DBuf(c, "wkv")
    KP = 4096
    VP = 6144
    lck = b.do("sync", lambda eo: eo.dma_start(c.bfd[:, 0:2048], io.ckd[:]),
               dma=b.S("rope"))
    lsk = b.do("sync", lambda eo: eo.dma_start(c.bfd[:, 2048:4096], io.skd[:]),
               dma=b.S("rope2"))
    lk = b.do("gpsimd", lambda eo: eo.dma_start(
        wslab_dst(c, (c.wb, 4096), 16), wslab_src(io.wk, 0)), dma=b.S("wk"))
    lv = b.do("gpsimd", lambda eo: eo.dma_start(
        wslab_dst(c, (c.wb, 6144), 16), wslab_src(io.wv, 0)), dma=b.S("wv"))

    def xn_g4(g4):
        src = c.bfb if g4 in (0, 2) else c.bfa
        cb = 0 if g4 < 2 else 512
        return lambda kc: src[:, kc * 1024 + cb:kc * 1024 + cb + 512]

    for g4 in range(4):
        xf = xn_g4(g4)
        ps = c.pA if g4 % 2 == 0 else c.pB
        pn = "pA" if g4 % 2 == 0 else "pB"
        wsl = c.wb[:, 6144:8192]
        mm = None
        for kc in range(16):
            st, sp = kc == 0, kc == 15
            aft = [lv, puse(c, pn)] if st else []
            mm = b.do("tensor", lambda eo, kc=kc, st=st, sp=sp, ps=ps, xf=xf, wsl=wsl:
                      eo.matmul(ps[:, 0:512], wsl[:, kc * 128:(kc + 1) * 128],
                                xf(kc), start=st, stop=sp),
                      after=aft, inc=sp)
        vg = c.bfc[:, 0:512]
        ev = b.do("vector", lambda eo, ps=ps, vg=vg: eo.tensor_copy(vg, ps[:, 0:512]),
                  after=(mm,), inc=True)
        prel(c, pn, ev)
        for t in range(4):
            pn2 = "pC" if t % 2 == 0 else "pD"
            pt = getattr(c, pn2)[:].bitcast(BF16)[:, 0:128]
            tt = b.do("tensor", lambda eo, pt=pt, vg=vg, t=t: eo.transpose(
                pt, vg[:, t * 128:(t + 1) * 128], io.identb),
                after=(ev, puse(c, pn2)), inc=True)
            ev2 = b.do("vector", lambda eo, pt=pt, g4=g4, t=t: eo.tensor_copy(
                c.bfd[:, VP + (g4 * 4 + t) * 128:VP + (g4 * 4 + t) * 128 + 128], pt),
                after=(tt,), inc=True)
            prel(c, pn2, ev2)
    prev_mk = None
    for g4 in range(4):
        xf = xn_g4(g4)
        ps = c.pA if g4 % 2 == 0 else c.pB
        pn = "pA" if g4 % 2 == 0 else "pB"
        wsl = c.wb[:, 4096:6144]
        mm = None
        for kc in range(16):
            st, sp = kc == 0, kc == 15
            aft = [lk, puse(c, pn)] if st else []
            mm = b.do("tensor", lambda eo, kc=kc, st=st, sp=sp, ps=ps, xf=xf, wsl=wsl:
                      eo.matmul(ps[:, 0:512], wsl[:, kc * 128:(kc + 1) * 128],
                                xf(kc), start=st, stop=sp),
                      after=aft, inc=sp)
        k0 = c.bfc[:, 512:1024]
        ev = b.do("vector", lambda eo, ps=ps, k0=k0: eo.tensor_copy(k0, ps[:, 0:512]),
                  after=(mm,), inc=True)
        prel(c, pn, ev)
        pn2 = "pE"
        rt = b.do("tensor", lambda eo, k0=k0: eo.matmul(
            c.pE[:, 0:512], io.rperm, k0, start=True, stop=True),
            after=(ev, puse(c, pn2)), inc=True)
        cksl = c.bfd[:, g4 * 512:g4 * 512 + 512]
        sksl = c.bfd[:, 2048 + g4 * 512:2048 + g4 * 512 + 512]
        k1 = c.bfc[:, 1024:1536]
        tmp = c.bfc[:, 1536:2048]
        b.do("vector", lambda eo, k0=k0, cksl=cksl, k1=k1: eo.tensor_mul(k1, k0, cksl),
             after=(lck,))
        t2 = b.do("vector", lambda eo, sksl=sksl, tmp=tmp: eo.tensor_mul(
            tmp, c.pE[:, 0:512], sksl), after=(rt, lsk), inc=True)
        prel(c, pn2, t2)
        t3 = b.do("vector", lambda eo, k1=k1, tmp=tmp: eo.tensor_add(k1, k1, tmp),
                  inc=True)
        ksq = c.bfc[:, 1536:2048]
        t4 = b.do("scalar", lambda eo, k1=k1, ksq=ksq: eo.activation(
            ksq, k1, AF.Square), after=(t3,), inc=True)
        pn3 = "pF"
        sq = b.do("tensor", lambda eo, ksq=ksq: eo.matmul(
            c.pF[0:1, 0:512], io.onescol_b, ksq, start=True, stop=True),
            after=(t4, puse(c, pn3)), inc=True)
        u = c.fs[0:1, 1024:1536]
        t5 = b.do("scalar", lambda eo, u=u: eo.activation(
            u, c.pF[0:1, 0:512], AF.Sqrt, bias=1e-6, scale=1.0 / 128.0),
            after=(sq,), inc=True)
        prel(c, pn3, t5)
        t6 = b.do("vector", lambda eo, u=u: eo.reciprocal(u, u), after=(t5,), inc=True)
        ub = c.fs[:].bitcast(BF16)[0:1, 1024:1536]
        t7 = b.do("vector", lambda eo, u=u, ub=ub: eo.tensor_copy(ub, u),
                  after=[prev_mk], inc=True)
        pn4 = "pC"
        mk = b.do("tensor", lambda eo, ub=ub: eo.matmul(
            c.pC[:, 0:512], io.knw_row, ub, start=True, stop=True),
            after=(t7, puse(c, pn4)), inc=True)
        prev_mk = mk
        kf = b.do("vector", lambda eo, k1=k1, g4=g4: eo.tensor_mul(
            c.bfd[:, KP + g4 * 512:KP + g4 * 512 + 512], k1, c.pC[:, 0:512]),
            after=(mk,), inc=True)
        prel(c, pn4, kf)
    b.barrier()
    lcq = b.do("sync", lambda eo: eo.dma_start(c.bfc[:, 0:1024], io.cqd[:]),
               dma=b.S("rope"))
    lsq = b.do("sync", lambda eo: eo.dma_start(c.bfc[:, 1024:2048], io.sqd[:]),
               dma=b.S("rope2"))
    io.cq = c.bfc[:, 0:1024]
    io.sq = c.bfc[:, 1024:2048]
    qc_evs = []
    prev_mq = None
    reg_const(c, 1.28e-4)
    for mc in range(4):
        sl = wdb.next()
        dst = wslab_dst(c, (c.wb, sl * 2048), 16)
        ld = wdb.load(sl, dst, wslab_src(io.wqc, mc))
        ps = c.pA if mc % 2 == 0 else c.pB
        pn = "pA" if mc % 2 == 0 else "pB"
        toks = []
        for kc in range(16):
            for nt in range(2):
                st, sp = kc == 0, kc == 15
                aft = [ld, puse(c, pn)] if st and nt == 0 else []
                mm = b.do("tensor", lambda eo, kc=kc, st=st, sp=sp, ps=ps, nt=nt, sl=sl:
                          eo.matmul(ps[:, nt * 512:nt * 512 + 512],
                                    c.wb[:, sl * 2048 + kc * 128:sl * 2048 + (kc + 1) * 128],
                                    c.bfa[:, kc * 1024 + nt * 512:kc * 1024 + nt * 512 + 512],
                                    start=st, stop=sp),
                          after=aft, inc=sp)
                if sp:
                    toks.append(mm)
        ev = b.do("vector", lambda eo, ps=ps, mc=mc: eo.tensor_copy(
            c.bfc[:, 2048 + mc * 1024:2048 + (mc + 1) * 1024], ps[:]),
            after=toks, inc=True)
        prel(c, pn, ev)
        wdb.done(sl, ev)
        qc_evs.append(ev)
    for h in range(16):
        sl = wdb.next()
        dst = wslab_dst(c, (c.wb, sl * 2048), 4)
        ld = wdb.load(sl, dst, wslab_src(io.wqu, h))
        ps = c.pA if h % 2 == 0 else c.pB
        pn = "pA" if h % 2 == 0 else "pB"
        toks = []
        for kc in range(4):
            for nt in range(2):
                st, sp = kc == 0, kc == 3
                aft = ([ld, puse(c, pn)] if st and nt == 0 else []) + [qc_evs[kc]]
                mm = b.do("tensor", lambda eo, kc=kc, st=st, sp=sp, ps=ps, nt=nt, sl=sl:
                          eo.matmul(ps[:, nt * 512:nt * 512 + 512],
                                    c.wb[:, sl * 2048 + kc * 128:sl * 2048 + (kc + 1) * 128],
                                    c.bfc[:, 2048 + kc * 1024 + nt * 512:2048 + kc * 1024 + nt * 512 + 512],
                                    start=st, stop=sp),
                          after=aft, inc=sp)
                if sp:
                    toks.append(mm)
        q0 = c.bfc[:, 6144:7168]
        ev = b.do("vector", lambda eo, ps=ps, q0=q0: eo.tensor_copy(q0, ps[:]),
                  after=toks, inc=True)
        prel(c, pn, ev)
        wdb.done(sl, ev)
        qh = c.bfb[:, h * 1024:(h + 1) * 1024]
        tmp = c.bfc[:, 7168:8192]
        for nt in range(2):
            pn2 = "pE" if nt == 0 else "pF"
            p2 = getattr(c, pn2)
            rt = b.do("tensor", lambda eo, q0=q0, nt=nt, p2=p2: eo.matmul(
                p2[:, 0:512], io.rperm, q0[:, nt * 512:nt * 512 + 512],
                start=True, stop=True), after=(ev, puse(c, pn2)), inc=True)
            b.do("vector", lambda eo, q0=q0, nt=nt, qh=qh: eo.tensor_mul(
                qh[:, nt * 512:nt * 512 + 512], q0[:, nt * 512:nt * 512 + 512],
                io.cq[:, nt * 512:nt * 512 + 512]), after=(lcq,))
            t2 = b.do("vector", lambda eo, nt=nt, tmp=tmp, p2=p2: eo.tensor_mul(
                tmp[:, nt * 512:nt * 512 + 512], p2[:, 0:512],
                io.sq[:, nt * 512:nt * 512 + 512]), after=(rt, lsq), inc=True)
            prel(c, pn2, t2)
        t3 = b.do("vector", lambda eo, qh=qh, tmp=tmp: eo.tensor_add(qh, qh, tmp),
                  inc=True)
        qsq = c.bfc[:, 7168:8192]
        t4 = b.do("scalar", lambda eo, qh=qh, qsq=qsq: eo.activation(
            qsq, qh, AF.Square), after=(t3,), inc=True)
        u = c.fs[0:1, 1024:2048]
        t5s = []
        for nt in range(2):
            pn3 = "pE" if nt == 0 else "pF"
            p3 = getattr(c, pn3)
            sq = b.do("tensor", lambda eo, qsq=qsq, nt=nt, p3=p3: eo.matmul(
                p3[0:1, 0:512], io.onescol_b, qsq[:, nt * 512:nt * 512 + 512],
                start=True, stop=True), after=(t4, puse(c, pn3)), inc=True)
            t5 = b.do("scalar", lambda eo, nt=nt, p3=p3: eo.activation(
                c.fs[0:1, 1024 + nt * 512:1024 + nt * 512 + 512], p3[0:1, 0:512],
                AF.Sqrt, bias=1.28e-4), after=(sq,), inc=True)
            prel(c, pn3, t5)
            t5s.append(t5)
        t6 = b.do("vector", lambda eo: eo.reciprocal(u, u), after=t5s, inc=True)
        ub = c.fs[:].bitcast(BF16)[0:1, 1024:2048]
        t7 = b.do("vector", lambda eo: eo.tensor_copy(ub, u),
                  after=[prev_mq], inc=True)
        for nt in range(2):
            pn4 = "pE" if nt == 0 else "pF"
            p4 = getattr(c, pn4)
            mq = b.do("tensor", lambda eo, nt=nt, p4=p4: eo.matmul(
                p4[:, 0:512], io.qnw_row, ub[0:1, nt * 512:nt * 512 + 512],
                start=True, stop=True), after=(t7, puse(c, pn4)), inc=True)
            prev_mq = mq
            qf = b.do("vector", lambda eo, qh=qh, nt=nt, p4=p4: eo.tensor_mul(
                qh[:, nt * 512:nt * 512 + 512], qh[:, nt * 512:nt * 512 + 512],
                p4[:, 0:512]), after=(mq,), inc=True)
            prel(c, pn4, qf)
    b.barrier()


def emit_l0_scores(c, io):
    b = c.b
    KP, VP = 4096, 6144
    lm = b.do("sync", lambda eo: eo.dma_start(c.wb[0:1, 0:5120], io.erow[:]),
              dma=b.S("mask"))
    sb = b.do("tensor", lambda eo: eo.matmul(
        c.pE[:, 0:16], io.onesrow_f, io.sink_row, start=True, stop=True),
        after=(puse(c, "pE"),), inc=True)
    e1 = b.do("vector", lambda eo: eo.tensor_copy(c.sc[:, 16:32], c.pE[:, 0:16]),
              after=(sb,), inc=True)
    prel(c, "pE", e1)
    b.do("vector", lambda eo: eo.tensor_scalar(
        c.sc[:, 32:48], c.sc[:, 16:32], -1.0, None, ALU.mult))
    tiles = [(cq, h, j) for cq in range(2) for h in range(16) for j in range(4)]
    state = {"cp": [None, None], "pv": [None, None], "T": [None, None],
             "norm": [None, None]}
    flushT = _mk_flushT(c, io, state)
    SC = ("pA", "pB")
    pend = [None, None]
    v2p = None

    def do_v2(v2):
        ti_, cq_, h_, j_, qt_, probs_, probn_, rsum_, ske_, rden_, ex_, sx_, \
            par_ = v2
        b.do("vector", lambda eo: eo.tensor_add(rden_, rsum_, ske_),
             after=(ex_, sx_))
        b.do("vector", lambda eo: eo.reciprocal(rden_, rden_))
        nt = b.do("vector", lambda eo: eo.tensor_scalar(
            probn_, probs_, rden_, None, ALU.mult),
            after=(state["T"][par_],), inc=True)
        state["norm"][par_] = nt
        pend[par_] = (ti_, cq_, h_, j_, qt_, nt, probn_)

    for ti, (cq, h, j) in enumerate(tiles):
        if ti % 4 == 0:
            b.drip()
        par = ti % 2
        pn = SC[par]
        ps = getattr(c, pn)
        probs = c.bfc[:, par * 2048:par * 2048 + 640]
        probn = c.bfc[:, par * 2048 + 640:par * 2048 + 1280]
        negm = c.sc[:, 48 + par:49 + par]
        rsum = c.sc[:, 52 + par:53 + par]
        ske = c.sc[:, 56 + par:57 + par]
        rden = c.sc[:, 60 + par:61 + par]
        qt = c.bfb[:, h * 1024 + cq * 512 + j * 128:
                   h * 1024 + cq * 512 + j * 128 + 128]
        kb = KP + cq * 1024 + j * 128
        b.do("tensor", lambda eo, ps=ps, qt=qt, kb=kb: eo.matmul(
            ps[:, 0:512], qt, c.bfd[:, kb:kb + 512], start=True, stop=False),
            after=(puse(c, pn),), inc=False)
        b.do("tensor", lambda eo, ps=ps, qt=qt, kb=kb: eo.matmul(
            ps[:, 512:640], qt, c.bfd[:, kb + 512:kb + 640],
            start=True, stop=True), inc=False)
        ebase = (cq * 4 + j) * 640
        m2 = b.do("tensor", lambda eo, ps=ps, ebase=ebase: eo.matmul(
            ps[:, 0:512], io.onesrow_b, c.wb[0:1, ebase:ebase + 512],
            start=False, stop=True), after=(lm,), inc=True)
        if pend[par] is not None:
            flushT(pend[par])
            pend[par] = None
        b.do("vector", lambda eo, ps=ps: eo.tensor_add(
            ps[:, 0:128], ps[:, 0:128], io.winm[:, 0:128]), after=(m2,))
        b.do("vector", lambda eo, ps=ps: eo.tensor_add(
            ps[:, 512:640], ps[:, 512:640], io.winm[:, 128:256]))
        b.do("vector", lambda eo, ps=ps, negm=negm: eo.reduce_max(
            negm, ps[:, 0:640], axis=AX.X, negate=True))
        tm = b.do("vector", lambda eo, h=h, negm=negm: eo.tensor_single_scalar(
            negm, negm, c.sc[:, 32 + h:33 + h], ALU.min), inc=True)
        ex = b.do("scalar", lambda eo, ps=ps, probs=probs, negm=negm, rsum=rsum:
                  eo.activation(probs, ps[:, 0:640], AF.Exp, bias=negm,
                                accum_out=rsum),
                  after=(tm, state["norm"][par]), inc=True)
        prel(c, pn, ex)
        sx = b.do("scalar", lambda eo, h=h, ske=ske, negm=negm: eo.activation(
            ske, negm, AF.Exp, bias=c.sc[:, 16 + h:17 + h]), inc=True)
        if v2p is not None:
            do_v2(v2p)
        v2p = (ti, cq, h, j, qt, probs, probn, rsum, ske, rden, ex, sx, par)
    do_v2(v2p)
    for par in range(2):
        if pend[par] is not None:
            flushT(pend[par])
    b.barrier()


def _mk_flushT(c, io, state):
    b = c.b
    VP = 6144

    def flushT(pend):
        ti, cq, h, j, qt, nt, probn = pend
        tsl = ti % 2
        pcn = ("pC", "pD")[tsl]
        pvn = ("pE", "pF")[tsl]
        pvp = getattr(c, pvn)
        ptsb = c.bfc[:, 4096 + tsl * 640:4096 + tsl * 640 + 640]
        tts = None
        for st in range(5):
            pt = getattr(c, pcn)[:].bitcast(BF16)
            tts = b.do("tensor", lambda eo, st=st, pt=pt, probn=probn:
                       eo.transpose(pt[:, st * 128:(st + 1) * 128],
                                    probn[:, st * 128:(st + 1) * 128],
                                    io.identb),
                       after=(nt, state["cp"][tsl]) if st == 0 else (),
                       inc=(st == 4))
        state["T"][tsl] = tts
        cp = b.do("scalar", lambda eo, pcn=pcn, ptsb=ptsb: eo.copy(
            ptsb, getattr(c, pcn)[:].bitcast(BF16)[:, 0:640]),
            after=(tts, state["pv"][tsl]), inc=True)
        state["cp"][tsl] = cp
        prel(c, pcn, cp)
        pv = None
        for st in range(5):
            tile = cq * 8 + j + st
            stt_, spp = st == 0, st == 4
            aft = [cp, puse(c, pvn)] if stt_ else []
            pv = b.do("tensor", lambda eo, st=st, tile=tile, stt_=stt_, spp=spp,
                      pvp=pvp, ptsb=ptsb: eo.matmul(
                          pvp[:, 0:128],
                          c.bfd[:, VP + tile * 128:VP + tile * 128 + 128],
                          ptsb[:, st * 128:(st + 1) * 128],
                          start=stt_, stop=spp), after=aft, inc=spp)
        av = b.do("vector", lambda eo, qt=qt, pvp=pvp: eo.tensor_copy(
            qt, pvp[:, 0:128]), after=(pv,), inc=True)
        state["pv"][tsl] = av
        prel(c, pvn, av)
    return flushT


def emit_l0_out(c, io):
    b = c.b
    wdb = DBuf(c, "wo")
    w1r = [io.w1[:][g].rearrange("(kc p) m -> p kc m", p=128) for g in range(4)]
    w2r = io.w2[:].rearrange("(r p) m -> p r m", p=128)
    prev_w2 = None
    for g in range(4):
        mid_evs = [None] * 16
        for so in range(4):
            sl = wdb.next()
            dst = c.wb[:, sl * 2048:sl * 2048 + 2048].rearrange(
                "p (kc m) -> p kc m", m=512)
            ld = wdb.load(sl, dst, w1r[g][:, :, so * 512:(so + 1) * 512])
            lastmm = None
            for oc4 in range(4):
                oc = so * 4 + oc4
                pn = "pA" if oc % 2 == 0 else "pB"
                ps = getattr(c, pn)
                toks = []
                for kc in range(4):
                    for nt in range(2):
                        st, sp = kc == 0, kc == 3
                        aft = [ld, puse(c, pn)] if st and nt == 0 else []
                        mm = b.do("tensor", lambda eo, sl=sl, kc=kc, oc4=oc4, g=g,
                                  nt=nt, st=st, sp=sp, ps=ps: eo.matmul(
                                      ps[:, nt * 512:nt * 512 + 512],
                                      c.wb[:, sl * 2048 + kc * 512 + oc4 * 128:
                                           sl * 2048 + kc * 512 + oc4 * 128 + 128],
                                      c.bfb[:, (4 * g + kc) * 1024 + nt * 512:
                                            (4 * g + kc) * 1024 + nt * 512 + 512],
                                      start=st, stop=sp), after=aft, inc=sp)
                        if sp:
                            toks.append(mm)
                lastmm = toks[-1]
                ev = b.do("vector", lambda eo, ps=ps, oc=oc: eo.tensor_copy(
                    c.bfc[:, oc * 1024:(oc + 1) * 1024], ps[:]),
                    after=toks + [prev_w2], inc=True)
                prel(c, pn, ev)
                mid_evs[oc] = ev
            wdb.done(sl, lastmm)
        for mc in range(16):
            sl = wdb.next()
            dst = c.wb[:, sl * 2048:sl * 2048 + 2048].rearrange(
                "p (kc m) -> p kc m", m=128)
            ld = wdb.load(sl, dst, w2r[:, g * 16:(g + 1) * 16,
                                       mc * 128:(mc + 1) * 128])
            pn = "pA" if mc % 2 == 0 else "pB"
            ps = getattr(c, pn)
            toks = []
            for kc in range(16):
                for nt in range(2):
                    st, sp = kc == 0, kc == 15
                    aft = ([ld, puse(c, pn)] if st and nt == 0 else []) + \
                        [mid_evs[kc]]
                    mm = b.do("tensor", lambda eo, sl=sl, kc=kc, nt=nt, st=st,
                              sp=sp, ps=ps: eo.matmul(
                                  ps[:, nt * 512:nt * 512 + 512],
                                  c.wb[:, sl * 2048 + kc * 128:sl * 2048 + (kc + 1) * 128],
                                  c.bfc[:, kc * 1024 + nt * 512:kc * 1024 + nt * 512 + 512],
                                  start=st, stop=sp), after=aft, inc=sp)
                    if sp:
                        toks.append(mm)
            prev_w2 = toks[-1]
            ad = b.do("vector", lambda eo, ps=ps, mc=mc: eo.tensor_add(
                c.x[:, mc * 1024:(mc + 1) * 1024],
                c.x[:, mc * 1024:(mc + 1) * 1024], ps[:]), after=toks, inc=True)
            prel(c, pn, ad)
            wdb.done(sl, ad)
    b.barrier()


def decl_l0_inputs(nc, io):
    DT = nc.dram_tensor
    io.wqc = DT("wqc", [D, 512], F32, kind="ExternalInput")
    io.wqu = DT("wqu", [512, D], F32, kind="ExternalInput")
    io.wk = DT("wk", [D, HD], F32, kind="ExternalInput")
    io.wv = DT("wv", [D, HD], F32, kind="ExternalInput")
    io.w1 = DT("w1", [4, 512, D], F32, kind="ExternalInput")
    io.w2 = DT("w2", [4 * D, D], F32, kind="ExternalInput")
    io.ckd = DT("ckd", [128, LOC], BF16, kind="ExternalInput")
    io.skd = DT("skd", [128, LOC], BF16, kind="ExternalInput")
    io.cqd = DT("cqd", [128, OWN], BF16, kind="ExternalInput")
    io.sqd = DT("sqd", [128, OWN], BF16, kind="ExternalInput")
    io.erow = DT("erow", [1, 5120], BF16, kind="ExternalInput")
    io.winm = _sb_const(nc, "winm", [128, 256], BF16)
    io.rperm = _sb_const(nc, "rperm", [128, 128], BF16)
    io.onescol_b = _sb_const(nc, "onescol_b", [128, 1], BF16)
    io.onesrow_b = _sb_const(nc, "onesrow_b", [1, 128], BF16)
    io.onesrow_f = _sb_const(nc, "onesrow_f", [1, 128], F32)
    io.qnw_row = _sb_const(nc, "qnw_row", [1, 128], F32, BF16)
    io.knw_row = _sb_const(nc, "knw_row", [1, 128], F32, BF16)
    io.sink_row = _sb_const(nc, "sink_row", [1, 16], F32)


def build_l0_test():
    nc = bass.Bass("TRN2", target_bir_lowering=False, debug=False)
    bld = Bld(nc)
    io = IO()
    io.h_in = nc.dram_tensor("h_in", [LOC, D], F32, kind="ExternalInput")
    decl_common_inputs(nc, io)
    decl_l0_inputs(nc, io)
    o_x = nc.dram_tensor("o_x", [D, OWN], F32, kind="ExternalOutput")
    c = alloc_common(nc, bld)
    for v in (1e-5, 1.0 / 128.0, 1e-6, 1.28e-4):
        reg_const(c, v)
    emit_const_init(c)
    emit_const_loads(c)
    bld.barrier()
    emit_p0(c, io)
    emit_l0_kvq(c, io)
    emit_l0_scores(c, io)
    emit_l0_out(c, io)
    b = bld
    for kc in range(16):
        t = b.do("sync", lambda eo, kc=kc: eo.dma_start(
            o_x[:].rearrange("(k p) t -> k p t", p=128)[kc],
            c.x[:, kc * 1024:(kc + 1) * 1024]), dma=b.S("o0"))
    b.do("sync", lambda eo, t=t: eo.nop(), after=(t,))
    bld.finalize()
    return nc


def emit_moe(c, io):
    b = c.b
    lrt = b.do("gpsimd", lambda eo: eo.dma_start(
        c.bfd[:, 0:128].rearrange("p (kc m) -> p kc m", m=8),
        io.routerT[:].rearrange("(kc p) m -> p kc m", p=128)), dma=b.S("wrt"))
    comb_toks = []
    e_ = c.sc[:, 64:72]
    negmx = c.sc[:, 72:73]
    m1 = c.sc[:, 73:74]
    t2c = c.sc[:, 74:82]
    m2 = c.sc[:, 82:83]
    wsum = c.sc[:, 83:84]
    for tt in range(8):
        mm = None
        for kc in range(16):
            st, sp = kc == 0, kc == 15
            aft = [lrt, puse(c, "pC")] if st else []
            mm = b.do("tensor", lambda eo, kc=kc, st=st, sp=sp, tt=tt: eo.matmul(
                c.pC[:, 0:8],
                c.bfa[:, kc * 1024 + tt * 128:kc * 1024 + tt * 128 + 128],
                c.bfd[:, kc * 8:(kc + 1) * 8], start=st, stop=sp),
                after=aft, inc=sp)
        rm = b.do("vector", lambda eo: eo.reduce_max(negmx, c.pC[:, 0:8], axis=AX.X,
             negate=True), after=(mm,), inc=True)
        ex = b.do("scalar", lambda eo: eo.activation(e_, c.pC[:, 0:8], AF.Exp,
                  bias=negmx), after=(rm,), inc=True)
        prel(c, "pC", ex)
        b.do("vector", lambda eo: eo.reduce_max(m1, e_, axis=AX.X), after=(ex,))
        b.do("vector", lambda eo: eo.tensor_single_scalar(t2c, e_, m1, ALU.is_ge))
        b.do("vector", lambda eo: eo.scalar_tensor_tensor(
            t2c, t2c, -2.0, e_, ALU.mult, ALU.add))
        b.do("vector", lambda eo: eo.reduce_max(m2, t2c, axis=AX.X))
        b.do("vector", lambda eo: eo.tensor_single_scalar(t2c, e_, m2, ALU.is_ge))
        b.do("vector", lambda eo: eo.tensor_mul(t2c, t2c, e_))
        b.do("vector", lambda eo: eo.reduce_sum(wsum, t2c, axis=AX.X))
        b.do("vector", lambda eo: eo.reciprocal(wsum, wsum))
        comb_toks.append(b.do("vector", lambda eo, tt=tt: eo.tensor_scalar(
            c.fs[:, tt * 8:tt * 8 + 8], t2c, wsum, None, ALU.mult), inc=True))
    lsg = b.do("gpsimd", lambda eo: eo.dma_start(
        c.bfd[:, 128:144].rearrange("p (kc one) -> p kc one", one=1),
        io.sgT[:].rearrange("(kc p) one -> p kc one", p=128),
        allow_slow_non_contiguous=True), dma=b.S("wsg"))
    sgrow = c.fs[0:1, 1024:2048]
    sg_toks = []
    for nt in range(2):
        pn = "pE" if nt == 0 else "pF"
        ps = getattr(c, pn)
        mm = None
        for kc in range(16):
            st, sp = kc == 0, kc == 15
            aft = [lsg, puse(c, pn)] if st else []
            mm = b.do("tensor", lambda eo, kc=kc, st=st, sp=sp, nt=nt, ps=ps:
                      eo.matmul(ps[0:1, 0:512], c.bfd[:, 128 + kc:129 + kc],
                                c.bfa[:, kc * 1024 + nt * 512:kc * 1024 + nt * 512 + 512],
                                start=st, stop=sp), after=aft, inc=sp)
        sg = b.do("scalar", lambda eo, nt=nt, ps=ps: eo.activation(
            sgrow[0:1, nt * 512:nt * 512 + 512], ps[0:1, 0:512], AF.Sigmoid),
            after=(mm,), inc=True)
        prel(c, pn, sg)
        sg_toks.append(sg)
    sgb = c.fs[:].bitcast(BF16)[0:1, 1024:2048]
    scb = b.do("vector", lambda eo: eo.tensor_copy(sgb, sgrow), after=sg_toks,
               inc=True)
    for nt in range(2):
        pn = "pE" if nt == 0 else "pF"
        ps = getattr(c, pn)
        mm = b.do("tensor", lambda eo, nt=nt, ps=ps: eo.matmul(
            ps[:, 0:512], io.onesrow_b, sgb[0:1, nt * 512:nt * 512 + 512],
            start=True, stop=True), after=(scb, puse(c, pn)), inc=True)
        row_war = mm
        ev = b.do("vector", lambda eo, nt=nt, ps=ps: eo.tensor_copy(
            c.bfd[:, 3072 + nt * 512:3072 + nt * 512 + 512], ps[:, 0:512]),
            after=(mm,), inc=True)
        prel(c, pn, ev)

    wg = DBuf(c, "wg", nslots=4)
    wd = DBuf(c, "wd")

    def gu_mm(slot_ld, sl, oc_in_slab, psn, xcols=1024):
        ps = getattr(c, psn)
        toks = []
        for kc in range(16):
            for nt in range(xcols // 512):
                st, sp = kc == 0, kc == 15
                aft = [slot_ld, puse(c, psn)] if st and nt == 0 else []
                mm = b.do("tensor", lambda eo, kc=kc, st=st, sp=sp, nt=nt, sl=sl,
                          oc_in_slab=oc_in_slab, ps=ps: eo.matmul(
                              ps[:, nt * 512:nt * 512 + 512],
                              c.wb[:, sl * 2048 + kc * 128:sl * 2048 + (kc + 1) * 128],
                              c.bfa[:, kc * 1024 + nt * 512:kc * 1024 + nt * 512 + 512],
                              start=st, stop=sp), after=aft, inc=sp)
                if sp:
                    toks.append(mm)
        return toks

    prev_av = None
    shg_r = io.shgT[:].rearrange("(kc p) m -> p kc m", p=128)
    shu_r = io.shuT[:].rearrange("(kc p) m -> p kc m", p=128)
    shd_r = io.shdT[:].rearrange("(r p) m -> p r m", p=128)
    last_down = None
    def gu_load(wsrc_r, gc):
        sl_ = wg.next()
        ld_ = wg.load(sl_, wslab_dst(c, (c.wb, sl_ * 2048), 16),
                      wsrc_r[:, :, gc * 128:(gc + 1) * 128])
        return sl_, ld_

    for iq in range(4):
        noc = 11
        act_toks = [None] * 11
        pre = {0: (gu_load(shg_r, iq * 11), gu_load(shu_r, iq * 11))}
        for oc in range(noc):
            gc = iq * 11 + oc
            (slg, ldg), (slu, ldu) = pre.pop(oc)
            if oc + 1 < noc:
                pre[oc + 1] = (gu_load(shg_r, gc + 1), gu_load(shu_r, gc + 1))
            gt = gu_mm(ldg, slg, 0, "pA")
            ut = gu_mm(ldu, slu, 0, "pB")
            sl_ = c.bfd[:, 1024:2048]
            sv = b.do("scalar", lambda eo, sl_=sl_: eo.activation(
                sl_, c.pA[:], AF.Silu), after=gt + [prev_av], inc=True)
            prel(c, "pA", sv)
            wg.done(slg, sv)
            av = b.do("vector", lambda eo, sl_=sl_, oc=oc: eo.tensor_mul(
                c.bfc[:, oc * 1024:(oc + 1) * 1024], sl_, c.pB[:]),
                after=ut + [sv, last_down], inc=True)
            prel(c, "pB", av)
            wg.done(slu, av)
            act_toks[oc] = av
            prev_av = av
        for hc in range(16):
            sl2 = wd.next()
            ld2 = wd.load(sl2, c.bfd[:, 4096 + sl2 * 1408:4096 + sl2 * 1408 + 1408]
                          .rearrange("p (kc m) -> p kc m", m=128),
                          shd_r[:, iq * 11:(iq + 1) * 11, hc * 128:(hc + 1) * 128])
            dtoks = {}
            for kc in range(11):
                for nt in range(2):
                    pn = "pC" if nt == 0 else "pD"
                    ps = getattr(c, pn)
                    st, sp = kc == 0, kc == 10
                    aft = ([ld2, puse(c, pn)] if st else []) + [act_toks[kc]]
                    mm = b.do("tensor", lambda eo, kc=kc, st=st, sp=sp, nt=nt,
                              sl2=sl2, ps=ps: eo.matmul(
                                  ps[:, 0:512],
                                  c.bfd[:, 4096 + sl2 * 1408 + kc * 128:
                                       4096 + sl2 * 1408 + (kc + 1) * 128],
                                  c.bfc[:, kc * 1024 + nt * 512:kc * 1024 + nt * 512 + 512],
                                  start=st, stop=sp), after=aft, inc=sp)
                    if sp:
                        dtoks[nt] = mm
            for nt in range(2):
                pn = "pC" if nt == 0 else "pD"
                ps = getattr(c, pn)
                mm = dtoks[nt]
                tv = b.do("vector", lambda eo, nt=nt, hc=hc, ps=ps: eo.tensor_mul(
                    c.bfd[:].bitcast(F32)[:, 3456:3968],
                    ps[:, 0:512], c.bfd[:, 3072 + nt * 512:3072 + nt * 512 + 512]),
                    after=(mm,), inc=True)
                prel(c, pn, tv)
                if nt == 1:
                    wd.done(sl2, tv)
                b.do("vector", lambda eo, nt=nt, hc=hc: eo.tensor_add(
                    c.x[:, hc * 1024 + nt * 512:hc * 1024 + nt * 512 + 512],
                    c.x[:, hc * 1024 + nt * 512:hc * 1024 + nt * 512 + 512],
                    c.bfd[:].bitcast(F32)[:, 3456:3968]))
                if nt == 1 and hc == 15:
                    last_down = mm
    gup_r = [io.gupT[:][e].rearrange("(kc p) m -> p kc m", p=128) for e in range(NE)]
    dp_r = [io.dpT[:][e].rearrange("(r p) m -> p r m", p=128) for e in range(NE)]
    last_pool = None
    row_war = locals().get("row_war")
    for e in range(NE):
        comb_evs = []
        pool_toks = [None] * 11
        rowp = c.fs[0:1, 1024:2048]
        rowb = c.fs[:].bitcast(BF16)[0:1, 1024:2048]
        tts = []
        for tt in range(8):
            tp = b.do("tensor", lambda eo, tt=tt, e=e: eo.transpose(
                c.pE[0:1, 0:128], c.fs[:, tt * 8 + e:tt * 8 + e + 1],
                io.identf), after=[puse(c, "pE"), comb_toks[tt]], inc=True)
            ev = b.do("vector", lambda eo, tt=tt: eo.tensor_copy(
                rowp[0:1, tt * 128:(tt + 1) * 128], c.pE[0:1, 0:128]),
                after=(tp,), inc=True)
            prel(c, "pE", ev)
        cb = b.do("vector", lambda eo: eo.tensor_copy(rowb, rowp),
                  after=[row_war], inc=True)
        for nt in range(2):
            pn = "pE" if nt == 0 else "pF"
            ps = getattr(c, pn)
            mm = b.do("tensor", lambda eo, nt=nt, ps=ps: eo.matmul(
                ps[:, 0:512], io.onesrow_b, rowb[0:1, nt * 512:nt * 512 + 512],
                start=True, stop=True), after=(cb, puse(c, pn)), inc=True)
            row_war = mm
            ev = b.do("vector", lambda eo, nt=nt, ps=ps: eo.tensor_copy(
                c.bfd[:, 2048 + nt * 512:2048 + nt * 512 + 512], ps[:, 0:512]),
                after=(mm, last_pool), inc=True)
            prel(c, pn, ev)
            comb_evs.append(ev)
        pre = {0: (gu_load(gup_r[e], 0), gu_load(gup_r[e], 11))}
        for oc in range(11):
            (slg, ldg), (slu, ldu) = pre.pop(oc)
            if oc + 1 < 11:
                pre[oc + 1] = (gu_load(gup_r[e], oc + 1),
                               gu_load(gup_r[e], 12 + oc))
            gt = gu_mm(ldg, slg, 0, "pA")
            ut = gu_mm(ldu, slu, 0, "pB")
            sl_ = c.bfd[:, 1024:2048]
            sv = b.do("scalar", lambda eo, sl_=sl_: eo.activation(
                sl_, c.pA[:], AF.Silu), after=gt + [prev_av], inc=True)
            prel(c, "pA", sv)
            wg.done(slg, sv)
            av = b.do("vector", lambda eo, sl_=sl_, oc=oc: eo.tensor_mul(
                c.bfc[:, oc * 1024:(oc + 1) * 1024], sl_, c.pB[:]),
                after=ut + [sv, last_down], inc=True)
            prel(c, "pB", av)
            wg.done(slu, av)
            prev_av = av
            pool_toks[oc] = b.do("vector", lambda eo, oc=oc: eo.tensor_mul(
                c.bfc[:, oc * 1024:(oc + 1) * 1024],
                c.bfc[:, oc * 1024:(oc + 1) * 1024],
                c.bfd[:, 2048:3072]), after=[av] + comb_evs, inc=True)
            last_pool = pool_toks[oc]
        for hc in range(16):
            sl2 = wd.next()
            ld2 = wd.load(sl2, c.bfd[:, 4096 + sl2 * 1408:4096 + sl2 * 1408 + 1408]
                          .rearrange("p (kc m) -> p kc m", m=128),
                          dp_r[e][:, :, hc * 128:(hc + 1) * 128])
            dtoks = {}
            for kc in range(11):
                for nt in range(2):
                    pn = "pC" if nt == 0 else "pD"
                    ps = getattr(c, pn)
                    st, sp = kc == 0, kc == 10
                    aft = ([ld2, puse(c, pn)] if st else []) + [pool_toks[kc]]
                    mm = b.do("tensor", lambda eo, kc=kc, st=st, sp=sp, nt=nt,
                              sl2=sl2, ps=ps: eo.matmul(
                                  ps[:, 0:512],
                                  c.bfd[:, 4096 + sl2 * 1408 + kc * 128:
                                       4096 + sl2 * 1408 + (kc + 1) * 128],
                                  c.bfc[:, kc * 1024 + nt * 512:kc * 1024 + nt * 512 + 512],
                                  start=st, stop=sp), after=aft, inc=sp)
                    if sp:
                        dtoks[nt] = mm
            for nt in range(2):
                pn = "pC" if nt == 0 else "pD"
                ps = getattr(c, pn)
                mm = dtoks[nt]
                ad = b.do("vector", lambda eo, nt=nt, hc=hc, ps=ps: eo.tensor_add(
                    c.x[:, hc * 1024 + nt * 512:hc * 1024 + nt * 512 + 512],
                    c.x[:, hc * 1024 + nt * 512:hc * 1024 + nt * 512 + 512],
                    ps[:, 0:512]), after=(mm,), inc=True)
                prel(c, pn, ad)
                if nt == 1:
                    wd.done(sl2, ad)
                    if hc == 15:
                        last_down = mm
    b.barrier()


def decl_moe_inputs(nc, io):
    DT = nc.dram_tensor
    io.routerT = DT("routerT", [D, 8], F32, kind="ExternalInput")
    io.gupT = DT("gupT", [NE, D, 2 * IMOE], F32, kind="ExternalInput")
    io.dpT = DT("dpT", [NE, IMOE, D], F32, kind="ExternalInput")
    io.shgT = DT("shgT", [D, ISH], F32, kind="ExternalInput")
    io.shuT = DT("shuT", [D, ISH], F32, kind="ExternalInput")
    io.shdT = DT("shdT", [ISH, D], F32, kind="ExternalInput")
    io.sgT = DT("sgT", [D, 1], F32, kind="ExternalInput")


def build_layer0_test():
    nc = bass.Bass("TRN2", target_bir_lowering=False, debug=False)
    bld = Bld(nc)
    io = IO()
    io.h_in = nc.dram_tensor("h_in", [LOC, D], F32, kind="ExternalInput")
    decl_common_inputs(nc, io)
    decl_l0_inputs(nc, io)
    decl_moe_inputs(nc, io)
    o_x = nc.dram_tensor("o_x", [D, OWN], F32, kind="ExternalOutput")
    c = alloc_common(nc, bld)
    for v in (1e-5, 1.0 / 128.0, 1e-6, 1.28e-4):
        reg_const(c, v)
    emit_const_init(c)
    emit_const_loads(c)
    bld.barrier()
    emit_p0(c, io)
    emit_l0_kvq(c, io)
    emit_l0_scores(c, io)
    emit_l0_out(c, io)
    emit_moe(c, io)
    b = bld
    for kc in range(16):
        t = b.do("sync", lambda eo, kc=kc: eo.dma_start(
            o_x[:].rearrange("(k p) t -> k p t", p=128)[kc],
            c.x[:, kc * 1024:(kc + 1) * 1024]), dma=b.S("o0"))
    b.do("sync", lambda eo, t=t: eo.nop(), after=(t,))
    bld.finalize()
    return nc


def row_bcast_exact(c, psum_ap, row_f32, hi_b, lo_b, after=(), pn=None):
    b = c.b
    io = c.io
    t1 = b.do("vector", lambda eo: eo.tensor_copy(hi_b, row_f32), after=after,
              inc=True)
    t2 = b.do("vector", lambda eo: eo.tensor_tensor(lo_b, row_f32, hi_b,
              ALU.subtract), inc=True)
    n = row_f32.shape[1]
    m1 = b.do("tensor", lambda eo: eo.matmul(psum_ap, io.onesrow_b, hi_b,
              start=True, stop=False), after=(t1, t2, puse(c, pn)), inc=True)
    m2 = b.do("tensor", lambda eo: eo.matmul(psum_ap, io.onesrow_b, lo_b,
              start=False, stop=True), inc=True)
    return m2


def emit_ln1(c, io):
    b = c.b
    reg_const(c, 1e-5)
    bfcF = c.bfc[:].bitcast(F32)
    xsq = bfcF[:, 0:1024]
    prev_sq = None
    toks = []
    for kc in range(16):
        sq = b.do("scalar", lambda eo, kc=kc: eo.activation(
            xsq, c.x[:, kc * 1024:(kc + 1) * 1024], AF.Square),
            after=[prev_sq], inc=True)
        for nt in range(2):
            pnm = "pC" if nt == 0 else "pD"
            ps = getattr(c, pnm)
            aft = [puse(c, pnm)] if kc == 0 else []
            t = b.do("tensor", lambda eo, kc=kc, nt=nt, ps=ps: eo.matmul(
                ps[0:1, 0:512], io.onescol_f,
                c.x[:, kc * 1024 + nt * 512:kc * 1024 + nt * 512 + 512],
                start=kc == 0, stop=kc == 15), after=aft, inc=kc == 15)
            if t:
                toks.append(t)
            pnm2 = "pE" if nt == 0 else "pF"
            ps2 = getattr(c, pnm2)
            aft = ([puse(c, pnm2)] if kc == 0 else []) + [sq]
            t2_ = b.do("tensor", lambda eo, kc=kc, nt=nt, ps2=ps2: eo.matmul(
                ps2[0:1, 0:512], io.onescol_f,
                xsq[:, nt * 512:nt * 512 + 512],
                start=kc == 0, stop=kc == 15), after=aft, inc=True)
            if kc == 15:
                toks.append(t2_)
            if nt == 1:
                prev_sq = t2_
    mean = c.fs[0:1, 0:1024]
    v2 = c.fs[0:1, 1024:2048]
    tmp = bfcF[0:1, 1024:2048]
    for nt in range(2):
        psm = (c.pC if nt == 0 else c.pD)[0:1, 0:512]
        psq = (c.pE if nt == 0 else c.pF)[0:1, 0:512]
        b.do("vector", lambda eo, nt=nt, psm=psm: eo.tensor_scalar(
            mean[0:1, nt * 512:nt * 512 + 512], psm, 1.0 / 2048.0, None,
            ALU.mult), after=toks)
        b.do("vector", lambda eo, nt=nt, psq=psq: eo.tensor_scalar(
            v2[0:1, nt * 512:nt * 512 + 512], psq, 1.0 / 2048.0, None,
            ALU.mult), after=toks)
    t1 = b.do("vector", lambda eo: eo.tensor_mul(tmp, mean, mean), inc=True)
    t2 = b.do("vector", lambda eo: eo.tensor_tensor(v2, v2, tmp, ALU.subtract),
              inc=True)
    t3 = b.do("scalar", lambda eo: eo.activation(tmp, v2, AF.Sqrt, bias=1e-5),
              after=(t2,), inc=True)
    t4 = b.do("vector", lambda eo: eo.reciprocal(v2, tmp), after=(t3,), inc=True)
    t5 = b.do("vector", lambda eo: eo.scalar_tensor_tensor(
        tmp, mean, -1.0, v2, ALU.mult, ALU.mult), inc=True)
    hi_b = c.bfc[0:1, 4096:5120]
    lo_b = c.bfc[0:1, 5120:6144]
    mma = []
    for nt in range(2):
        mma.append(row_bcast_exact(
            c, c.pA[:, nt * 512:nt * 512 + 512],
            v2[0:1, nt * 512:nt * 512 + 512],
            hi_b[0:1, nt * 512:nt * 512 + 512],
            lo_b[0:1, nt * 512:nt * 512 + 512],
            after=(t4,), pn="pA" if nt == 0 else None))
    hi_b2 = c.bfc[0:1, 6144:7168]
    lo_b2 = c.bfc[0:1, 7168:8192]
    for nt in range(2):
        mma.append(row_bcast_exact(
            c, c.pB[:, nt * 512:nt * 512 + 512],
            tmp[0:1, nt * 512:nt * 512 + 512],
            hi_b2[0:1, nt * 512:nt * 512 + 512],
            lo_b2[0:1, nt * 512:nt * 512 + 512],
            after=(t5,), pn="pB" if nt == 0 else None))
    t32 = bfcF[:, 2048 - 1024:2048]
    prev_t = None
    for kc in range(16):
        tt1 = b.do("vector", lambda eo, kc=kc: eo.tensor_mul(
            t32, c.x[:, kc * 1024:(kc + 1) * 1024], c.pA[:]),
            after=mma + [prev_t], inc=True)
        tt2 = b.do("vector", lambda eo, kc=kc: eo.tensor_tensor(
            c.bfa[:, kc * 1024:(kc + 1) * 1024], t32, c.pB[:], ALU.add),
            inc=True)
        prev_t = tt2
    prel(c, "pA", prev_t)
    prel(c, "pB", prev_t)
    b.barrier()


def decl_l1_inputs(nc, io, mode):
    DT = nc.dram_tensor
    io.wproj = DT("wproj", [D, D], F32, kind="ExternalInput")
    io.bproj = _sb_const(nc, "bproj", [128, 16], F32)
    io.qgT = DT("qgT", [D, IFF], I32, kind="ExternalInput")
    io.quT = DT("quT", [D, IFF], I32, kind="ExternalInput")
    io.qdT = DT("qdT", [IFF, D], I32, kind="ExternalInput")
    io.gscol = _sb_const(nc, "gscol", [128, 64], F32)
    io.uscol = _sb_const(nc, "uscol", [128, 64], F32)
    io.dscol = _sb_const(nc, "dscol", [128, 16], F32)
    io.onescol_f = _sb_const(nc, "onescol_f", [128, 1], F32)
    io.ones128 = _sb_const(nc, "ones128", [128, 128], BF16)
    io.tri = _sb_const(nc, "tri", [128, 128], BF16)
    io.bmat = _sb_const(nc, "bmat", [128, 64], F32)
    io.f1m = _sb_const(nc, "f1m", [128, 16], F32)
    io.f1g = _sb_const(nc, "f1g", [128, 64], F32)
    io.aprd = DT("aprd", [128, 1024], F32, kind="ExternalInput")


def emit_l1_qkv(c, io, kv_dram, kick=None):
    b = c.b
    wdb = DBuf(c, "wq")
    sdb = DBuf(c, "kvs", queue="sync")
    for mc in list(range(16, 48)) + list(range(16)):
        sl = wdb.next()
        dst = wslab_dst(c, (c.wb, sl * 2048), 16)
        ld = wdb.load(sl, dst, wslab_src(io.wqkv, mc))
        pn = "pA" if mc % 2 == 0 else "pB"
        ps = getattr(c, pn)
        toks = []
        for kc in range(16):
            for nt in range(2):
                st, sp = kc == 0, kc == 15
                aft = [ld, puse(c, pn)] if st and nt == 0 else []
                mm = b.do("tensor", lambda eo, kc=kc, st=st, sp=sp, ps=ps, nt=nt,
                          sl=sl: eo.matmul(
                              ps[:, nt * 512:nt * 512 + 512],
                              c.wb[:, sl * 2048 + kc * 128:sl * 2048 + (kc + 1) * 128],
                              c.bfa[:, kc * 1024 + nt * 512:kc * 1024 + nt * 512 + 512],
                              start=st, stop=sp), after=aft, inc=sp)
                if sp:
                    toks.append(mm)
        bcol = io.bqkv[:, mc:mc + 1]
        if mc < 16:
            ev = b.do("vector", lambda eo, ps=ps, mc=mc, bcol=bcol: eo.tensor_scalar(
                c.bfb[:, mc * 1024:(mc + 1) * 1024], ps[:], bcol, 1.0 / 128.0,
                ALU.add, ALU.mult), after=toks, inc=True)
            prel(c, pn, ev)
            wdb.done(sl, ev)
        elif mc < 32:
            kc_ = mc - 16
            ssl = sdb.next()
            stg = c.bfd[:, ssl * 1024:(ssl + 1) * 1024]
            ev = b.do("vector", lambda eo, ps=ps, bcol=bcol, stg=stg:
                      eo.tensor_scalar(stg, ps[:], bcol, None, ALU.add),
                      after=toks + [sdb.cons[ssl]], inc=True)
            prel(c, pn, ev)
            wdb.done(sl, ev)
            st_ = sdb.load(ssl, kv_dram[kc_ * 128:(kc_ + 1) * 128, :], stg,
                           after=(ev,))
            sdb.done(ssl, st_)
        else:
            mcv = mc - 32
            ssl = sdb.next()
            stg = c.bfd[:, ssl * 1024:(ssl + 1) * 1024]
            ev = b.do("vector", lambda eo, ps=ps, bcol=bcol, stg=stg:
                      eo.tensor_scalar(stg, ps[:], bcol, None, ALU.add),
                      after=toks + [sdb.cons[ssl]], inc=True)
            prel(c, pn, ev)
            wdb.done(sl, ev)
            vstg = c.bfd[:, 2048 + ssl * 1024:2048 + (ssl + 1) * 1024]
            evs = None
            for tt in range(8):
                pn2 = "pE" if tt % 2 == 0 else "pF"
                pt = getattr(c, pn2)[:].bitcast(BF16)
                tr = b.do("tensor", lambda eo, tt=tt, stg=stg, pt=pt: eo.transpose(
                    pt[:, 0:128], stg[:, tt * 128:(tt + 1) * 128], io.identb),
                    after=(ev, puse(c, pn2)), inc=True)
                evs = b.do("vector", lambda eo, tt=tt, pt=pt, vstg=vstg:
                           eo.tensor_copy(vstg[:, tt * 128:(tt + 1) * 128],
                                          pt[:, 0:128]), after=(tr,), inc=True)
                prel(c, pn2, evs)
            dst2 = dap(kv_dram, 2048 * 1024 + mcv * 128,
                       [(2048, 128), (128 * 2048, 8), (1, 128)])
            st_ = sdb.load(ssl, dst2,
                           vstg[:].rearrange("p (t f) -> p t f", f=128),
                           after=(evs,))
            sdb.done(ssl, st_)
        if mc == 47 and kick is not None:
            kick([(sm, sm.n) for sm in sdb.sems])
    b.barrier()


ALIBI_SLOPES = [2.0 ** (-0.5 * (i + 1)) for i in range(16)]


def emit_l1_attn(c, io, ag_dram, own_dram):
    b = c.b
    la = b.do("sync", lambda eo: eo.dma_start(c.fs[:, 0:1024], io.aprd[:]),
              dma=b.S("apr"))
    kdb = DBuf(c, "l1kv", nslots=2, queue="sync")
    wpar = [None, None, None, None]
    SPS = (("pA", 0, "pA0"), ("pA", 512, "pA1"), ("pB", 0, "pB0"), ("pB", 512, "pB1"))
    for cq in range(2):
        for h in range(16):
            hi5 = h >= 5
            sl = kdb.next()
            kbuf = c.bfc[:, sl * 8192:sl * 8192 + 4096]
            vbuf = c.bfc[:, sl * 8192 + 4096:sl * 8192 + 8192]
            kown = c.bfd[:, 4096 + sl * 1024:4096 + sl * 1024 + 512]
            vown = c.bfd[:, 4096 + sl * 1024 + 512:4096 + sl * 1024 + 1024]
            ksrc = ag_dram[:].rearrange("(r x) cc -> x r cc", x=4096)[
                h * 128:(h + 1) * 128, :, cq * 512:(cq + 1) * 512]
            kdb.load(sl, kbuf[:].rearrange("p (r cc) -> p r cc", cc=512), ksrc)
            for r in range(8):
                vsrc = dap(ag_dram,
                           (r * 4096 + 2048) * 1024 + cq * 512 * 2048 + h * 128,
                           [(2048, 128), (128 * 2048, 4), (1, 128)])
                kdb.load(sl, vbuf[:, r * 512:(r + 1) * 512].rearrange(
                    "p (t f) -> p t f", f=128), vsrc)
            kdb.load(sl, kown, own_dram[h * 128:(h + 1) * 128,
                                        cq * 512:(cq + 1) * 512])
            vsrc2 = dap(own_dram, 2048 * 1024 + cq * 512 * 2048 + h * 128,
                        [(2048, 128), (128 * 2048, 4), (1, 128)])
            ldt = kdb.load(sl, vown[:].rearrange("p (t f) -> p t f", f=128), vsrc2)
            qh = c.bfb[:, h * 1024 + cq * 512:h * 1024 + cq * 512 + 512]
            steps = [("ag", J) for J in range(32)] + \
                    [("lo", Jl) for Jl in (3, 2, 1, 0)]
            pend = None
            pend2 = None
            first = [True]
            pvlast = [None, None]

            def flush(last, pend_=None):
                kind, idx, w, n, tss = (pend_ or pend)[:5]
                st_ = first[0]
                if kind == "ag":
                    lh = vbuf[:, idx * 128:(idx + 1) * 128]
                    co = 0
                else:
                    lh = vown[:, idx * 128:(idx + 1) * 128]
                    co = idx * 128
                aft = [tss, (puse(c, "pE") if st_ else None)]
                pv = b.do("tensor", lambda eo, lh=lh, w=w, n=n, co=co, st_=st_,
                          last=last: eo.matmul(
                              c.pE[:, co:co + n], lh, w[:, 0:n],
                              start=st_, stop=last), after=aft, inc=True)
                dn = b.do("tensor", lambda eo, w=w, n=n, co=co, st_=st_,
                          last=last: eo.matmul(
                              c.pF[:, co:co + n], io.ones128, w[:, 0:n],
                              start=st_, stop=last),
                          after=[puse(c, "pF") if st_ else None], inc=True)
                first[0] = False
                pvlast[0], pvlast[1] = pv, dn
                return dn

            si = 0
            for kind, idx in steps:
                pn, poff, pkey = SPS[si % 4]
                ps = getattr(c, pn)[:, poff:poff + 512]
                wsl = si % 4
                w = c.bfd[:, 6144 + wsl * 512:6144 + wsl * 512 + 512]
                if kind == "ag":
                    n = 512
                    mm = b.do("tensor", lambda eo, idx=idx, ps=ps, kbuf=kbuf,
                              qh=qh: eo.matmul(
                                  ps[:, 0:512], kbuf[:, idx * 128:(idx + 1) * 128],
                                  qh, start=True, stop=True),
                              after=(ldt, puse(c, pkey)), inc=True)
                else:
                    n = 512 - idx * 128
                    mm = b.do("tensor", lambda eo, idx=idx, n=n, ps=ps,
                              kown=kown, qh=qh: eo.matmul(
                                  ps[:, 0:n], kown[:, idx * 128:(idx + 1) * 128],
                                  qh[:, idx * 128:512], start=True, stop=True),
                              after=(ldt, puse(c, pkey)), inc=True)
                if kind == "ag" and si % 2 == 0:
                    held = (kind, idx, w, n, mm, pkey, wsl)
                    si += 1
                    continue
                if pend is not None:
                    wpar[pend[6]] = flush(False)
                    pend = pend2
                    pend2 = None
                if pend is not None:
                    wpar[pend[6]] = flush(False)
                    pend = None
                if kind == "ag":
                    k0, i0, w0, n0, mm0, pk0, ws0 = held
                    wpair = c.bfd[:, 6144 + ws0 * 512:6144 + ws0 * 512 + 1024]
                    pspair = getattr(c, pn)[:]
                    ex = b.do("scalar", lambda eo, wpair=wpair, pspair=pspair:
                              eo.activation(wpair, pspair, AF.Exp),
                              after=(mm, wpar[ws0], wpar[wsl], la), inc=True)
                    prel(c, pk0, ex)
                    prel(c, pkey, ex)
                else:
                    ex = b.do("scalar", lambda eo, ps=ps, w=w, n=n: eo.activation(
                        w[:, 0:n], ps[:, 0:n], AF.Exp),
                        after=(mm, wpar[wsl], la), inc=True)
                    prel(c, pkey, ex)
                if kind == "ag":
                    def do_ts(w_, idx_):
                        if hi5:
                            return b.do("vector", lambda eo, w_=w_, idx_=idx_:
                                        eo.tensor_scalar(
                                            w_, w_,
                                            c.fs[:, (cq * 16 + h) * 32 + idx_:
                                                 (cq * 16 + h) * 32 + idx_ + 1],
                                            None, ALU.mult), after=(ex,),
                                        inc=True)
                        t_ = None
                        for jt in range(4):
                            t_ = b.do("vector", lambda eo, w_=w_, jt=jt,
                                      idx_=idx_: eo.tensor_scalar(
                                          w_[:, jt * 128:(jt + 1) * 128],
                                          w_[:, jt * 128:(jt + 1) * 128],
                                          c.fs[:, (cq * 16 + h) * 32 + idx_:
                                               (cq * 16 + h) * 32 + idx_ + 1],
                                          io.bmat[:, h * 4 + jt:h * 4 + jt + 1],
                                          ALU.mult, ALU.mult),
                                      after=(ex,), inc=True)
                        return t_
                    t0_ = do_ts(w0, i0)
                    t1_ = do_ts(w, idx)
                    pend = (k0, i0, w0, n0, t0_, pk0, ws0)
                    pend2 = ("ag", idx, w, n, t1_, pkey, wsl)
                    si += 1
                    continue
                elif False:
                    pass
                else:
                    if hi5:
                        tss = b.do("vector", lambda eo, w=w, n=n, h=h, idx=idx:
                                   eo.tensor_scalar(
                                       w[:, 0:n], w[:, 0:n],
                                       io.f1g[:, h * 4 + idx:h * 4 + idx + 1],
                                       None, ALU.mult), after=(ex,), inc=True)
                    else:
                        tss = None
                        for jt in range(idx, 4):
                            g = math.exp(min(ALIBI_SLOPES[h] * 128.0 * (idx - jt),
                                             0.0))
                            g = float(np.float32(g))
                            tss = b.do("vector", lambda eo, w=w, jt=jt, idx=idx,
                                       h=h, g=g: eo.tensor_scalar(
                                           w[:, (jt - idx) * 128:(jt - idx + 1) * 128],
                                           w[:, (jt - idx) * 128:(jt - idx + 1) * 128],
                                           io.f1m[:, h:h + 1], g,
                                           ALU.mult, ALU.mult),
                                       after=(ex,), inc=True)
                    tss = b.do("vector", lambda eo, w=w: eo.tensor_mul(
                        w[:, 0:128], w[:, 0:128], io.tri), inc=True)
                pend = (kind, idx, w, n, tss, pkey, wsl)
                si += 1
            if pend2 is not None:
                wpar[pend[6]] = flush(False)
                pend = pend2
            wpar[pend[6]] = flush(True)
            kdb.done(sl, pvlast[1])
            crow = c.fs[0:1, 1024:1536]
            crb = c.fs[:].bitcast(BF16)[0:1, 3072:3584]
            r1 = b.do("scalar", lambda eo: eo.activation(
                crow, c.pF[0:1, 0:512], AF.Ln, scale=2.0 ** -32),
                after=[pvlast[1], puse(c, "crow")], inc=True)
            r1b = b.do("scalar", lambda eo: eo.activation(
                crow, crow, AF.Exp, scale=-1.0, bias=-22.18070977791825),
                after=(r1,), inc=True)
            prel(c, "pF", r1b)
            r2 = b.do("vector", lambda eo: eo.tensor_copy(crb, crow),
                      after=[r1b, puse(c, "crb")], inc=True)
            pnc = "pC"
            cbm = b.do("tensor", lambda eo: eo.matmul(
                c.pC[:, 0:512], io.onesrow_b, crb, start=True, stop=True),
                after=(r2, puse(c, pnc)), inc=True)
            prel(c, "crb", cbm)
            prel(c, "crow", cbm)
            cbs = c.bfd[:].bitcast(F32)[:, 1024:1536]
            r3 = b.do("vector", lambda eo: eo.tensor_copy(cbs, c.pC[:, 0:512]),
                      after=(cbm,), inc=True)
            prel(c, pnc, r3)
            av = b.do("vector", lambda eo, qh=qh: eo.tensor_mul(
                qh, c.pE[:, 0:512], cbs), after=[pvlast[0], r3], inc=True)
            prel(c, "pE", av)
    b.barrier()


def emit_l1_oproj(c, io):
    b = c.b
    wdb = DBuf(c, "wp")
    for mc in range(16):
        sl = wdb.next()
        dst = wslab_dst(c, (c.wb, sl * 2048), 16)
        ld = wdb.load(sl, dst, wslab_src(io.wproj, mc))
        pn = "pA" if mc % 2 == 0 else "pB"
        ps = getattr(c, pn)
        toks = []
        for kc in range(16):
            for nt in range(2):
                st, sp = kc == 0, kc == 15
                aft = [ld, puse(c, pn)] if st and nt == 0 else []
                mm = b.do("tensor", lambda eo, kc=kc, st=st, sp=sp, ps=ps, nt=nt,
                          sl=sl: eo.matmul(
                              ps[:, nt * 512:nt * 512 + 512],
                              c.wb[:, sl * 2048 + kc * 128:sl * 2048 + (kc + 1) * 128],
                              c.bfb[:, kc * 1024 + nt * 512:kc * 1024 + nt * 512 + 512],
                              start=st, stop=sp), after=aft, inc=sp)
                if sp:
                    toks.append(mm)
        ad = b.do("vector", lambda eo, ps=ps, mc=mc: eo.scalar_tensor_tensor(
            c.x[:, mc * 1024:(mc + 1) * 1024], ps[:], io.bproj[:, mc:mc + 1],
            c.x[:, mc * 1024:(mc + 1) * 1024], ALU.add, ALU.add),
            after=toks, inc=True)
        prel(c, pn, ad)
        wdb.done(sl, ad)
    b.barrier()


def emit_l1_ffn(c, io, precast=True):
    b = c.b
    toks = []
    for nt in range(2):
        pnm = "pE" if nt == 0 else "pF"
        ps = getattr(c, pnm)
        mm = None
        for kc in range(16):
            st, sp = kc == 0, kc == 15
            aft = [puse(c, pnm)] if st else []
            mm = b.do("tensor", lambda eo, kc=kc, st=st, sp=sp, nt=nt, ps=ps:
                      eo.matmul(ps[0:1, 0:512], io.onescol_b,
                                c.bfa[:, kc * 1024 + nt * 512:kc * 1024 + nt * 512 + 512],
                                start=st, stop=sp), after=aft, inc=sp)
        toks.append(mm)
    m128x = c.fs[0:1, 0:1024]
    hix = c.fs[:].bitcast(BF16)[0:1, 2048:3072]
    lox = c.fs[:].bitcast(BF16)[0:1, 3072:4096]
    for nt in range(2):
        psr = (c.pE if nt == 0 else c.pF)[0:1, 0:512]
        t = b.do("vector", lambda eo, nt=nt, psr=psr: eo.tensor_scalar(
            m128x[0:1, nt * 512:nt * 512 + 512], psr, -128.0, None, ALU.mult),
            after=toks, inc=True)
        prel(c, "pE" if nt == 0 else "pF", t)
    t1 = b.do("vector", lambda eo: eo.tensor_copy(hix, m128x), inc=True)
    xc = b.do("vector", lambda eo: eo.tensor_tensor(lox, m128x, hix,
              ALU.subtract), inc=True)
    if precast:
        qg_r = io.qgB[:].rearrange("(kc p) m -> p kc m", p=128)
        qu_r = io.quB[:].rearrange("(kc p) m -> p kc m", p=128)
        qd_r = io.qdB[:].rearrange("(r p) m -> p r m", p=128)
        wq_ = "sync"
        pct = getattr(c, "pc_tok", None)
    else:
        qg_r = io.qgT[:].rearrange("(kc p) m -> p kc m", p=128)
        qu_r = io.quT[:].rearrange("(kc p) m -> p kc m", p=128)
        qd_r = io.qdT[:].rearrange("(r p) m -> p r m", p=128)
        wq_ = "gpsimd"
        pct = None
    wg = DBuf(c, "fg", queue=wq_)
    wu = DBuf(c, "fu", queue=wq_)
    wd = DBuf(c, "fd", queue=wq_)
    if pct is not None:
        for db in (wg, wu, wd):
            db.cons = [pct] * db.nslots
    bfdF = c.bfd[:].bitcast(F32)
    prev_sl = None
    last_down = None
    for iq in range(4):
        act_toks = [None] * 16
        for ic in range(16):
            i64 = iq * 16 + ic
            gld, uld = [], []
            for hf in range(2):
                slg = wg.next()
                gld.append((slg, wg.load(
                    slg, wslab_dst(c, (c.wb, slg * 1024), 8),
                    qg_r[:, hf * 8:(hf + 1) * 8, i64 * 128:(i64 + 1) * 128])))
                slu = wu.next()
                uld.append((slu, wu.load(
                    slu, wslab_dst(c, (c.wb, 2048 + slu * 1024), 8),
                    qu_r[:, hf * 8:(hf + 1) * 8, i64 * 128:(i64 + 1) * 128])))
            gts, uts = [], []
            for (psn, base, lds, res) in (("pA", 0, gld, gts),
                                          ("pB", 2048, uld, uts)):
                ps = getattr(c, psn)
                for kc in range(16):
                    for nt in range(2):
                        hf = kc // 8
                        sl_, ld = lds[hf]
                        st = kc == 0
                        aft = ([puse(c, psn), xc] if st and nt == 0 else []) + [ld]
                        b.do("tensor", lambda eo, kc=kc, st=st, nt=nt,
                             sl_=sl_, base=base, ps=ps: eo.matmul(
                                 ps[:, nt * 512:nt * 512 + 512],
                                 c.wb[:, base + sl_ * 1024 + (kc % 8) * 128:
                                      base + sl_ * 1024 + (kc % 8) * 128 + 128],
                                 c.bfa[:, kc * 1024 + nt * 512:
                                       kc * 1024 + nt * 512 + 512],
                                 start=st, stop=False), after=aft, inc=False)
                for nt in range(2):
                    b.do("tensor", lambda eo, nt=nt, ps=ps: eo.matmul(
                        ps[:, nt * 512:nt * 512 + 512], io.onesrow_b,
                        hix[0:1, nt * 512:nt * 512 + 512], start=False,
                        stop=False))
                    mm = b.do("tensor", lambda eo, nt=nt, ps=ps: eo.matmul(
                        ps[:, nt * 512:nt * 512 + 512], io.onesrow_b,
                        lox[0:1, nt * 512:nt * 512 + 512], start=False,
                        stop=True), inc=True)
                    res.append(mm)
            gsb = c.bfd[:, 0:1024]
            usb = c.bfd[:, 1024:2048]
            ssb = c.bfd[:, 2048:3072]
            ge = b.do("vector", lambda eo, i64=i64, gsb=gsb: eo.tensor_scalar(
                gsb, c.pA[:], io.gscol[:, i64:i64 + 1], None, ALU.mult),
                after=gts + [prev_sl], inc=True)
            prel(c, "pA", ge)
            for sl_, _ in gld:
                wg.done(sl_, ge)
            ue = b.do("vector", lambda eo, i64=i64, usb=usb: eo.tensor_scalar(
                usb, c.pB[:], io.uscol[:, i64:i64 + 1], None, ALU.mult),
                after=uts, inc=True)
            prel(c, "pB", ue)
            for sl_, _ in uld:
                wu.done(sl_, ue)
            sv = b.do("scalar", lambda eo, gsb=gsb, ssb=ssb: eo.activation(
                ssb, gsb, AF.Silu), after=(ge,), inc=True)
            prev_sl = sv
            hv = b.do("vector", lambda eo, ic=ic, ssb=ssb, usb=usb: eo.tensor_mul(
                c.bfc[:, ic * 1024:(ic + 1) * 1024], ssb, usb),
                after=(sv, last_down), inc=True)
            act_toks[ic] = hv
        toks2 = []
        for nt in range(2):
            pnm = "pE" if nt == 0 else "pF"
            ps = getattr(c, pnm)
            mm = None
            for kc in range(16):
                st, sp = kc == 0, kc == 15
                aft = [puse(c, pnm), act_toks[kc]] if st else [act_toks[kc]]
                mm = b.do("tensor", lambda eo, kc=kc, st=st, sp=sp, nt=nt, ps=ps:
                          eo.matmul(ps[0:1, 0:512], io.onescol_b,
                                    c.bfc[:, kc * 1024 + nt * 512:kc * 1024 + nt * 512 + 512],
                                    start=st, stop=sp), after=aft, inc=sp)
            toks2.append(mm)
        m128h = bfdF[0:1, 2048:3072]
        hih = c.bfd[0:1, 6144:7168]
        loh = c.bfd[0:1, 7168:8192]
        for nt in range(2):
            psr = (c.pE if nt == 0 else c.pF)[0:1, 0:512]
            t = b.do("vector", lambda eo, nt=nt, psr=psr: eo.tensor_scalar(
                m128h[0:1, nt * 512:nt * 512 + 512], psr, -128.0, None,
                ALU.mult), after=toks2 + [puse(c, "hrow")], inc=True)
            prel(c, "pE" if nt == 0 else "pF", t)
        b.do("vector", lambda eo: eo.tensor_copy(hih, m128h))
        hc_ = b.do("vector", lambda eo: eo.tensor_tensor(loh, m128h, hih,
                   ALU.subtract), inc=True)
        for hc in range(16):
            sl2 = wd.next()
            ld2 = wd.load(sl2, wslab_dst(c, (c.wb, 4096 + sl2 * 2048), 16),
                          qd_r[:, iq * 16:(iq + 1) * 16, hc * 128:(hc + 1) * 128])
            toks3 = []
            ads = []
            for kc in range(16):
                for nt in range(2):
                    pn = "pC" if nt == 0 else "pD"
                    ps = getattr(c, pn)
                    st = kc == 0
                    aft = ([ld2, puse(c, pn), hc_] if st else []) + [act_toks[kc]]
                    b.do("tensor", lambda eo, kc=kc, st=st, nt=nt, sl2=sl2,
                         ps=ps: eo.matmul(
                             ps[:, 0:512],
                             c.wb[:, 4096 + sl2 * 2048 + kc * 128:
                                  4096 + sl2 * 2048 + (kc + 1) * 128],
                             c.bfc[:, kc * 1024 + nt * 512:
                                   kc * 1024 + nt * 512 + 512],
                             start=st, stop=False), after=aft, inc=False)
            for nt in range(2):
                pn = "pC" if nt == 0 else "pD"
                ps = getattr(c, pn)
                b.do("tensor", lambda eo, nt=nt, ps=ps: eo.matmul(
                    ps[:, 0:512], io.onesrow_b,
                    hih[0:1, nt * 512:nt * 512 + 512], start=False, stop=False))
                mm = b.do("tensor", lambda eo, nt=nt, ps=ps: eo.matmul(
                    ps[:, 0:512], io.onesrow_b,
                    loh[0:1, nt * 512:nt * 512 + 512], start=False, stop=True),
                    inc=True)
                toks3.append(mm)
                ad = b.do("vector", lambda eo, ps=ps, hc=hc, nt=nt:
                          eo.scalar_tensor_tensor(
                              c.x[:, hc * 1024 + nt * 512:hc * 1024 + nt * 512 + 512],
                              ps[:, 0:512], io.dscol[:, hc:hc + 1],
                              c.x[:, hc * 1024 + nt * 512:hc * 1024 + nt * 512 + 512],
                              ALU.mult, ALU.add), after=(mm,), inc=True)
                prel(c, pn, ad)
                ads.append(ad)
            wd.done(sl2, ads[-1])
            if hc == 15:
                last_down = toks3[-1]
                prel(c, "hrow", toks3[-1])
    b.barrier()


def emit_out(c, io, o_out):
    b = c.b
    stage = c.bfc[:].bitcast(F32)
    prev_st = [None, None]
    for tt in range(8):
        sl = tt % 2
        stg = stage[:, sl * 2048:(sl + 1) * 2048]
        evs = None
        for kc in range(16):
            pn = "pE" if kc % 2 == 0 else "pF"
            pf = getattr(c, pn)[:, 0:128]
            tr = b.do("tensor", lambda eo, pf=pf, kc=kc, tt=tt: eo.transpose(
                pf, c.x[:, kc * 1024 + tt * 128:kc * 1024 + (tt + 1) * 128],
                io.identf), after=(puse(c, pn),), inc=True)
            evs = b.do("vector", lambda eo, pf=pf, kc=kc, stg=stg: eo.tensor_copy(
                stg[:, kc * 128:(kc + 1) * 128], pf),
                after=(tr, prev_st[sl]), inc=True)
            prel(c, pn, evs)
        st_ = b.do("sync", lambda eo, tt=tt, stg=stg: eo.dma_start(
            o_out[tt * 128:(tt + 1) * 128, :], stg), after=(evs,),
            dma=b.S("oo%d" % sl))
        prev_st[sl] = st_
    b.do("sync", lambda eo: eo.nop(),
         after=[(b.S("oo0"), b.S("oo0").n), (b.S("oo1"), b.S("oo1").n)])


def build_launch1():
    nc = bass.Bass("TRN2", target_bir_lowering=False, debug=False)
    bld = Bld(nc)
    io = IO()
    io.h_in = nc.dram_tensor("h_in", [LOC, D], F32, kind="ExternalInput")
    decl_common_inputs(nc, io)
    decl_l0_inputs(nc, io)
    decl_moe_inputs(nc, io)
    DT = nc.dram_tensor
    io.wqkv = DT("wqkv", [D, 3 * D], F32, kind="ExternalInput")
    io.bqkv = _sb_const(nc, "bqkv", [128, 48], F32)
    io.onescol_f = _sb_const(nc, "onescol_f", [128, 1], F32)
    kvout = DT("kvout", [4096, OWN], BF16, kind="ExternalOutput")
    x1_d = DT("x1_d", [D, OWN], F32, kind="ExternalOutput")
    xn1_d = DT("xn1_d", [D, OWN], BF16, kind="ExternalOutput")
    q_d = DT("q_d", [D, OWN], BF16, kind="ExternalOutput")
    c = alloc_common(nc, bld)
    c.io = io
    for v in (1e-5, 1.0 / 128.0, 1e-6, 1.28e-4):
        reg_const(c, v)
    emit_const_init(c)
    emit_const_loads(c)
    bld.barrier()
    emit_p0(c, io)
    emit_l0_kvq(c, io)
    emit_l0_scores(c, io)
    emit_l0_out(c, io)
    emit_moe(c, io)
    emit_ln1(c, io)
    emit_l1_qkv(c, io, kvout)
    b = bld
    for kc in range(16):
        b.do("sync", lambda eo, kc=kc: eo.dma_start(
            x1_d[:].rearrange("(k p) t -> k p t", p=128)[kc],
            c.x[:, kc * 1024:(kc + 1) * 1024]), dma=b.S("o0"))
        b.do("sync", lambda eo, kc=kc: eo.dma_start(
            xn1_d[:].rearrange("(k p) t -> k p t", p=128)[kc],
            c.bfa[:, kc * 1024:(kc + 1) * 1024]), dma=b.S("o0"))
        t = b.do("sync", lambda eo, kc=kc: eo.dma_start(
            q_d[:].rearrange("(k p) t -> k p t", p=128)[kc],
            c.bfb[:, kc * 1024:(kc + 1) * 1024]), dma=b.S("o0"))
    b.do("sync", lambda eo, t=t: eo.nop(), after=(t,))
    bld.finalize()
    return nc


def build_launch2():
    nc = bass.Bass("TRN2", target_bir_lowering=False, debug=False)
    bld = Bld(nc)
    io = IO()
    decl_common_inputs(nc, io)
    decl_l1_inputs(nc, io, "l2")
    DT = nc.dram_tensor
    io.onesrow_b = _sb_const(nc, "onesrow_b", [1, 128], BF16)
    io.onesrow_f = _sb_const(nc, "onesrow_f", [1, 128], F32)
    io.onescol_b = _sb_const(nc, "onescol_b", [128, 1], BF16)
    x1_d = DT("x1_d", [D, OWN], F32, kind="ExternalInput")
    xn1_d = DT("xn1_d", [D, OWN], BF16, kind="ExternalInput")
    q_d = DT("q_d", [D, OWN], BF16, kind="ExternalInput")
    kvin = DT("kvin", [4096, OWN], BF16, kind="ExternalInput")
    agx = DT("agx", [8 * 4096, OWN], BF16, kind="ExternalInput")
    o_out = DT("o_out", [OWN, D], F32, kind="ExternalOutput")
    c = alloc_common(nc, bld)
    c.io = io
    reg_const(c, 2.0 ** -32)
    reg_const(c, -1.0)
    reg_const(c, -22.18070977791825)
    io.qgB = DT("qgB", [D, IFF], BF16)
    io.quB = DT("quB", [D, IFF], BF16)
    io.qdB = DT("qdB", [IFF, D], BF16)
    emit_const_init(c)
    emit_const_loads(c)
    b = bld
    pc = None
    for (src, dst, nch) in ((io.qgT, io.qgB, 8), (io.quT, io.quB, 8),
                            (io.qdT, io.qdB, 8)):
        rows = src.shape[0] // nch
        for ch in range(nch):
            pc = b.do("gpsimd", lambda eo, src=src, dst=dst, ch=ch, rows=rows:
                      eo.dma_start(dst[ch * rows:(ch + 1) * rows, :],
                                   src[ch * rows:(ch + 1) * rows, :]),
                      dma=b.S("bgc"))
    c.pc_tok = pc
    t1 = b.do("sync", lambda eo: eo.dma_start(
        c.x[:].rearrange("p (k t) -> p k t", t=1024),
        x1_d[:].rearrange("(k p) t -> p k t", p=128)), dma=b.S("i0"))
    t2 = b.do("sync", lambda eo: eo.dma_start(
        c.bfa[:].rearrange("p (k t) -> p k t", t=1024),
        xn1_d[:].rearrange("(k p) t -> p k t", p=128)), dma=b.S("i1"))
    t3 = b.do("sync", lambda eo: eo.dma_start(
        c.bfb[:].rearrange("p (k t) -> p k t", t=1024),
        q_d[:].rearrange("(k p) t -> p k t", p=128)), dma=b.S("i2"))
    b.do("sync", lambda eo: eo.nop(), after=(t1, t2, t3))
    bld.barrier()
    emit_l1_ffn(c, io)
    emit_l1_attn(c, io, agx, kvin)
    emit_l1_oproj(c, io)
    emit_out(c, io, o_out)
    bld.finalize()
    return nc


def build_ln1_test():
    nc = bass.Bass("TRN2", target_bir_lowering=False, debug=False)
    bld = Bld(nc)
    io = IO()
    decl_common_inputs(nc, io)
    io.onescol_f = _sb_const(nc, "onescol_f", [128, 1], F32)
    io.onesrow_b = _sb_const(nc, "onesrow_b", [1, 128], BF16)
    DT = nc.dram_tensor
    x_in = DT("x_in", [D, OWN], F32, kind="ExternalInput")
    o_xn = DT("o_xn", [D, OWN], BF16, kind="ExternalOutput")
    c = alloc_common(nc, bld)
    c.io = io
    reg_const(c, 1e-5)
    emit_const_init(c)
    emit_const_loads(c)
    b = bld
    t1 = b.do("sync", lambda eo: eo.dma_start(
        c.x[:].rearrange("p (k t) -> p k t", t=1024),
        x_in[:].rearrange("(k p) t -> p k t", p=128)), dma=b.S("i0"))
    b.do("sync", lambda eo: eo.nop(), after=(t1,))
    bld.barrier()
    emit_ln1(c, io)
    for kc in range(16):
        t = b.do("sync", lambda eo, kc=kc: eo.dma_start(
            o_xn[:].rearrange("(k p) t -> k p t", p=128)[kc],
            c.bfa[:, kc * 1024:(kc + 1) * 1024]), dma=b.S("o0"))
    b.do("sync", lambda eo, t=t: eo.nop(), after=(t,))
    bld.finalize()
    return nc


def build_fused():
    nc = bass.Bass("TRN2", target_bir_lowering=False, debug=False)
    bld = Bld(nc)
    io = IO()
    io.h_in = nc.dram_tensor("h_in", [LOC, D], F32, kind="ExternalInput")
    decl_common_inputs(nc, io)
    decl_l0_inputs(nc, io)
    decl_moe_inputs(nc, io)
    decl_l1_inputs(nc, io, "fused")
    DT = nc.dram_tensor
    io.wqkv = DT("wqkv", [D, 3 * D], F32, kind="ExternalInput")
    io.bqkv = _sb_const(nc, "bqkv", [128, 48], F32)
    io.qgB = DT("qgB", [D, IFF], BF16)
    io.quB = DT("quB", [D, IFF], BF16)
    io.qdB = DT("qdB", [IFF, D], BF16)
    ag_in = DT("ag_in", [4096, OWN], BF16)
    ag_out = DT("ag_out", [8 * 4096, OWN], BF16)
    o_out = DT("o_out", [OWN, D], F32, kind="ExternalOutput")
    c = alloc_common(nc, bld)
    c.io = io
    for v in (1e-5, 1.0 / 128.0, 1e-6, 1.28e-4):
        reg_const(c, v)
    reg_const(c, 2.0 ** -32)
    reg_const(c, -1.0)
    reg_const(c, -22.18070977791825)
    emit_const_init(c)
    emit_const_loads(c)
    bld.barrier()
    b = bld
    for (src, dst, nch) in ((io.qgT, io.qgB, 8), (io.quT, io.quB, 8),
                            (io.qdT, io.qdB, 8)):
        rows = src.shape[0] // nch
        for ch in range(nch):
            b.bg(lambda eo, src=src, dst=dst, ch=ch, rows=rows: eo.dma_start(
                dst[ch * rows:(ch + 1) * rows, :],
                src[ch * rows:(ch + 1) * rows, :]))
    emit_p0(c, io)
    emit_l0_kvq(c, io)
    emit_l0_scores(c, io)
    emit_l0_out(c, io)
    emit_moe(c, io)
    while bld.bg_dmas:
        bld.drip()
    c.pc_tok = (bld.S("bgc"), bld.S("bgc").n)
    emit_ln1(c, io)

    def kick(aft):
        b.do("gpsimd", lambda eo: eo.collective_compute(
            "AllGather", ALU.bypass, replica_groups=[list(range(8))],
            ins=[ag_in.ap().opt()], outs=[ag_out.ap().opt()]),
            after=aft, inc=True)
    emit_l1_qkv(c, io, ag_in, kick=kick)
    emit_l1_ffn(c, io)
    bld.barrier()
    emit_l1_attn(c, io, ag_out, ag_in)
    emit_l1_oproj(c, io)
    emit_out(c, io, o_out)
    bld.finalize()
    return nc




def _chunks_of(core):
    return core, 7 - core


def _h_rows(core, h_full):
    a, bb = _chunks_of(core)
    rows = np.zeros((LOC, D), np.float32)
    for i, (b_, s0) in enumerate([(0, a * CH), (1, bb * CH)]):
        halo = np.zeros((CH, D), np.float32)
        if s0 >= CH:
            halo = h_full[b_, s0 - CH:s0]
        rows[i * 1024:(i + 1) * 1024] = np.concatenate(
            [halo, h_full[b_, s0:s0 + CH]], 0)
    return np.ascontiguousarray(rows)


def _positions(core):
    a, bb = _chunks_of(core)
    own = np.concatenate([np.arange(a * CH, (a + 1) * CH),
                          np.arange(bb * CH, (bb + 1) * CH)])
    loc = np.concatenate([np.arange(a * CH - CH, (a + 1) * CH),
                          np.arange(bb * CH - CH, (bb + 1) * CH)])
    return own, loc


def _rope_tables(pos):
    n = len(pos)
    inv = 1.0 / (10000.0 ** (np.arange(0, 64, 2) / 64.0))
    ang = np.maximum(pos, 0)[None, :] * inv[:, None]
    C = np.ones((128, n), np.float64)
    S = np.zeros((128, n), np.float64)
    C[0:32] = np.cos(ang)
    C[32:64] = np.cos(ang)
    S[0:32] = -np.sin(ang)
    S[32:64] = np.sin(ang)
    return C.astype(BF), S.astype(BF)


def _rperm():
    R = np.zeros((128, 128), np.float32)
    for j in range(32):
        R[j, 32 + j] = 1.0
        R[32 + j, j] = 1.0
    return R.astype(BF)


def _l0_erow(core):
    a, bb = _chunks_of(core)
    out = np.zeros((1, 5120), np.float64)
    kj = np.arange(640)
    for cq, st in ((0, a * CH), (1, bb * CH)):
        for j in range(4):
            exists = (st - CH + j * 128 + kj) >= 0
            out[0, (cq * 4 + j) * 640:(cq * 4 + j) * 640 + 640] = np.where(
                exists, 0.0, NEG)
    return out.astype(BF)


def _l0_winm():
    qi = np.arange(128)[:, None]
    kj = np.arange(128)[None, :]
    first = np.where(kj > qi, 0.0, NEG)
    last = np.where(kj <= qi, 0.0, NEG)
    return np.concatenate([first, last], 1).astype(BF)


def _aprime(core):
    a, bb = _chunks_of(core)
    out = np.zeros((128, 1024), np.float64)
    p = np.arange(128)
    for cq, own_chunk in ((0, a), (1, bb)):
        for h in range(16):
            s = ALIBI_SLOPES[h]
            for r in range(8):
                for jm in range(4):
                    J = r * 4 + jm
                    jg = (r * 4 + jm) if cq == 0 else ((7 - r) * 4 + jm)
                    if jg >= 4 * own_chunk:
                        continue
                    out[:, (cq * 16 + h) * 32 + J] = np.exp(
                        s * (128.0 * jg + p - 64.0 - 512.0 * own_chunk))
    return out.astype(np.float32)


def _l1_consts():
    f1 = np.zeros((128, 16), np.float64)
    bmat = np.zeros((128, 64), np.float64)
    p = np.arange(128)
    for h in range(16):
        s = ALIBI_SLOPES[h]
        f1[:, h] = np.exp(s * (p - 64.0))
        for jt in range(4):
            bmat[:, h * 4 + jt] = np.exp(-s * 128.0 * jt)
    tri = (np.arange(128)[:, None] <= np.arange(128)[None, :]).astype(BF)
    f1g = np.ones((128, 64), np.float64)
    for h in range(5, 16):
        s_ = ALIBI_SLOPES[h]
        for jl in range(4):
            f1g[:, h * 4 + jl] = np.exp(s_ * (np.arange(128) - 64.0 + 128.0 * jl))
    return {"f1m": f1.astype(np.float32), "bmat": bmat.astype(np.float32),
            "tri": tri, "ones128": np.ones((128, 128), BF),
            "onescol_f": np.ones((128, 1), np.float32),
            "f1g": f1g.astype(np.float32)}


_NC_CACHE = {}


def _get_nc(which):
    if which not in _NC_CACHE:
        _NC_CACHE[which] = (build_launch1 if which == 1 else build_launch2)()
    return _NC_CACHE[which]


def kernel_unfused(L0_attn_h, L0_attn_wq_comp, L0_attn_wq_up, L0_attn_wk, L0_attn_wv,
           L0_attn_q_norm_w, L0_attn_k_norm_w, L0_attn_sink_logit,
           L0_attn_out_w1, L0_attn_out_w2, L0_ffn_router_weight,
           L0_ffn_gate_up_proj, L0_ffn_down_proj, L0_ffn_sh_gate_w,
           L0_ffn_sh_up_w, L0_ffn_sh_down_w, L0_ffn_shared_gate_w,
           L1_attn_w_qkv, L1_attn_b_qkv, L1_attn_w_proj, L1_attn_b_proj,
           L1_ffn_gate_w, L1_ffn_gate_scale, L1_ffn_up_w, L1_ffn_up_scale,
           L1_ffn_down_w, L1_ffn_down_scale):
    h_full = np.asarray(L0_attn_h, np.float32)
    C = np.ascontiguousarray
    shared1 = {
        "wqc": C(np.asarray(L0_attn_wq_comp, np.float32)),
        "wqu": C(np.asarray(L0_attn_wq_up, np.float32)),
        "wk": C(np.asarray(L0_attn_wk, np.float32)),
        "wv": C(np.asarray(L0_attn_wv, np.float32)),
        "w1": C(np.asarray(L0_attn_out_w1, np.float32)),
        "w2": C(np.asarray(L0_attn_out_w2, np.float32)),
        "qnw_row": C(np.asarray(L0_attn_q_norm_w, np.float32).reshape(1, 128)),
        "knw_row": C(np.asarray(L0_attn_k_norm_w, np.float32).reshape(1, 128)),
        "sink_row": C(np.asarray(L0_attn_sink_logit, np.float32).reshape(1, 16)),
        "routerT": C(np.asarray(L0_ffn_router_weight, np.float32).T),
        "gupT": C(np.asarray(L0_ffn_gate_up_proj, np.float32).transpose(0, 2, 1)),
        "dpT": C(np.asarray(L0_ffn_down_proj, np.float32).transpose(0, 2, 1)),
        "shgT": C(np.asarray(L0_ffn_sh_gate_w, np.float32).T),
        "shuT": C(np.asarray(L0_ffn_sh_up_w, np.float32).T),
        "shdT": C(np.asarray(L0_ffn_sh_down_w, np.float32).T),
        "sgT": C(np.asarray(L0_ffn_shared_gate_w, np.float32).T),
        "wqkv": C(np.asarray(L1_attn_w_qkv, np.float32)),
        "bqkv": C(np.asarray(L1_attn_b_qkv, np.float32).reshape(48, 128).T),
        "winm": _l0_winm(),
        "identb": np.eye(128).astype(BF),
        "identf": np.eye(128, dtype=np.float32),
        "rperm": _rperm(),
        "onescol_b": np.ones((128, 1), BF),
        "onesrow_b": np.ones((1, 128), BF),
        "onesrow_f": np.ones((1, 128), np.float32),
        "onescol_f": np.ones((128, 1), np.float32),
    }
    in1 = []
    for cc in range(NCORE):
        own, loc = _positions(cc)
        cq_, sq_ = _rope_tables(own)
        ck_, sk_ = _rope_tables(loc)
        d = {"h_in": _h_rows(cc, h_full), "cqd": cq_, "sqd": sq_,
             "ckd": ck_, "skd": sk_, "erow": _l0_erow(cc)}
        d.update(shared1)
        in1.append(d)
    nc1 = _get_nc(1)
    res1 = run_bass_kernel_spmd(nc1, in1, core_ids=list(range(NCORE)))
    agx = np.concatenate([res1.results[cc]["kvout"] for cc in range(NCORE)], 0)
    lc = _l1_consts()
    shared2 = {
        "wproj": C(np.asarray(L1_attn_w_proj, np.float32)),
        "bproj": C(np.asarray(L1_attn_b_proj, np.float32).reshape(16, 128).T),
        "qgT": C(np.asarray(L1_ffn_gate_w, np.int32).T),
        "quT": C(np.asarray(L1_ffn_up_w, np.int32).T),
        "qdT": C(np.asarray(L1_ffn_down_w, np.int32).T),
        "gscol": C(np.asarray(L1_ffn_gate_scale, np.float32).reshape(64, 128).T),
        "uscol": C(np.asarray(L1_ffn_up_scale, np.float32).reshape(64, 128).T),
        "dscol": C(np.asarray(L1_ffn_down_scale, np.float32).reshape(16, 128).T),
        "agx": agx,
        "identb": shared1["identb"], "identf": shared1["identf"],
        "onesrow_b": shared1["onesrow_b"], "onesrow_f": shared1["onesrow_f"],
        "onescol_b": shared1["onescol_b"],
    }
    shared2.update(lc)
    in2 = []
    for cc in range(NCORE):
        d = {"x1_d": res1.results[cc]["x1_d"],
             "xn1_d": res1.results[cc]["xn1_d"],
             "q_d": res1.results[cc]["q_d"],
             "kvin": res1.results[cc]["kvout"],
             "aprd": _aprime(cc)}
        d.update(shared2)
        in2.append(d)
    nc2 = _get_nc(2)
    res2 = run_bass_kernel_spmd(nc2, in2, core_ids=list(range(NCORE)))
    out = np.zeros((B, T, D), np.float32)
    for cc in range(NCORE):
        a, bb = _chunks_of(cc)
        o = res2.results[cc]["o_out"]
        out[0, a * CH:(a + 1) * CH] = o[0:512]
        out[1, bb * CH:(bb + 1) * CH] = o[512:1024]
    return out




def kernel_fused(**inputs):
    h_full = np.asarray(inputs["L0_attn_h"], np.float32)
    C = np.ascontiguousarray
    lc = _l1_consts()
    shared = {
        "wqc": C(np.asarray(inputs["L0_attn_wq_comp"], np.float32)),
        "wqu": C(np.asarray(inputs["L0_attn_wq_up"], np.float32)),
        "wk": C(np.asarray(inputs["L0_attn_wk"], np.float32)),
        "wv": C(np.asarray(inputs["L0_attn_wv"], np.float32)),
        "w1": C(np.asarray(inputs["L0_attn_out_w1"], np.float32)),
        "w2": C(np.asarray(inputs["L0_attn_out_w2"], np.float32)),
        "qnw_row": C(np.asarray(inputs["L0_attn_q_norm_w"], np.float32).reshape(1, 128)),
        "knw_row": C(np.asarray(inputs["L0_attn_k_norm_w"], np.float32).reshape(1, 128)),
        "sink_row": C(np.asarray(inputs["L0_attn_sink_logit"], np.float32).reshape(1, 16)),
        "routerT": C(np.asarray(inputs["L0_ffn_router_weight"], np.float32).T),
        "gupT": C(np.asarray(inputs["L0_ffn_gate_up_proj"], np.float32).transpose(0, 2, 1)),
        "dpT": C(np.asarray(inputs["L0_ffn_down_proj"], np.float32).transpose(0, 2, 1)),
        "shgT": C(np.asarray(inputs["L0_ffn_sh_gate_w"], np.float32).T),
        "shuT": C(np.asarray(inputs["L0_ffn_sh_up_w"], np.float32).T),
        "shdT": C(np.asarray(inputs["L0_ffn_sh_down_w"], np.float32).T),
        "sgT": C(np.asarray(inputs["L0_ffn_shared_gate_w"], np.float32).T),
        "wqkv": C(np.asarray(inputs["L1_attn_w_qkv"], np.float32)),
        "bqkv": C(np.asarray(inputs["L1_attn_b_qkv"], np.float32).reshape(48, 128).T),
        "wproj": C(np.asarray(inputs["L1_attn_w_proj"], np.float32)),
        "bproj": C(np.asarray(inputs["L1_attn_b_proj"], np.float32).reshape(16, 128).T),
        "qgT": C(np.asarray(inputs["L1_ffn_gate_w"], np.int32).T),
        "quT": C(np.asarray(inputs["L1_ffn_up_w"], np.int32).T),
        "qdT": C(np.asarray(inputs["L1_ffn_down_w"], np.int32).T),
        "gscol": C(np.asarray(inputs["L1_ffn_gate_scale"], np.float32).reshape(64, 128).T),
        "uscol": C(np.asarray(inputs["L1_ffn_up_scale"], np.float32).reshape(64, 128).T),
        "dscol": C(np.asarray(inputs["L1_ffn_down_scale"], np.float32).reshape(16, 128).T),
        "winm": _l0_winm(),
        "identb": np.eye(128).astype(BF),
        "identf": np.eye(128, dtype=np.float32),
        "rperm": _rperm(),
        "onescol_b": np.ones((128, 1), BF),
        "onesrow_b": np.ones((1, 128), BF),
        "onesrow_f": np.ones((1, 128), np.float32),
    }
    shared.update(lc)
    ins = []
    for cc in range(NCORE):
        own, loc = _positions(cc)
        cq_, sq_ = _rope_tables(own)
        ck_, sk_ = _rope_tables(loc)
        d = {"h_in": _h_rows(cc, h_full), "cqd": cq_, "sqd": sq_,
             "ckd": ck_, "skd": sk_, "erow": _l0_erow(cc),
             "aprd": _aprime(cc)}
        d.update(shared)
        ins.append(d)
    if "fused" not in _NC_CACHE:
        _NC_CACHE["fused"] = build_fused()
    res = run_bass_kernel_spmd(_NC_CACHE["fused"], ins, core_ids=list(range(NCORE)))
    out = np.zeros((B, T, D), np.float32)
    for cc in range(NCORE):
        a, bb = _chunks_of(cc)
        o = res.results[cc]["o_out"]
        out[0, a * CH:(a + 1) * CH] = o[0:512]
        out[1, bb * CH:(bb + 1) * CH] = o[512:1024]
    return out


def kernel(**inputs):
    return kernel_fused(**inputs)


# version 16
# speedup vs baseline: 1.0136x; 1.0136x over previous
import math
import numpy as np
import ml_dtypes
import concourse.bass as bass
import concourse.mybir as mybir
from concourse.bass_utils import run_bass_kernel_spmd

F32 = mybir.dt.float32
BF16 = mybir.dt.bfloat16
I32 = mybir.dt.int32
AF = mybir.ActivationFunctionType
ALU = mybir.AluOpType
AX = mybir.AxisListType
BF = ml_dtypes.bfloat16

NCORE = 8
B, T, D = 2, 4096, 2048
CH = 512
OWN = 1024
LOC = 2048
KCD = 16
H, HD = 16, 128
NEG = -1e30
NE, IMOE, ISH = 8, 1408, 5632
IFF = 8192


class Sem:
    __slots__ = ("h", "n", "name")

    def __init__(self, nc, name):
        self.h = nc.alloc_semaphore(name)
        self.n = 0
        self.name = name


class Bld:
    ENG = ("tensor", "vector", "scalar", "gpsimd", "sync")

    def __init__(self, nc):
        self.nc = nc
        self.q = {e: [] for e in self.ENG}
        self.sem = {e: Sem(nc, "s_" + e) for e in self.ENG}
        self.dsems = {}
        self.eng_dma = {e: {} for e in self.ENG}
        self.waited = {e: {} for e in self.ENG}
        self.nops = 0
        self.bank = {}
        self.bg_dmas = []

    def bg(self, mk):
        self.bg_dmas.append(mk)

    def drip(self):
        if self.bg_dmas:
            mk = self.bg_dmas.pop(0)
            return self.do("gpsimd", mk, dma=self.S("bgc"))
        return None

    def S(self, name):
        if name not in self.dsems:
            self.dsems[name] = Sem(self.nc, "d_" + name)
        return self.dsems[name]

    CHAIN = ("vector", "scalar", "gpsimd")

    def do(self, eng, mk, after=(), inc=False, dma=None, force_own=False):
        s = dma if dma is not None else self.sem[eng]
        es = self.sem[eng]
        waits = []
        wmap = self.waited[eng]
        after = list(after)
        if dma is None and (eng in self.CHAIN or force_own) and es.n > 0:
            after.insert(0, (es, es.n))
        for t in after:
            if t is None:
                continue
            ps, v = t
            if ps is es and not (dma is None and (eng in self.CHAIN or force_own)):
                continue
            w = wmap.get(ps.name)
            if w is None or w < v:
                waits.append((ps, v))
                wmap[ps.name] = v
        amt = 16 if dma is not None else 1
        if dma is None and eng in self.CHAIN:
            inc = True
        tok = None
        if inc or dma is not None:
            s.n += amt
            tok = (s, s.n)
        if dma is not None:
            self.eng_dma[eng][s.name] = (s, s.n)
        self.nops += 1 + len(waits)
        doinc = inc or dma is not None

        def f(eo, waits=waits, mk=mk, doinc=doinc, s=s, amt=amt):
            for ps, v in waits:
                eo.wait_ge(ps.h, v)
            ins = mk(eo)
            if doinc:
                ins.then_inc(s.h, amt)
        self.q[eng].append(f)
        return tok

    def barrier(self):
        toks = {}
        for e in self.ENG:
            aft = list(self.eng_dma[e].values())
            toks[e] = self.do(e, lambda eo: eo.nop(), after=aft, inc=True,
                              force_own=True)
            self.eng_dma[e] = {}
        for e in self.ENG:
            self.do(e, lambda eo: eo.nop(),
                    after=[toks[e2] for e2 in self.ENG if e2 != e])
        self.bank = {}
        return toks

    def finalize(self):
        with self.nc.Block() as block:
            for e in self.ENG:
                lst = self.q[e]
                getattr(block, e)(lambda eo, lst=lst: [f(eo) for f in lst])


def dap(t, offset, dims):
    return bass.AP(t, offset, [list(d) for d in dims])


class Ctx:
    pass


def alloc_common(nc, bld):
    c = Ctx()
    c.nc, c.b = nc, bld
    A = nc.alloc_sbuf_tensor
    c.x = A("x_fm", [128, 16 * 1024], F32)
    c.bfa = A("bfa", [128, 16 * 1024], BF16)
    c.bfb = A("bfb", [128, 16 * 1024], BF16)
    c.bfc = A("bfc", [128, 16 * 1024], BF16)
    c.bfd = A("bfd", [128, 8 * 1024], BF16)
    c.wb = A("wb", [128, 8 * 1024], BF16)
    c.fs = A("fs", [128, 2 * 1024], F32)
    c.cst = A("cst", [128, 16], F32)
    c.sc = A("sc", [128, 192], F32)
    P = nc.alloc_psum_tensor
    c.pA = P("pA", [128, 1024], F32)
    c.pB = P("pB", [128, 1024], F32)
    c.pC = P("pC", [128, 512], F32)
    c.pD = P("pD", [128, 512], F32)
    c.pE = P("pE", [128, 512], F32)
    c.pF = P("pF", [128, 512], F32)
    c.consts = {}
    return c


def reg_const(c, val):
    if val not in c.consts:
        i = len(c.consts)
        assert i < 16
        c.consts[val] = i
        c.nc.const_aps.register(F32, float(val), c.cst[:, i:i + 1])


def emit_const_init(c):
    b = c.b
    for val, i in list(c.consts.items()):
        b.do("vector", lambda eo, v=val, i=i: eo.memset(c.cst[:, i:i + 1], v))
    return b.do("vector", lambda eo: eo.engine_nop(), inc=True)


def puse(c, slot):
    return c.b.bank.get(slot)


def prel(c, slot, tok):
    c.b.bank[slot] = tok


def emit_p0(c, io):
    b = c.b
    stage = c.bfc[:].bitcast(F32)
    h_r = io.h_in[:].rearrange("(i p) f -> i p f", p=128)
    reg_const(c, 1e-5)
    slot_done = [None, None]
    prev = {}
    P0SC = [(c.sc[:, 64 + 16 * par + k:65 + 16 * par + k] for k in range(7))
            for par in range(2)]
    for i in range(16):
        par = i % 2
        sl = stage[:, par * 2048:par * 2048 + 2048]
        ld = b.do("sync", lambda eo, sl=sl, i=i: eo.dma_start(sl, h_r[i]),
                  after=[slot_done[par]], dma=b.S("st%d" % par))
        base = 64 + 16 * par
        s_col = c.sc[:, base:base + 1]
        q_col = c.sc[:, base + 1:base + 2]
        mu = c.sc[:, base + 2:base + 3]
        ex2 = c.sc[:, base + 3:base + 4]
        var = c.sc[:, base + 4:base + 5]
        r = c.sc[:, base + 5:base + 6]
        bb = c.sc[:, base + 6:base + 7]
        xsq = c.bfd[:, par * 4096:par * 4096 + 2048]
        xn_tm = c.bfd[:, par * 4096 + 2048:par * 4096 + 4096]
        t1 = b.do("scalar", lambda eo, sl=sl, xsq=xsq, q_col=q_col:
                  eo.activation(xsq, sl, AF.Square, accum_out=q_col),
                  after=(ld, prev.get("t3%d" % par)), inc=True)
        t3 = b.do("scalar", lambda eo, sl=sl, xsq=xsq, s_col=s_col:
                  eo.activation(xsq, sl, AF.Identity, accum_out=s_col),
                  after=(t1,), inc=True)
        prev["t3%d" % par] = t3
        b.do("vector", lambda eo, mu=mu, s_col=s_col: eo.tensor_scalar(
            mu, s_col, 1.0 / 2048.0, None, ALU.mult), after=(t3,))
        b.do("vector", lambda eo, ex2=ex2, q_col=q_col: eo.tensor_scalar(
            ex2, q_col, 1.0 / 2048.0, None, ALU.mult), after=(t1,))
        b.do("vector", lambda eo, var=var, mu=mu: eo.scalar_tensor_tensor(
            var, mu, -1.0, mu, ALU.mult, ALU.mult),
            after=[prev.get("t8%d" % par)])
        t7 = b.do("vector", lambda eo, var=var, ex2=ex2: eo.tensor_tensor(
            var, var, ex2, ALU.add), inc=True)
        t8 = b.do("scalar", lambda eo, r=r, var=var: eo.activation(
            r, var, AF.Sqrt, bias=1e-5), after=(t7, prev.get("t11%d" % par)),
            inc=True)
        prev["t8%d" % par] = t8
        t9 = b.do("vector", lambda eo, r=r: eo.reciprocal(r, r), after=(t8,),
                  inc=True)
        b.do("vector", lambda eo, bb=bb, mu=mu, r=r: eo.scalar_tensor_tensor(
            bb, mu, -1.0, r, ALU.mult, ALU.mult))
        t11 = b.do("vector", lambda eo, sl=sl, xn_tm=xn_tm, r=r, bb=bb:
                   eo.tensor_scalar(xn_tm, sl, r, bb, ALU.mult, ALU.add),
                   after=[prev.get("lastT%d" % par)], inc=True)
        prev["t11%d" % par] = t11
        tb = i // 4
        own = tb in (1, 3)
        if own:
            col = (i - 4) * 128 if tb == 1 else 512 + (i - 12) * 128
            dst = c.bfa
        else:
            col = i * 128 if tb == 0 else 512 + (i - 8) * 128
            dst = c.bfb
        lastT = None
        evs = []
        for kc in range(16):
            if par == 0:
                bslot = ("pC", "pD")[kc % 2]
                pt = getattr(c, bslot)[:].bitcast(BF16)[:, 0:128]
            else:
                bslot = "pAb" + str(kc % 2)
                pt = c.pA[:].bitcast(BF16)[:, (kc % 2) * 1024:
                                           (kc % 2) * 1024 + 128]
            tt = b.do("tensor", lambda eo, pt=pt, kc=kc, xn_tm=xn_tm: eo.transpose(
                pt, xn_tm[:, kc * 128:(kc + 1) * 128], io.identb),
                after=(t11, puse(c, bslot)), inc=True)
            lastT = tt
            ev = b.do("vector", lambda eo, pt=pt, dst=dst, kc=kc, col=col:
                      eo.tensor_copy(dst[:, kc * 1024 + col:kc * 1024 + col + 128], pt),
                      after=(tt,), inc=True)
            prel(c, bslot, ev)
            evs.append(ev)
            if own:
                if par == 0:
                    fslot = ("pE", "pF")[kc % 2]
                    pf = getattr(c, fslot)[:, 0:128]
                else:
                    fslot = "pBb" + str(kc % 2)
                    pf = c.pB[:, (kc % 2) * 512:(kc % 2) * 512 + 128]
                tf = b.do("tensor", lambda eo, pf=pf, kc=kc, sl=sl: eo.transpose(
                    pf, sl[:, kc * 128:(kc + 1) * 128], io.identf),
                    after=(t11, puse(c, fslot)), inc=True)
                lastT = tf
                ev2 = b.do("vector", lambda eo, pf=pf, kc=kc, col=col:
                           eo.tensor_copy(c.x[:, kc * 1024 + col:kc * 1024 + col + 128], pf),
                           after=(tf,), inc=True)
                prel(c, fslot, ev2)
                evs.append(ev2)
        prev["lastT%d" % par] = lastT
        slot_done[par] = evs[-1]
    b.barrier()


class IO:
    pass


def decl_common_inputs(nc, io):
    DT = nc.dram_tensor
    io.identb = _sb_const(nc, "identb", [128, 128], BF16)
    io.identf = _sb_const(nc, "identf", [128, 128], F32)


_SB_CONST_INPUTS = []


def _sb_const(nc, name, shape, dtype, sbuf_dtype=None):
    sd = sbuf_dtype or dtype
    d = nc.dram_tensor(name, shape, dtype, kind="ExternalInput")
    s = nc.alloc_sbuf_tensor("sb_" + name, shape, sd)
    _SB_CONST_INPUTS.append((d, s, sd != dtype))
    return s[:]


def emit_const_loads(c):
    b = c.b
    t = t2 = None
    for d, s, cast in _SB_CONST_INPUTS:
        if cast:
            t2 = b.do("gpsimd", lambda eo, d=d, s=s: eo.dma_start(s[:], d[:]),
                      dma=b.S("cl2"))
        else:
            t = b.do("sync", lambda eo, d=d, s=s: eo.dma_start(s[:], d[:]),
                     dma=b.S("cl"))
    if t is not None:
        b.do("sync", lambda eo: eo.nop(), after=(t,))
    if t2 is not None:
        b.do("gpsimd", lambda eo: eo.nop(), after=(t2,))
    _SB_CONST_INPUTS.clear()


def build_p0_test():
    nc = bass.Bass("TRN2", target_bir_lowering=False, debug=False)
    bld = Bld(nc)
    io = IO()
    io.h_in = nc.dram_tensor("h_in", [LOC, D], F32, kind="ExternalInput")
    decl_common_inputs(nc, io)
    o_x = nc.dram_tensor("o_x", [D, OWN], F32, kind="ExternalOutput")
    o_xn = nc.dram_tensor("o_xn", [D, OWN], BF16, kind="ExternalOutput")
    o_xh = nc.dram_tensor("o_xh", [D, OWN], BF16, kind="ExternalOutput")
    c = alloc_common(nc, bld)
    reg_const(c, 1e-5)
    emit_const_init(c)
    emit_const_loads(c)
    bld.barrier()
    emit_p0(c, io)
    so = bld.sem["sync"]
    b = bld
    for kc in range(16):
        b.do("sync", lambda eo, kc=kc: eo.dma_start(
            o_x[:].rearrange("(k p) t -> k p t", p=128)[kc],
            c.x[:, kc * 1024:(kc + 1) * 1024]), dma=b.S("o0"))
        b.do("sync", lambda eo, kc=kc: eo.dma_start(
            o_xn[:].rearrange("(k p) t -> k p t", p=128)[kc],
            c.bfa[:, kc * 1024:(kc + 1) * 1024]), dma=b.S("o0"))
        t = b.do("sync", lambda eo, kc=kc: eo.dma_start(
            o_xh[:].rearrange("(k p) t -> k p t", p=128)[kc],
            c.bfb[:, kc * 1024:(kc + 1) * 1024]), dma=b.S("o0"))
    b.do("sync", lambda eo, t=t: eo.nop(), after=(t,))
    bld.finalize()
    return nc


class DBuf:

    def __init__(self, c, name, nslots=2, queue="gpsimd"):
        self.c = c
        self.sems = [c.b.S("%s%d" % (name, i)) for i in range(nslots)]
        self.cons = [None] * nslots
        self.nslots = nslots
        self.q = queue
        self.i = 0

    def next(self):
        k = self.i % self.nslots
        self.i += 1
        return k

    def load(self, k, dst_ap, src_ap, after=()):
        return self.c.b.do(self.q, lambda eo: eo.dma_start(dst_ap, src_ap),
                           after=[self.cons[k]] + list(after), dma=self.sems[k])

    def done(self, k, tok):
        self.cons[k] = tok


def wslab_src(wdram, mc, ncols=128):
    r = wdram[:].rearrange("(kc p) m -> p kc m", p=128)
    return r[:, :, mc * ncols:(mc + 1) * ncols]


def wslab_dst(c, region_cols, kcn, ncols=128):
    a, bcol = region_cols
    return a[:, bcol:bcol + kcn * ncols].rearrange("p (kc m) -> p kc m", m=ncols)


def emit_l0_kvq(c, io):
    b = c.b
    reg_const(c, 1.0 / 128.0)
    reg_const(c, 1e-6)
    reg_const(c, 1.28e-4)
    wdb = DBuf(c, "wkv")
    KP = 4096
    VP = 6144
    lck = b.do("sync", lambda eo: eo.dma_start(c.bfd[:, 0:2048], io.ckd[:]),
               dma=b.S("rope"))
    lsk = b.do("sync", lambda eo: eo.dma_start(c.bfd[:, 2048:4096], io.skd[:]),
               dma=b.S("rope2"))
    lk = b.do("gpsimd", lambda eo: eo.dma_start(
        wslab_dst(c, (c.wb, 4096), 16), wslab_src(io.wk, 0)), dma=b.S("wk"))
    lv = b.do("gpsimd", lambda eo: eo.dma_start(
        wslab_dst(c, (c.wb, 6144), 16), wslab_src(io.wv, 0)), dma=b.S("wv"))

    def xn_g4(g4):
        src = c.bfb if g4 in (0, 2) else c.bfa
        cb = 0 if g4 < 2 else 512
        return lambda kc: src[:, kc * 1024 + cb:kc * 1024 + cb + 512]

    for g4 in range(4):
        xf = xn_g4(g4)
        ps = c.pA if g4 % 2 == 0 else c.pB
        pn = "pA" if g4 % 2 == 0 else "pB"
        wsl = c.wb[:, 6144:8192]
        mm = None
        for kc in range(16):
            st, sp = kc == 0, kc == 15
            aft = [lv, puse(c, pn)] if st else []
            mm = b.do("tensor", lambda eo, kc=kc, st=st, sp=sp, ps=ps, xf=xf, wsl=wsl:
                      eo.matmul(ps[:, 0:512], wsl[:, kc * 128:(kc + 1) * 128],
                                xf(kc), start=st, stop=sp),
                      after=aft, inc=sp)
        vg = c.bfc[:, 0:512]
        ev = b.do("vector", lambda eo, ps=ps, vg=vg: eo.tensor_copy(vg, ps[:, 0:512]),
                  after=(mm,), inc=True)
        prel(c, pn, ev)
        for t in range(4):
            pn2 = "pC" if t % 2 == 0 else "pD"
            pt = getattr(c, pn2)[:].bitcast(BF16)[:, 0:128]
            tt = b.do("tensor", lambda eo, pt=pt, vg=vg, t=t: eo.transpose(
                pt, vg[:, t * 128:(t + 1) * 128], io.identb),
                after=(ev, puse(c, pn2)), inc=True)
            ev2 = b.do("vector", lambda eo, pt=pt, g4=g4, t=t: eo.tensor_copy(
                c.bfd[:, VP + (g4 * 4 + t) * 128:VP + (g4 * 4 + t) * 128 + 128], pt),
                after=(tt,), inc=True)
            prel(c, pn2, ev2)
    prev_mk = None
    for g4 in range(4):
        xf = xn_g4(g4)
        ps = c.pA if g4 % 2 == 0 else c.pB
        pn = "pA" if g4 % 2 == 0 else "pB"
        wsl = c.wb[:, 4096:6144]
        mm = None
        for kc in range(16):
            st, sp = kc == 0, kc == 15
            aft = [lk, puse(c, pn)] if st else []
            mm = b.do("tensor", lambda eo, kc=kc, st=st, sp=sp, ps=ps, xf=xf, wsl=wsl:
                      eo.matmul(ps[:, 0:512], wsl[:, kc * 128:(kc + 1) * 128],
                                xf(kc), start=st, stop=sp),
                      after=aft, inc=sp)
        k0 = c.bfc[:, 512:1024]
        ev = b.do("vector", lambda eo, ps=ps, k0=k0: eo.tensor_copy(k0, ps[:, 0:512]),
                  after=(mm,), inc=True)
        prel(c, pn, ev)
        pn2 = "pE"
        rt = b.do("tensor", lambda eo, k0=k0: eo.matmul(
            c.pE[:, 0:512], io.rperm, k0, start=True, stop=True),
            after=(ev, puse(c, pn2)), inc=True)
        cksl = c.bfd[:, g4 * 512:g4 * 512 + 512]
        sksl = c.bfd[:, 2048 + g4 * 512:2048 + g4 * 512 + 512]
        k1 = c.bfc[:, 1024:1536]
        tmp = c.bfc[:, 1536:2048]
        b.do("vector", lambda eo, k0=k0, cksl=cksl, k1=k1: eo.tensor_mul(k1, k0, cksl),
             after=(lck,))
        t2 = b.do("vector", lambda eo, sksl=sksl, tmp=tmp: eo.tensor_mul(
            tmp, c.pE[:, 0:512], sksl), after=(rt, lsk), inc=True)
        prel(c, pn2, t2)
        t3 = b.do("vector", lambda eo, k1=k1, tmp=tmp: eo.tensor_add(k1, k1, tmp),
                  inc=True)
        ksq = c.bfc[:, 1536:2048]
        t4 = b.do("scalar", lambda eo, k1=k1, ksq=ksq: eo.activation(
            ksq, k1, AF.Square), after=(t3,), inc=True)
        pn3 = "pF"
        sq = b.do("tensor", lambda eo, ksq=ksq: eo.matmul(
            c.pF[0:1, 0:512], io.onescol_b, ksq, start=True, stop=True),
            after=(t4, puse(c, pn3)), inc=True)
        u = c.fs[0:1, 1024:1536]
        t5 = b.do("scalar", lambda eo, u=u: eo.activation(
            u, c.pF[0:1, 0:512], AF.Sqrt, bias=1e-6, scale=1.0 / 128.0),
            after=(sq,), inc=True)
        prel(c, pn3, t5)
        t6 = b.do("vector", lambda eo, u=u: eo.reciprocal(u, u), after=(t5,), inc=True)
        ub = c.fs[:].bitcast(BF16)[0:1, 1024:1536]
        t7 = b.do("vector", lambda eo, u=u, ub=ub: eo.tensor_copy(ub, u),
                  after=[prev_mk], inc=True)
        pn4 = "pC"
        mk = b.do("tensor", lambda eo, ub=ub: eo.matmul(
            c.pC[:, 0:512], io.knw_row, ub, start=True, stop=True),
            after=(t7, puse(c, pn4)), inc=True)
        prev_mk = mk
        kf = b.do("vector", lambda eo, k1=k1, g4=g4: eo.tensor_mul(
            c.bfd[:, KP + g4 * 512:KP + g4 * 512 + 512], k1, c.pC[:, 0:512]),
            after=(mk,), inc=True)
        prel(c, pn4, kf)
    b.barrier()
    lcq = b.do("sync", lambda eo: eo.dma_start(c.bfc[:, 0:1024], io.cqd[:]),
               dma=b.S("rope"))
    lsq = b.do("sync", lambda eo: eo.dma_start(c.bfc[:, 1024:2048], io.sqd[:]),
               dma=b.S("rope2"))
    io.cq = c.bfc[:, 0:1024]
    io.sq = c.bfc[:, 1024:2048]
    qc_evs = []
    prev_mq = None
    reg_const(c, 1.28e-4)
    for mc in range(4):
        sl = wdb.next()
        dst = wslab_dst(c, (c.wb, sl * 2048), 16)
        ld = wdb.load(sl, dst, wslab_src(io.wqc, mc))
        ps = c.pA if mc % 2 == 0 else c.pB
        pn = "pA" if mc % 2 == 0 else "pB"
        toks = []
        for kc in range(16):
            for nt in range(2):
                st, sp = kc == 0, kc == 15
                aft = [ld, puse(c, pn)] if st and nt == 0 else []
                mm = b.do("tensor", lambda eo, kc=kc, st=st, sp=sp, ps=ps, nt=nt, sl=sl:
                          eo.matmul(ps[:, nt * 512:nt * 512 + 512],
                                    c.wb[:, sl * 2048 + kc * 128:sl * 2048 + (kc + 1) * 128],
                                    c.bfa[:, kc * 1024 + nt * 512:kc * 1024 + nt * 512 + 512],
                                    start=st, stop=sp),
                          after=aft, inc=sp)
                if sp:
                    toks.append(mm)
        ev = b.do("vector", lambda eo, ps=ps, mc=mc: eo.tensor_copy(
            c.bfc[:, 2048 + mc * 1024:2048 + (mc + 1) * 1024], ps[:]),
            after=toks, inc=True)
        prel(c, pn, ev)
        wdb.done(sl, ev)
        qc_evs.append(ev)
    def q_post(hd):
        h_, q0_, ev_ = hd
        qh = c.bfb[:, h_ * 1024:(h_ + 1) * 1024]
        par = h_ % 2
        tmp = c.bfd[:, par * 2048:par * 2048 + 1024]
        qsq = c.bfd[:, par * 2048 + 1024:par * 2048 + 2048]
        for nt in range(2):
            pn2 = "pE" if nt == 0 else "pF"
            p2 = getattr(c, pn2)
            rt = b.do("tensor", lambda eo, q0_=q0_, nt=nt, p2=p2: eo.matmul(
                p2[:, 0:512], io.rperm, q0_[:, nt * 512:nt * 512 + 512],
                start=True, stop=True), after=(ev_, puse(c, pn2)), inc=True)
            b.do("vector", lambda eo, q0_=q0_, nt=nt, qh=qh: eo.tensor_mul(
                qh[:, nt * 512:nt * 512 + 512], q0_[:, nt * 512:nt * 512 + 512],
                io.cq[:, nt * 512:nt * 512 + 512]), after=(lcq,))
            t2 = b.do("vector", lambda eo, nt=nt, tmp=tmp, p2=p2: eo.tensor_mul(
                tmp[:, nt * 512:nt * 512 + 512], p2[:, 0:512],
                io.sq[:, nt * 512:nt * 512 + 512]), after=(rt, lsq), inc=True)
            prel(c, pn2, t2)
        t3 = b.do("vector", lambda eo, qh=qh, tmp=tmp: eo.tensor_add(qh, qh, tmp),
                  inc=True)
        t4 = b.do("scalar", lambda eo, qh=qh, qsq=qsq: eo.activation(
            qsq, qh, AF.Square), after=(t3,), inc=True)
        return (h_, qh, t4, par)

    def q_post2(hd2):
        h_, qh, t4, par = hd2
        qsq = c.bfd[:, par * 2048 + 1024:par * 2048 + 2048]
        u = c.fs[0:1, par * 1024:par * 1024 + 1024]
        ub = c.fs[:].bitcast(BF16)[0:1, 2048 + par * 1024:3072 + par * 1024]
        t5s = []
        for nt in range(2):
            pn3 = "pC" if nt == 0 else "pD"
            p3 = getattr(c, pn3)
            sq = b.do("tensor", lambda eo, qsq=qsq, nt=nt, p3=p3: eo.matmul(
                p3[0:1, 0:512], io.onescol_b, qsq[:, nt * 512:nt * 512 + 512],
                start=True, stop=True), after=(t4, puse(c, pn3)), inc=True)
            t5 = b.do("scalar", lambda eo, nt=nt, p3=p3, u=u: eo.activation(
                u[0:1, nt * 512:nt * 512 + 512], p3[0:1, 0:512],
                AF.Sqrt, bias=1.28e-4), after=(sq,), inc=True)
            prel(c, pn3, t5)
            t5s.append(t5)
        t6 = b.do("vector", lambda eo, u=u: eo.reciprocal(u, u), after=t5s,
                  inc=True)
        t7 = b.do("vector", lambda eo, u=u, ub=ub: eo.tensor_copy(ub, u),
                  after=[prev_mq[par]], inc=True)
        for nt in range(2):
            pn4 = "pE" if nt == 0 else "pF"
            p4 = getattr(c, pn4)
            mq = b.do("tensor", lambda eo, nt=nt, p4=p4, ub=ub: eo.matmul(
                p4[:, 0:512], io.qnw_row, ub[0:1, nt * 512:nt * 512 + 512],
                start=True, stop=True), after=(t7, puse(c, pn4)), inc=True)
            qf = b.do("vector", lambda eo, qh=qh, nt=nt, p4=p4: eo.tensor_mul(
                qh[:, nt * 512:nt * 512 + 512], qh[:, nt * 512:nt * 512 + 512],
                p4[:, 0:512]), after=(mq,), inc=True)
            prel(c, pn4, qf)
            prev_mq[par] = mq

    prev_mq = [None, None]
    prev_q0rd = [None, None]
    hd = hd2 = None
    for h in range(16):
        sl = wdb.next()
        dst = wslab_dst(c, (c.wb, sl * 2048), 4)
        ld = wdb.load(sl, dst, wslab_src(io.wqu, h))
        ps = c.pA if h % 2 == 0 else c.pB
        pn = "pA" if h % 2 == 0 else "pB"
        toks = []
        for kc in range(4):
            for nt in range(2):
                st, sp = kc == 0, kc == 3
                aft = ([ld, puse(c, pn)] if st and nt == 0 else []) + [qc_evs[kc]]
                mm = b.do("tensor", lambda eo, kc=kc, st=st, sp=sp, ps=ps, nt=nt,
                          sl=sl: eo.matmul(
                              ps[:, nt * 512:nt * 512 + 512],
                              c.wb[:, sl * 2048 + kc * 128:sl * 2048 + (kc + 1) * 128],
                              c.bfc[:, 2048 + kc * 1024 + nt * 512:2048 + kc * 1024 + nt * 512 + 512],
                              start=st, stop=sp),
                          after=aft, inc=sp)
                if sp:
                    toks.append(mm)
        q0 = c.bfc[:, 6144 + (h % 2) * 1024:6144 + (h % 2) * 1024 + 1024]
        ev = b.do("vector", lambda eo, ps=ps, q0=q0: eo.tensor_copy(q0, ps[:]),
                  after=toks + [prev_q0rd[h % 2]], inc=True)
        prel(c, pn, ev)
        wdb.done(sl, ev)
        if hd2 is not None:
            q_post2(hd2)
            hd2 = None
        if hd is not None:
            hd2 = q_post(hd)
            prev_q0rd[hd[0] % 2] = hd2[2]
        hd = (h, q0, ev)
    if hd2 is not None:
        q_post2(hd2)
    hd2 = q_post(hd)
    q_post2(hd2)
    b.barrier()


def emit_l0_scores(c, io):
    b = c.b
    KP, VP = 4096, 6144
    lm = b.do("sync", lambda eo: eo.dma_start(c.wb[0:1, 0:5120], io.erow[:]),
              dma=b.S("mask"))
    sb = b.do("tensor", lambda eo: eo.matmul(
        c.pE[:, 0:16], io.onesrow_f, io.sink_row, start=True, stop=True),
        after=(puse(c, "pE"),), inc=True)
    e1 = b.do("vector", lambda eo: eo.tensor_copy(c.sc[:, 16:32], c.pE[:, 0:16]),
              after=(sb,), inc=True)
    prel(c, "pE", e1)
    b.do("vector", lambda eo: eo.tensor_scalar(
        c.sc[:, 32:48], c.sc[:, 16:32], -1.0, None, ALU.mult))
    tiles = [(cq, h, j) for cq in range(2) for h in range(16) for j in range(4)]
    state = {"cp": [None, None], "pv": [None, None], "T": [None, None],
             "norm": [None, None]}
    flushT = _mk_flushT(c, io, state)
    SC = ("pA", "pB")
    pend = [None, None]
    v2p = None

    def do_v2(v2):
        ti_, cq_, h_, j_, qt_, probs_, probn_, rsum_, ske_, rden_, ex_, sx_, \
            par_ = v2
        b.do("vector", lambda eo: eo.tensor_add(rden_, rsum_, ske_),
             after=(ex_, sx_))
        b.do("vector", lambda eo: eo.reciprocal(rden_, rden_))
        nt = b.do("vector", lambda eo: eo.tensor_scalar(
            probn_, probs_, rden_, None, ALU.mult),
            after=(state["T"][par_],), inc=True)
        state["norm"][par_] = nt
        pend[par_] = (ti_, cq_, h_, j_, qt_, nt, probn_)

    for ti, (cq, h, j) in enumerate(tiles):
        if ti % 4 == 0:
            b.drip()
        par = ti % 2
        pn = SC[par]
        ps = getattr(c, pn)
        probs = c.bfc[:, par * 2048:par * 2048 + 640]
        probn = c.bfc[:, par * 2048 + 640:par * 2048 + 1280]
        negm = c.sc[:, 48 + par:49 + par]
        rsum = c.sc[:, 52 + par:53 + par]
        ske = c.sc[:, 56 + par:57 + par]
        rden = c.sc[:, 60 + par:61 + par]
        qt = c.bfb[:, h * 1024 + cq * 512 + j * 128:
                   h * 1024 + cq * 512 + j * 128 + 128]
        kb = KP + cq * 1024 + j * 128
        b.do("tensor", lambda eo, ps=ps, qt=qt, kb=kb: eo.matmul(
            ps[:, 0:512], qt, c.bfd[:, kb:kb + 512], start=True, stop=False),
            after=(puse(c, pn),), inc=False)
        b.do("tensor", lambda eo, ps=ps, qt=qt, kb=kb: eo.matmul(
            ps[:, 512:640], qt, c.bfd[:, kb + 512:kb + 640],
            start=True, stop=True), inc=False)
        ebase = (cq * 4 + j) * 640
        m2 = b.do("tensor", lambda eo, ps=ps, ebase=ebase: eo.matmul(
            ps[:, 0:512], io.onesrow_b, c.wb[0:1, ebase:ebase + 512],
            start=False, stop=True), after=(lm,), inc=True)
        if pend[par] is not None:
            flushT(pend[par])
            pend[par] = None
        b.do("vector", lambda eo, ps=ps: eo.tensor_add(
            ps[:, 0:128], ps[:, 0:128], io.winm[:, 0:128]), after=(m2,))
        b.do("vector", lambda eo, ps=ps: eo.tensor_add(
            ps[:, 512:640], ps[:, 512:640], io.winm[:, 128:256]))
        b.do("vector", lambda eo, ps=ps, negm=negm: eo.reduce_max(
            negm, ps[:, 0:640], axis=AX.X, negate=True))
        tm = b.do("vector", lambda eo, h=h, negm=negm: eo.tensor_single_scalar(
            negm, negm, c.sc[:, 32 + h:33 + h], ALU.min), inc=True)
        ex = b.do("scalar", lambda eo, ps=ps, probs=probs, negm=negm, rsum=rsum:
                  eo.activation(probs, ps[:, 0:640], AF.Exp, bias=negm,
                                accum_out=rsum),
                  after=(tm, state["norm"][par]), inc=True)
        prel(c, pn, ex)
        sx = b.do("scalar", lambda eo, h=h, ske=ske, negm=negm: eo.activation(
            ske, negm, AF.Exp, bias=c.sc[:, 16 + h:17 + h]), inc=True)
        if v2p is not None:
            do_v2(v2p)
        v2p = (ti, cq, h, j, qt, probs, probn, rsum, ske, rden, ex, sx, par)
    do_v2(v2p)
    for par in range(2):
        if pend[par] is not None:
            flushT(pend[par])
    b.barrier()


def _mk_flushT(c, io, state):
    b = c.b
    VP = 6144

    def flushT(pend):
        ti, cq, h, j, qt, nt, probn = pend
        tsl = ti % 2
        pcn = ("pC", "pD")[tsl]
        pvn = ("pE", "pF")[tsl]
        pvp = getattr(c, pvn)
        ptsb = c.bfc[:, 4096 + tsl * 640:4096 + tsl * 640 + 640]
        tts = None
        for st in range(5):
            pt = getattr(c, pcn)[:].bitcast(BF16)
            tts = b.do("tensor", lambda eo, st=st, pt=pt, probn=probn:
                       eo.transpose(pt[:, st * 128:(st + 1) * 128],
                                    probn[:, st * 128:(st + 1) * 128],
                                    io.identb),
                       after=(nt, state["cp"][tsl]) if st == 0 else (),
                       inc=(st == 4))
        state["T"][tsl] = tts
        cp = b.do("scalar", lambda eo, pcn=pcn, ptsb=ptsb: eo.copy(
            ptsb, getattr(c, pcn)[:].bitcast(BF16)[:, 0:640]),
            after=(tts, state["pv"][tsl]), inc=True)
        state["cp"][tsl] = cp
        prel(c, pcn, cp)
        pv = None
        for st in range(5):
            tile = cq * 8 + j + st
            stt_, spp = st == 0, st == 4
            aft = [cp, puse(c, pvn)] if stt_ else []
            pv = b.do("tensor", lambda eo, st=st, tile=tile, stt_=stt_, spp=spp,
                      pvp=pvp, ptsb=ptsb: eo.matmul(
                          pvp[:, 0:128],
                          c.bfd[:, VP + tile * 128:VP + tile * 128 + 128],
                          ptsb[:, st * 128:(st + 1) * 128],
                          start=stt_, stop=spp), after=aft, inc=spp)
        av = b.do("vector", lambda eo, qt=qt, pvp=pvp: eo.tensor_copy(
            qt, pvp[:, 0:128]), after=(pv,), inc=True)
        state["pv"][tsl] = av
        prel(c, pvn, av)
    return flushT


def emit_l0_out(c, io):
    b = c.b
    wdb = DBuf(c, "wo")
    w1r = [io.w1[:][g].rearrange("(kc p) m -> p kc m", p=128) for g in range(4)]
    w2r = io.w2[:].rearrange("(r p) m -> p r m", p=128)
    prev_w2 = None
    for g in range(4):
        mid_evs = [None] * 16
        for so in range(4):
            sl = wdb.next()
            dst = c.wb[:, sl * 2048:sl * 2048 + 2048].rearrange(
                "p (kc m) -> p kc m", m=512)
            ld = wdb.load(sl, dst, w1r[g][:, :, so * 512:(so + 1) * 512])
            lastmm = None
            for oc4 in range(4):
                oc = so * 4 + oc4
                pn = "pA" if oc % 2 == 0 else "pB"
                ps = getattr(c, pn)
                toks = []
                for kc in range(4):
                    for nt in range(2):
                        st, sp = kc == 0, kc == 3
                        aft = [ld, puse(c, pn)] if st and nt == 0 else []
                        mm = b.do("tensor", lambda eo, sl=sl, kc=kc, oc4=oc4, g=g,
                                  nt=nt, st=st, sp=sp, ps=ps: eo.matmul(
                                      ps[:, nt * 512:nt * 512 + 512],
                                      c.wb[:, sl * 2048 + kc * 512 + oc4 * 128:
                                           sl * 2048 + kc * 512 + oc4 * 128 + 128],
                                      c.bfb[:, (4 * g + kc) * 1024 + nt * 512:
                                            (4 * g + kc) * 1024 + nt * 512 + 512],
                                      start=st, stop=sp), after=aft, inc=sp)
                        if sp:
                            toks.append(mm)
                lastmm = toks[-1]
                ev = b.do("vector", lambda eo, ps=ps, oc=oc: eo.tensor_copy(
                    c.bfc[:, oc * 1024:(oc + 1) * 1024], ps[:]),
                    after=toks + [prev_w2], inc=True)
                prel(c, pn, ev)
                mid_evs[oc] = ev
            wdb.done(sl, lastmm)
        for mc in range(16):
            sl = wdb.next()
            dst = c.wb[:, sl * 2048:sl * 2048 + 2048].rearrange(
                "p (kc m) -> p kc m", m=128)
            ld = wdb.load(sl, dst, w2r[:, g * 16:(g + 1) * 16,
                                       mc * 128:(mc + 1) * 128])
            pn = "pA" if mc % 2 == 0 else "pB"
            ps = getattr(c, pn)
            toks = []
            for kc in range(16):
                for nt in range(2):
                    st, sp = kc == 0, kc == 15
                    aft = ([ld, puse(c, pn)] if st and nt == 0 else []) + \
                        [mid_evs[kc]]
                    mm = b.do("tensor", lambda eo, sl=sl, kc=kc, nt=nt, st=st,
                              sp=sp, ps=ps: eo.matmul(
                                  ps[:, nt * 512:nt * 512 + 512],
                                  c.wb[:, sl * 2048 + kc * 128:sl * 2048 + (kc + 1) * 128],
                                  c.bfc[:, kc * 1024 + nt * 512:kc * 1024 + nt * 512 + 512],
                                  start=st, stop=sp), after=aft, inc=sp)
                    if sp:
                        toks.append(mm)
            prev_w2 = toks[-1]
            ad = b.do("vector", lambda eo, ps=ps, mc=mc: eo.tensor_add(
                c.x[:, mc * 1024:(mc + 1) * 1024],
                c.x[:, mc * 1024:(mc + 1) * 1024], ps[:]), after=toks, inc=True)
            prel(c, pn, ad)
            wdb.done(sl, ad)
    b.barrier()


def decl_l0_inputs(nc, io):
    DT = nc.dram_tensor
    io.wqc = DT("wqc", [D, 512], F32, kind="ExternalInput")
    io.wqu = DT("wqu", [512, D], F32, kind="ExternalInput")
    io.wk = DT("wk", [D, HD], F32, kind="ExternalInput")
    io.wv = DT("wv", [D, HD], F32, kind="ExternalInput")
    io.w1 = DT("w1", [4, 512, D], F32, kind="ExternalInput")
    io.w2 = DT("w2", [4 * D, D], F32, kind="ExternalInput")
    io.ckd = DT("ckd", [128, LOC], BF16, kind="ExternalInput")
    io.skd = DT("skd", [128, LOC], BF16, kind="ExternalInput")
    io.cqd = DT("cqd", [128, OWN], BF16, kind="ExternalInput")
    io.sqd = DT("sqd", [128, OWN], BF16, kind="ExternalInput")
    io.erow = DT("erow", [1, 5120], BF16, kind="ExternalInput")
    io.winm = _sb_const(nc, "winm", [128, 256], BF16)
    io.rperm = _sb_const(nc, "rperm", [128, 128], BF16)
    io.onescol_b = _sb_const(nc, "onescol_b", [128, 1], BF16)
    io.onesrow_b = _sb_const(nc, "onesrow_b", [1, 128], BF16)
    io.onesrow_f = _sb_const(nc, "onesrow_f", [1, 128], F32)
    io.qnw_row = _sb_const(nc, "qnw_row", [1, 128], F32, BF16)
    io.knw_row = _sb_const(nc, "knw_row", [1, 128], F32, BF16)
    io.sink_row = _sb_const(nc, "sink_row", [1, 16], F32)


def build_l0_test():
    nc = bass.Bass("TRN2", target_bir_lowering=False, debug=False)
    bld = Bld(nc)
    io = IO()
    io.h_in = nc.dram_tensor("h_in", [LOC, D], F32, kind="ExternalInput")
    decl_common_inputs(nc, io)
    decl_l0_inputs(nc, io)
    o_x = nc.dram_tensor("o_x", [D, OWN], F32, kind="ExternalOutput")
    c = alloc_common(nc, bld)
    for v in (1e-5, 1.0 / 128.0, 1e-6, 1.28e-4):
        reg_const(c, v)
    emit_const_init(c)
    emit_const_loads(c)
    bld.barrier()
    emit_p0(c, io)
    emit_l0_kvq(c, io)
    emit_l0_scores(c, io)
    emit_l0_out(c, io)
    b = bld
    for kc in range(16):
        t = b.do("sync", lambda eo, kc=kc: eo.dma_start(
            o_x[:].rearrange("(k p) t -> k p t", p=128)[kc],
            c.x[:, kc * 1024:(kc + 1) * 1024]), dma=b.S("o0"))
    b.do("sync", lambda eo, t=t: eo.nop(), after=(t,))
    bld.finalize()
    return nc


def emit_moe(c, io):
    b = c.b
    lrt = b.do("gpsimd", lambda eo: eo.dma_start(
        c.bfd[:, 0:128].rearrange("p (kc m) -> p kc m", m=8),
        io.routerT[:].rearrange("(kc p) m -> p kc m", p=128)), dma=b.S("wrt"))
    comb_toks = []
    e_ = c.sc[:, 64:72]
    negmx = c.sc[:, 72:73]
    m1 = c.sc[:, 73:74]
    t2c = c.sc[:, 74:82]
    m2 = c.sc[:, 82:83]
    wsum = c.sc[:, 83:84]
    for tt in range(8):
        mm = None
        for kc in range(16):
            st, sp = kc == 0, kc == 15
            aft = [lrt, puse(c, "pC")] if st else []
            mm = b.do("tensor", lambda eo, kc=kc, st=st, sp=sp, tt=tt: eo.matmul(
                c.pC[:, 0:8],
                c.bfa[:, kc * 1024 + tt * 128:kc * 1024 + tt * 128 + 128],
                c.bfd[:, kc * 8:(kc + 1) * 8], start=st, stop=sp),
                after=aft, inc=sp)
        rm = b.do("vector", lambda eo: eo.reduce_max(negmx, c.pC[:, 0:8], axis=AX.X,
             negate=True), after=(mm,), inc=True)
        ex = b.do("scalar", lambda eo: eo.activation(e_, c.pC[:, 0:8], AF.Exp,
                  bias=negmx), after=(rm,), inc=True)
        prel(c, "pC", ex)
        b.do("vector", lambda eo: eo.reduce_max(m1, e_, axis=AX.X), after=(ex,))
        b.do("vector", lambda eo: eo.tensor_single_scalar(t2c, e_, m1, ALU.is_ge))
        b.do("vector", lambda eo: eo.scalar_tensor_tensor(
            t2c, t2c, -2.0, e_, ALU.mult, ALU.add))
        b.do("vector", lambda eo: eo.reduce_max(m2, t2c, axis=AX.X))
        b.do("vector", lambda eo: eo.tensor_single_scalar(t2c, e_, m2, ALU.is_ge))
        b.do("vector", lambda eo: eo.tensor_mul(t2c, t2c, e_))
        b.do("vector", lambda eo: eo.reduce_sum(wsum, t2c, axis=AX.X))
        b.do("vector", lambda eo: eo.reciprocal(wsum, wsum))
        comb_toks.append(b.do("vector", lambda eo, tt=tt: eo.tensor_scalar(
            c.fs[:, tt * 8:tt * 8 + 8], t2c, wsum, None, ALU.mult), inc=True))
    lsg = b.do("gpsimd", lambda eo: eo.dma_start(
        c.bfd[:, 128:144].rearrange("p (kc one) -> p kc one", one=1),
        io.sgT[:].rearrange("(kc p) one -> p kc one", p=128),
        allow_slow_non_contiguous=True), dma=b.S("wsg"))
    sgrow = c.fs[0:1, 1024:2048]
    sg_toks = []
    for nt in range(2):
        pn = "pE" if nt == 0 else "pF"
        ps = getattr(c, pn)
        mm = None
        for kc in range(16):
            st, sp = kc == 0, kc == 15
            aft = [lsg, puse(c, pn)] if st else []
            mm = b.do("tensor", lambda eo, kc=kc, st=st, sp=sp, nt=nt, ps=ps:
                      eo.matmul(ps[0:1, 0:512], c.bfd[:, 128 + kc:129 + kc],
                                c.bfa[:, kc * 1024 + nt * 512:kc * 1024 + nt * 512 + 512],
                                start=st, stop=sp), after=aft, inc=sp)
        sg = b.do("scalar", lambda eo, nt=nt, ps=ps: eo.activation(
            sgrow[0:1, nt * 512:nt * 512 + 512], ps[0:1, 0:512], AF.Sigmoid),
            after=(mm,), inc=True)
        prel(c, pn, sg)
        sg_toks.append(sg)
    sgb = c.fs[:].bitcast(BF16)[0:1, 1024:2048]
    scb = b.do("vector", lambda eo: eo.tensor_copy(sgb, sgrow), after=sg_toks,
               inc=True)
    for nt in range(2):
        pn = "pE" if nt == 0 else "pF"
        ps = getattr(c, pn)
        mm = b.do("tensor", lambda eo, nt=nt, ps=ps: eo.matmul(
            ps[:, 0:512], io.onesrow_b, sgb[0:1, nt * 512:nt * 512 + 512],
            start=True, stop=True), after=(scb, puse(c, pn)), inc=True)
        row_war = mm
        ev = b.do("vector", lambda eo, nt=nt, ps=ps: eo.tensor_copy(
            c.bfd[:, 3072 + nt * 512:3072 + nt * 512 + 512], ps[:, 0:512]),
            after=(mm,), inc=True)
        prel(c, pn, ev)

    wg = DBuf(c, "wg", nslots=4)
    wd = DBuf(c, "wd")

    def gu_mm(slot_ld, sl, oc_in_slab, psn, xcols=1024):
        ps = getattr(c, psn)
        toks = []
        for kc in range(16):
            for nt in range(xcols // 512):
                st, sp = kc == 0, kc == 15
                aft = [slot_ld, puse(c, psn)] if st and nt == 0 else []
                mm = b.do("tensor", lambda eo, kc=kc, st=st, sp=sp, nt=nt, sl=sl,
                          oc_in_slab=oc_in_slab, ps=ps: eo.matmul(
                              ps[:, nt * 512:nt * 512 + 512],
                              c.wb[:, sl * 2048 + kc * 128:sl * 2048 + (kc + 1) * 128],
                              c.bfa[:, kc * 1024 + nt * 512:kc * 1024 + nt * 512 + 512],
                              start=st, stop=sp), after=aft, inc=sp)
                if sp:
                    toks.append(mm)
        return toks

    prev_av = None
    shg_r = io.shgT[:].rearrange("(kc p) m -> p kc m", p=128)
    shu_r = io.shuT[:].rearrange("(kc p) m -> p kc m", p=128)
    shd_r = io.shdT[:].rearrange("(r p) m -> p r m", p=128)
    last_down = None
    def gu_load(wsrc_r, gc):
        sl_ = wg.next()
        ld_ = wg.load(sl_, wslab_dst(c, (c.wb, sl_ * 2048), 16),
                      wsrc_r[:, :, gc * 128:(gc + 1) * 128])
        return sl_, ld_

    for iq in range(4):
        noc = 11
        act_toks = [None] * 11
        pre = {0: (gu_load(shg_r, iq * 11), gu_load(shu_r, iq * 11))}
        for oc in range(noc):
            gc = iq * 11 + oc
            (slg, ldg), (slu, ldu) = pre.pop(oc)
            if oc + 1 < noc:
                pre[oc + 1] = (gu_load(shg_r, gc + 1), gu_load(shu_r, gc + 1))
            gt = gu_mm(ldg, slg, 0, "pA")
            ut = gu_mm(ldu, slu, 0, "pB")
            sl_ = c.bfd[:, 1024:2048]
            sv = b.do("scalar", lambda eo, sl_=sl_: eo.activation(
                sl_, c.pA[:], AF.Silu), after=gt + [prev_av], inc=True)
            prel(c, "pA", sv)
            wg.done(slg, sv)
            av = b.do("vector", lambda eo, sl_=sl_, oc=oc: eo.tensor_mul(
                c.bfc[:, oc * 1024:(oc + 1) * 1024], sl_, c.pB[:]),
                after=ut + [sv, last_down], inc=True)
            prel(c, "pB", av)
            wg.done(slu, av)
            act_toks[oc] = av
            prev_av = av
        for hc in range(16):
            sl2 = wd.next()
            ld2 = wd.load(sl2, c.bfd[:, 4096 + sl2 * 1408:4096 + sl2 * 1408 + 1408]
                          .rearrange("p (kc m) -> p kc m", m=128),
                          shd_r[:, iq * 11:(iq + 1) * 11, hc * 128:(hc + 1) * 128])
            dtoks = {}
            for kc in range(11):
                for nt in range(2):
                    pn = "pC" if nt == 0 else "pD"
                    ps = getattr(c, pn)
                    st, sp = kc == 0, kc == 10
                    aft = ([ld2, puse(c, pn)] if st else []) + [act_toks[kc]]
                    mm = b.do("tensor", lambda eo, kc=kc, st=st, sp=sp, nt=nt,
                              sl2=sl2, ps=ps: eo.matmul(
                                  ps[:, 0:512],
                                  c.bfd[:, 4096 + sl2 * 1408 + kc * 128:
                                       4096 + sl2 * 1408 + (kc + 1) * 128],
                                  c.bfc[:, kc * 1024 + nt * 512:kc * 1024 + nt * 512 + 512],
                                  start=st, stop=sp), after=aft, inc=sp)
                    if sp:
                        dtoks[nt] = mm
            for nt in range(2):
                pn = "pC" if nt == 0 else "pD"
                ps = getattr(c, pn)
                mm = dtoks[nt]
                tv = b.do("vector", lambda eo, nt=nt, hc=hc, ps=ps: eo.tensor_mul(
                    c.bfd[:].bitcast(F32)[:, 3456:3968],
                    ps[:, 0:512], c.bfd[:, 3072 + nt * 512:3072 + nt * 512 + 512]),
                    after=(mm,), inc=True)
                prel(c, pn, tv)
                if nt == 1:
                    wd.done(sl2, tv)
                b.do("vector", lambda eo, nt=nt, hc=hc: eo.tensor_add(
                    c.x[:, hc * 1024 + nt * 512:hc * 1024 + nt * 512 + 512],
                    c.x[:, hc * 1024 + nt * 512:hc * 1024 + nt * 512 + 512],
                    c.bfd[:].bitcast(F32)[:, 3456:3968]))
                if nt == 1 and hc == 15:
                    last_down = mm
    gup_r = [io.gupT[:][e].rearrange("(kc p) m -> p kc m", p=128) for e in range(NE)]
    dp_r = [io.dpT[:][e].rearrange("(r p) m -> p r m", p=128) for e in range(NE)]
    last_pool = None
    row_war = locals().get("row_war")
    for e in range(NE):
        comb_evs = []
        pool_toks = [None] * 11
        rowp = c.fs[0:1, 1024:2048]
        rowb = c.fs[:].bitcast(BF16)[0:1, 1024:2048]
        tts = []
        for tt in range(8):
            tp = b.do("tensor", lambda eo, tt=tt, e=e: eo.transpose(
                c.pE[0:1, 0:128], c.fs[:, tt * 8 + e:tt * 8 + e + 1],
                io.identf), after=[puse(c, "pE"), comb_toks[tt]], inc=True)
            ev = b.do("vector", lambda eo, tt=tt: eo.tensor_copy(
                rowp[0:1, tt * 128:(tt + 1) * 128], c.pE[0:1, 0:128]),
                after=(tp,), inc=True)
            prel(c, "pE", ev)
        cb = b.do("vector", lambda eo: eo.tensor_copy(rowb, rowp),
                  after=[row_war], inc=True)
        for nt in range(2):
            pn = "pE" if nt == 0 else "pF"
            ps = getattr(c, pn)
            mm = b.do("tensor", lambda eo, nt=nt, ps=ps: eo.matmul(
                ps[:, 0:512], io.onesrow_b, rowb[0:1, nt * 512:nt * 512 + 512],
                start=True, stop=True), after=(cb, puse(c, pn)), inc=True)
            row_war = mm
            ev = b.do("vector", lambda eo, nt=nt, ps=ps: eo.tensor_copy(
                c.bfd[:, 2048 + nt * 512:2048 + nt * 512 + 512], ps[:, 0:512]),
                after=(mm, last_pool), inc=True)
            prel(c, pn, ev)
            comb_evs.append(ev)
        pre = {0: (gu_load(gup_r[e], 0), gu_load(gup_r[e], 11))}
        for oc in range(11):
            (slg, ldg), (slu, ldu) = pre.pop(oc)
            if oc + 1 < 11:
                pre[oc + 1] = (gu_load(gup_r[e], oc + 1),
                               gu_load(gup_r[e], 12 + oc))
            gt = gu_mm(ldg, slg, 0, "pA")
            ut = gu_mm(ldu, slu, 0, "pB")
            sl_ = c.bfd[:, 1024:2048]
            sv = b.do("scalar", lambda eo, sl_=sl_: eo.activation(
                sl_, c.pA[:], AF.Silu), after=gt + [prev_av], inc=True)
            prel(c, "pA", sv)
            wg.done(slg, sv)
            av = b.do("vector", lambda eo, sl_=sl_, oc=oc: eo.tensor_mul(
                c.bfc[:, oc * 1024:(oc + 1) * 1024], sl_, c.pB[:]),
                after=ut + [sv, last_down], inc=True)
            prel(c, "pB", av)
            wg.done(slu, av)
            prev_av = av
            pool_toks[oc] = b.do("vector", lambda eo, oc=oc: eo.tensor_mul(
                c.bfc[:, oc * 1024:(oc + 1) * 1024],
                c.bfc[:, oc * 1024:(oc + 1) * 1024],
                c.bfd[:, 2048:3072]), after=[av] + comb_evs, inc=True)
            last_pool = pool_toks[oc]
        for hc in range(16):
            sl2 = wd.next()
            ld2 = wd.load(sl2, c.bfd[:, 4096 + sl2 * 1408:4096 + sl2 * 1408 + 1408]
                          .rearrange("p (kc m) -> p kc m", m=128),
                          dp_r[e][:, :, hc * 128:(hc + 1) * 128])
            dtoks = {}
            for kc in range(11):
                for nt in range(2):
                    pn = "pC" if nt == 0 else "pD"
                    ps = getattr(c, pn)
                    st, sp = kc == 0, kc == 10
                    aft = ([ld2, puse(c, pn)] if st else []) + [pool_toks[kc]]
                    mm = b.do("tensor", lambda eo, kc=kc, st=st, sp=sp, nt=nt,
                              sl2=sl2, ps=ps: eo.matmul(
                                  ps[:, 0:512],
                                  c.bfd[:, 4096 + sl2 * 1408 + kc * 128:
                                       4096 + sl2 * 1408 + (kc + 1) * 128],
                                  c.bfc[:, kc * 1024 + nt * 512:kc * 1024 + nt * 512 + 512],
                                  start=st, stop=sp), after=aft, inc=sp)
                    if sp:
                        dtoks[nt] = mm
            for nt in range(2):
                pn = "pC" if nt == 0 else "pD"
                ps = getattr(c, pn)
                mm = dtoks[nt]
                ad = b.do("vector", lambda eo, nt=nt, hc=hc, ps=ps: eo.tensor_add(
                    c.x[:, hc * 1024 + nt * 512:hc * 1024 + nt * 512 + 512],
                    c.x[:, hc * 1024 + nt * 512:hc * 1024 + nt * 512 + 512],
                    ps[:, 0:512]), after=(mm,), inc=True)
                prel(c, pn, ad)
                if nt == 1:
                    wd.done(sl2, ad)
                    if hc == 15:
                        last_down = mm
    b.barrier()


def decl_moe_inputs(nc, io):
    DT = nc.dram_tensor
    io.routerT = DT("routerT", [D, 8], F32, kind="ExternalInput")
    io.gupT = DT("gupT", [NE, D, 2 * IMOE], F32, kind="ExternalInput")
    io.dpT = DT("dpT", [NE, IMOE, D], F32, kind="ExternalInput")
    io.shgT = DT("shgT", [D, ISH], F32, kind="ExternalInput")
    io.shuT = DT("shuT", [D, ISH], F32, kind="ExternalInput")
    io.shdT = DT("shdT", [ISH, D], F32, kind="ExternalInput")
    io.sgT = DT("sgT", [D, 1], F32, kind="ExternalInput")


def build_layer0_test():
    nc = bass.Bass("TRN2", target_bir_lowering=False, debug=False)
    bld = Bld(nc)
    io = IO()
    io.h_in = nc.dram_tensor("h_in", [LOC, D], F32, kind="ExternalInput")
    decl_common_inputs(nc, io)
    decl_l0_inputs(nc, io)
    decl_moe_inputs(nc, io)
    o_x = nc.dram_tensor("o_x", [D, OWN], F32, kind="ExternalOutput")
    c = alloc_common(nc, bld)
    for v in (1e-5, 1.0 / 128.0, 1e-6, 1.28e-4):
        reg_const(c, v)
    emit_const_init(c)
    emit_const_loads(c)
    bld.barrier()
    emit_p0(c, io)
    emit_l0_kvq(c, io)
    emit_l0_scores(c, io)
    emit_l0_out(c, io)
    emit_moe(c, io)
    b = bld
    for kc in range(16):
        t = b.do("sync", lambda eo, kc=kc: eo.dma_start(
            o_x[:].rearrange("(k p) t -> k p t", p=128)[kc],
            c.x[:, kc * 1024:(kc + 1) * 1024]), dma=b.S("o0"))
    b.do("sync", lambda eo, t=t: eo.nop(), after=(t,))
    bld.finalize()
    return nc


def row_bcast_exact(c, psum_ap, row_f32, hi_b, lo_b, after=(), pn=None):
    b = c.b
    io = c.io
    t1 = b.do("vector", lambda eo: eo.tensor_copy(hi_b, row_f32), after=after,
              inc=True)
    t2 = b.do("vector", lambda eo: eo.tensor_tensor(lo_b, row_f32, hi_b,
              ALU.subtract), inc=True)
    n = row_f32.shape[1]
    m1 = b.do("tensor", lambda eo: eo.matmul(psum_ap, io.onesrow_b, hi_b,
              start=True, stop=False), after=(t1, t2, puse(c, pn)), inc=True)
    m2 = b.do("tensor", lambda eo: eo.matmul(psum_ap, io.onesrow_b, lo_b,
              start=False, stop=True), inc=True)
    return m2


def emit_ln1(c, io):
    b = c.b
    reg_const(c, 1e-5)
    bfcF = c.bfc[:].bitcast(F32)
    xsq = bfcF[:, 0:1024]
    prev_sq = None
    toks = []
    for kc in range(16):
        sq = b.do("scalar", lambda eo, kc=kc: eo.activation(
            xsq, c.x[:, kc * 1024:(kc + 1) * 1024], AF.Square),
            after=[prev_sq], inc=True)
        for nt in range(2):
            pnm = "pC" if nt == 0 else "pD"
            ps = getattr(c, pnm)
            aft = [puse(c, pnm)] if kc == 0 else []
            t = b.do("tensor", lambda eo, kc=kc, nt=nt, ps=ps: eo.matmul(
                ps[0:1, 0:512], io.onescol_f,
                c.x[:, kc * 1024 + nt * 512:kc * 1024 + nt * 512 + 512],
                start=kc == 0, stop=kc == 15), after=aft, inc=kc == 15)
            if t:
                toks.append(t)
            pnm2 = "pE" if nt == 0 else "pF"
            ps2 = getattr(c, pnm2)
            aft = ([puse(c, pnm2)] if kc == 0 else []) + [sq]
            t2_ = b.do("tensor", lambda eo, kc=kc, nt=nt, ps2=ps2: eo.matmul(
                ps2[0:1, 0:512], io.onescol_f,
                xsq[:, nt * 512:nt * 512 + 512],
                start=kc == 0, stop=kc == 15), after=aft, inc=True)
            if kc == 15:
                toks.append(t2_)
            if nt == 1:
                prev_sq = t2_
    mean = c.fs[0:1, 0:1024]
    v2 = c.fs[0:1, 1024:2048]
    tmp = bfcF[0:1, 1024:2048]
    for nt in range(2):
        psm = (c.pC if nt == 0 else c.pD)[0:1, 0:512]
        psq = (c.pE if nt == 0 else c.pF)[0:1, 0:512]
        b.do("vector", lambda eo, nt=nt, psm=psm: eo.tensor_scalar(
            mean[0:1, nt * 512:nt * 512 + 512], psm, 1.0 / 2048.0, None,
            ALU.mult), after=toks)
        b.do("vector", lambda eo, nt=nt, psq=psq: eo.tensor_scalar(
            v2[0:1, nt * 512:nt * 512 + 512], psq, 1.0 / 2048.0, None,
            ALU.mult), after=toks)
    t1 = b.do("vector", lambda eo: eo.tensor_mul(tmp, mean, mean), inc=True)
    t2 = b.do("vector", lambda eo: eo.tensor_tensor(v2, v2, tmp, ALU.subtract),
              inc=True)
    t3 = b.do("scalar", lambda eo: eo.activation(tmp, v2, AF.Sqrt, bias=1e-5),
              after=(t2,), inc=True)
    t4 = b.do("vector", lambda eo: eo.reciprocal(v2, tmp), after=(t3,), inc=True)
    t5 = b.do("vector", lambda eo: eo.scalar_tensor_tensor(
        tmp, mean, -1.0, v2, ALU.mult, ALU.mult), inc=True)
    hi_b = c.bfc[0:1, 4096:5120]
    lo_b = c.bfc[0:1, 5120:6144]
    mma = []
    for nt in range(2):
        mma.append(row_bcast_exact(
            c, c.pA[:, nt * 512:nt * 512 + 512],
            v2[0:1, nt * 512:nt * 512 + 512],
            hi_b[0:1, nt * 512:nt * 512 + 512],
            lo_b[0:1, nt * 512:nt * 512 + 512],
            after=(t4,), pn="pA" if nt == 0 else None))
    hi_b2 = c.bfc[0:1, 6144:7168]
    lo_b2 = c.bfc[0:1, 7168:8192]
    for nt in range(2):
        mma.append(row_bcast_exact(
            c, c.pB[:, nt * 512:nt * 512 + 512],
            tmp[0:1, nt * 512:nt * 512 + 512],
            hi_b2[0:1, nt * 512:nt * 512 + 512],
            lo_b2[0:1, nt * 512:nt * 512 + 512],
            after=(t5,), pn="pB" if nt == 0 else None))
    t32 = bfcF[:, 2048 - 1024:2048]
    prev_t = None
    for kc in range(16):
        tt1 = b.do("vector", lambda eo, kc=kc: eo.tensor_mul(
            t32, c.x[:, kc * 1024:(kc + 1) * 1024], c.pA[:]),
            after=mma + [prev_t], inc=True)
        tt2 = b.do("vector", lambda eo, kc=kc: eo.tensor_tensor(
            c.bfa[:, kc * 1024:(kc + 1) * 1024], t32, c.pB[:], ALU.add),
            inc=True)
        prev_t = tt2
    prel(c, "pA", prev_t)
    prel(c, "pB", prev_t)
    b.barrier()


def decl_l1_inputs(nc, io, mode):
    DT = nc.dram_tensor
    io.wproj = DT("wproj", [D, D], F32, kind="ExternalInput")
    io.bproj = _sb_const(nc, "bproj", [128, 16], F32)
    io.qgT = DT("qgT", [D, IFF], I32, kind="ExternalInput")
    io.quT = DT("quT", [D, IFF], I32, kind="ExternalInput")
    io.qdT = DT("qdT", [IFF, D], I32, kind="ExternalInput")
    io.gscol = _sb_const(nc, "gscol", [128, 64], F32)
    io.uscol = _sb_const(nc, "uscol", [128, 64], F32)
    io.dscol = _sb_const(nc, "dscol", [128, 16], F32)
    io.onescol_f = _sb_const(nc, "onescol_f", [128, 1], F32)
    io.ones128 = _sb_const(nc, "ones128", [128, 128], BF16)
    io.tri = _sb_const(nc, "tri", [128, 128], BF16)
    io.bmat = _sb_const(nc, "bmat", [128, 64], F32)
    io.f1m = _sb_const(nc, "f1m", [128, 16], F32)
    io.f1g = _sb_const(nc, "f1g", [128, 64], F32)
    io.aprd = DT("aprd", [128, 1024], F32, kind="ExternalInput")


def emit_l1_qkv(c, io, kv_dram, kick=None):
    b = c.b
    wdb = DBuf(c, "wq")
    sdb = DBuf(c, "kvs", queue="sync")
    for mc in list(range(16, 48)) + list(range(16)):
        sl = wdb.next()
        dst = wslab_dst(c, (c.wb, sl * 2048), 16)
        ld = wdb.load(sl, dst, wslab_src(io.wqkv, mc))
        pn = "pA" if mc % 2 == 0 else "pB"
        ps = getattr(c, pn)
        toks = []
        for kc in range(16):
            for nt in range(2):
                st, sp = kc == 0, kc == 15
                aft = [ld, puse(c, pn)] if st and nt == 0 else []
                mm = b.do("tensor", lambda eo, kc=kc, st=st, sp=sp, ps=ps, nt=nt,
                          sl=sl: eo.matmul(
                              ps[:, nt * 512:nt * 512 + 512],
                              c.wb[:, sl * 2048 + kc * 128:sl * 2048 + (kc + 1) * 128],
                              c.bfa[:, kc * 1024 + nt * 512:kc * 1024 + nt * 512 + 512],
                              start=st, stop=sp), after=aft, inc=sp)
                if sp:
                    toks.append(mm)
        bcol = io.bqkv[:, mc:mc + 1]
        if mc < 16:
            ev = b.do("vector", lambda eo, ps=ps, mc=mc, bcol=bcol: eo.tensor_scalar(
                c.bfb[:, mc * 1024:(mc + 1) * 1024], ps[:], bcol, 1.0 / 128.0,
                ALU.add, ALU.mult), after=toks, inc=True)
            prel(c, pn, ev)
            wdb.done(sl, ev)
        elif mc < 32:
            kc_ = mc - 16
            ssl = sdb.next()
            stg = c.bfd[:, ssl * 1024:(ssl + 1) * 1024]
            ev = b.do("vector", lambda eo, ps=ps, bcol=bcol, stg=stg:
                      eo.tensor_scalar(stg, ps[:], bcol, None, ALU.add),
                      after=toks + [sdb.cons[ssl]], inc=True)
            prel(c, pn, ev)
            wdb.done(sl, ev)
            st_ = sdb.load(ssl, kv_dram[kc_ * 128:(kc_ + 1) * 128, :], stg,
                           after=(ev,))
            sdb.done(ssl, st_)
        else:
            mcv = mc - 32
            ssl = sdb.next()
            stg = c.bfd[:, ssl * 1024:(ssl + 1) * 1024]
            ev = b.do("vector", lambda eo, ps=ps, bcol=bcol, stg=stg:
                      eo.tensor_scalar(stg, ps[:], bcol, None, ALU.add),
                      after=toks + [sdb.cons[ssl]], inc=True)
            prel(c, pn, ev)
            wdb.done(sl, ev)
            vstg = c.bfd[:, 2048 + ssl * 1024:2048 + (ssl + 1) * 1024]
            evs = None
            for tt in range(8):
                pn2 = "pE" if tt % 2 == 0 else "pF"
                pt = getattr(c, pn2)[:].bitcast(BF16)
                tr = b.do("tensor", lambda eo, tt=tt, stg=stg, pt=pt: eo.transpose(
                    pt[:, 0:128], stg[:, tt * 128:(tt + 1) * 128], io.identb),
                    after=(ev, puse(c, pn2)), inc=True)
                evs = b.do("vector", lambda eo, tt=tt, pt=pt, vstg=vstg:
                           eo.tensor_copy(vstg[:, tt * 128:(tt + 1) * 128],
                                          pt[:, 0:128]), after=(tr,), inc=True)
                prel(c, pn2, evs)
            dst2 = dap(kv_dram, 2048 * 1024 + mcv * 128,
                       [(2048, 128), (128 * 2048, 8), (1, 128)])
            st_ = sdb.load(ssl, dst2,
                           vstg[:].rearrange("p (t f) -> p t f", f=128),
                           after=(evs,))
            sdb.done(ssl, st_)
        if mc == 47 and kick is not None:
            kick([(sm, sm.n) for sm in sdb.sems])
    b.barrier()


ALIBI_SLOPES = [2.0 ** (-0.5 * (i + 1)) for i in range(16)]


def emit_l1_attn(c, io, ag_dram, own_dram):
    b = c.b
    la = b.do("sync", lambda eo: eo.dma_start(c.fs[:, 0:1024], io.aprd[:]),
              dma=b.S("apr"))
    kdb = DBuf(c, "l1kv", nslots=2, queue="sync")
    wpar = [None, None, None, None]
    SPS = (("pA", 0, "pA0"), ("pA", 512, "pA1"), ("pB", 0, "pB0"), ("pB", 512, "pB1"))
    for cq in range(2):
        for h in range(16):
            hi5 = h >= 5
            sl = kdb.next()
            kbuf = c.bfc[:, sl * 8192:sl * 8192 + 4096]
            vbuf = c.bfc[:, sl * 8192 + 4096:sl * 8192 + 8192]
            kown = c.bfd[:, 4096 + sl * 1024:4096 + sl * 1024 + 512]
            vown = c.bfd[:, 4096 + sl * 1024 + 512:4096 + sl * 1024 + 1024]
            ksrc = ag_dram[:].rearrange("(r x) cc -> x r cc", x=4096)[
                h * 128:(h + 1) * 128, :, cq * 512:(cq + 1) * 512]
            kdb.load(sl, kbuf[:].rearrange("p (r cc) -> p r cc", cc=512), ksrc)
            for r in range(8):
                vsrc = dap(ag_dram,
                           (r * 4096 + 2048) * 1024 + cq * 512 * 2048 + h * 128,
                           [(2048, 128), (128 * 2048, 4), (1, 128)])
                kdb.load(sl, vbuf[:, r * 512:(r + 1) * 512].rearrange(
                    "p (t f) -> p t f", f=128), vsrc)
            kdb.load(sl, kown, own_dram[h * 128:(h + 1) * 128,
                                        cq * 512:(cq + 1) * 512])
            vsrc2 = dap(own_dram, 2048 * 1024 + cq * 512 * 2048 + h * 128,
                        [(2048, 128), (128 * 2048, 4), (1, 128)])
            ldt = kdb.load(sl, vown[:].rearrange("p (t f) -> p t f", f=128), vsrc2)
            qh = c.bfb[:, h * 1024 + cq * 512:h * 1024 + cq * 512 + 512]
            steps = [("ag", J) for J in range(32)] + \
                    [("lo", Jl) for Jl in (3, 2, 1, 0)]
            pend = None
            pend2 = None
            first = [True]
            pvlast = [None, None]

            def flush(last, pend_=None):
                kind, idx, w, n, tss = (pend_ or pend)[:5]
                st_ = first[0]
                if kind == "ag":
                    lh = vbuf[:, idx * 128:(idx + 1) * 128]
                    co = 0
                else:
                    lh = vown[:, idx * 128:(idx + 1) * 128]
                    co = idx * 128
                aft = [tss, (puse(c, "pE") if st_ else None)]
                pv = b.do("tensor", lambda eo, lh=lh, w=w, n=n, co=co, st_=st_,
                          last=last: eo.matmul(
                              c.pE[:, co:co + n], lh, w[:, 0:n],
                              start=st_, stop=last), after=aft, inc=True)
                dn = b.do("tensor", lambda eo, w=w, n=n, co=co, st_=st_,
                          last=last: eo.matmul(
                              c.pF[:, co:co + n], io.ones128, w[:, 0:n],
                              start=st_, stop=last),
                          after=[puse(c, "pF") if st_ else None], inc=True)
                first[0] = False
                pvlast[0], pvlast[1] = pv, dn
                return dn

            si = 0
            for kind, idx in steps:
                pn, poff, pkey = SPS[si % 4]
                ps = getattr(c, pn)[:, poff:poff + 512]
                wsl = si % 4
                w = c.bfd[:, 6144 + wsl * 512:6144 + wsl * 512 + 512]
                if kind == "ag":
                    n = 512
                    mm = b.do("tensor", lambda eo, idx=idx, ps=ps, kbuf=kbuf,
                              qh=qh: eo.matmul(
                                  ps[:, 0:512], kbuf[:, idx * 128:(idx + 1) * 128],
                                  qh, start=True, stop=True),
                              after=(ldt, puse(c, pkey)), inc=True)
                else:
                    n = 512 - idx * 128
                    mm = b.do("tensor", lambda eo, idx=idx, n=n, ps=ps,
                              kown=kown, qh=qh: eo.matmul(
                                  ps[:, 0:n], kown[:, idx * 128:(idx + 1) * 128],
                                  qh[:, idx * 128:512], start=True, stop=True),
                              after=(ldt, puse(c, pkey)), inc=True)
                if kind == "ag" and si % 2 == 0:
                    held = (kind, idx, w, n, mm, pkey, wsl)
                    si += 1
                    continue
                if pend is not None:
                    wpar[pend[6]] = flush(False)
                    pend = pend2
                    pend2 = None
                if pend is not None:
                    wpar[pend[6]] = flush(False)
                    pend = None
                if kind == "ag":
                    k0, i0, w0, n0, mm0, pk0, ws0 = held
                    wpair = c.bfd[:, 6144 + ws0 * 512:6144 + ws0 * 512 + 1024]
                    pspair = getattr(c, pn)[:]
                    ex = b.do("scalar", lambda eo, wpair=wpair, pspair=pspair:
                              eo.activation(wpair, pspair, AF.Exp),
                              after=(mm, wpar[ws0], wpar[wsl], la), inc=True)
                    prel(c, pk0, ex)
                    prel(c, pkey, ex)
                else:
                    ex = b.do("scalar", lambda eo, ps=ps, w=w, n=n: eo.activation(
                        w[:, 0:n], ps[:, 0:n], AF.Exp),
                        after=(mm, wpar[wsl], la), inc=True)
                    prel(c, pkey, ex)
                if kind == "ag":
                    def do_ts(w_, idx_):
                        if hi5:
                            return b.do("vector", lambda eo, w_=w_, idx_=idx_:
                                        eo.tensor_scalar(
                                            w_, w_,
                                            c.fs[:, (cq * 16 + h) * 32 + idx_:
                                                 (cq * 16 + h) * 32 + idx_ + 1],
                                            None, ALU.mult), after=(ex,),
                                        inc=True)
                        t_ = None
                        for jt in range(4):
                            t_ = b.do("vector", lambda eo, w_=w_, jt=jt,
                                      idx_=idx_: eo.tensor_scalar(
                                          w_[:, jt * 128:(jt + 1) * 128],
                                          w_[:, jt * 128:(jt + 1) * 128],
                                          c.fs[:, (cq * 16 + h) * 32 + idx_:
                                               (cq * 16 + h) * 32 + idx_ + 1],
                                          io.bmat[:, h * 4 + jt:h * 4 + jt + 1],
                                          ALU.mult, ALU.mult),
                                      after=(ex,), inc=True)
                        return t_
                    t0_ = do_ts(w0, i0)
                    t1_ = do_ts(w, idx)
                    pend = (k0, i0, w0, n0, t0_, pk0, ws0)
                    pend2 = ("ag", idx, w, n, t1_, pkey, wsl)
                    si += 1
                    continue
                elif False:
                    pass
                else:
                    if hi5:
                        tss = b.do("vector", lambda eo, w=w, n=n, h=h, idx=idx:
                                   eo.tensor_scalar(
                                       w[:, 0:n], w[:, 0:n],
                                       io.f1g[:, h * 4 + idx:h * 4 + idx + 1],
                                       None, ALU.mult), after=(ex,), inc=True)
                    else:
                        tss = None
                        for jt in range(idx, 4):
                            g = math.exp(min(ALIBI_SLOPES[h] * 128.0 * (idx - jt),
                                             0.0))
                            g = float(np.float32(g))
                            tss = b.do("vector", lambda eo, w=w, jt=jt, idx=idx,
                                       h=h, g=g: eo.tensor_scalar(
                                           w[:, (jt - idx) * 128:(jt - idx + 1) * 128],
                                           w[:, (jt - idx) * 128:(jt - idx + 1) * 128],
                                           io.f1m[:, h:h + 1], g,
                                           ALU.mult, ALU.mult),
                                       after=(ex,), inc=True)
                    tss = b.do("vector", lambda eo, w=w: eo.tensor_mul(
                        w[:, 0:128], w[:, 0:128], io.tri), inc=True)
                pend = (kind, idx, w, n, tss, pkey, wsl)
                si += 1
            if pend2 is not None:
                wpar[pend[6]] = flush(False)
                pend = pend2
            wpar[pend[6]] = flush(True)
            kdb.done(sl, pvlast[1])
            crow = c.fs[0:1, 1024:1536]
            crb = c.fs[:].bitcast(BF16)[0:1, 3072:3584]
            r1 = b.do("scalar", lambda eo: eo.activation(
                crow, c.pF[0:1, 0:512], AF.Ln, scale=2.0 ** -32),
                after=[pvlast[1], puse(c, "crow")], inc=True)
            r1b = b.do("scalar", lambda eo: eo.activation(
                crow, crow, AF.Exp, scale=-1.0, bias=-22.18070977791825),
                after=(r1,), inc=True)
            prel(c, "pF", r1b)
            r2 = b.do("vector", lambda eo: eo.tensor_copy(crb, crow),
                      after=[r1b, puse(c, "crb")], inc=True)
            pnc = "pC"
            cbm = b.do("tensor", lambda eo: eo.matmul(
                c.pC[:, 0:512], io.onesrow_b, crb, start=True, stop=True),
                after=(r2, puse(c, pnc)), inc=True)
            prel(c, "crb", cbm)
            prel(c, "crow", cbm)
            cbs = c.bfd[:].bitcast(F32)[:, 1024:1536]
            r3 = b.do("vector", lambda eo: eo.tensor_copy(cbs, c.pC[:, 0:512]),
                      after=(cbm,), inc=True)
            prel(c, pnc, r3)
            av = b.do("vector", lambda eo, qh=qh: eo.tensor_mul(
                qh, c.pE[:, 0:512], cbs), after=[pvlast[0], r3], inc=True)
            prel(c, "pE", av)
    b.barrier()


def emit_l1_oproj(c, io):
    b = c.b
    wdb = DBuf(c, "wp")
    for mc in range(16):
        sl = wdb.next()
        dst = wslab_dst(c, (c.wb, sl * 2048), 16)
        ld = wdb.load(sl, dst, wslab_src(io.wproj, mc))
        pn = "pA" if mc % 2 == 0 else "pB"
        ps = getattr(c, pn)
        toks = []
        for kc in range(16):
            for nt in range(2):
                st, sp = kc == 0, kc == 15
                aft = [ld, puse(c, pn)] if st and nt == 0 else []
                mm = b.do("tensor", lambda eo, kc=kc, st=st, sp=sp, ps=ps, nt=nt,
                          sl=sl: eo.matmul(
                              ps[:, nt * 512:nt * 512 + 512],
                              c.wb[:, sl * 2048 + kc * 128:sl * 2048 + (kc + 1) * 128],
                              c.bfb[:, kc * 1024 + nt * 512:kc * 1024 + nt * 512 + 512],
                              start=st, stop=sp), after=aft, inc=sp)
                if sp:
                    toks.append(mm)
        ad = b.do("vector", lambda eo, ps=ps, mc=mc: eo.scalar_tensor_tensor(
            c.x[:, mc * 1024:(mc + 1) * 1024], ps[:], io.bproj[:, mc:mc + 1],
            c.x[:, mc * 1024:(mc + 1) * 1024], ALU.add, ALU.add),
            after=toks, inc=True)
        prel(c, pn, ad)
        wdb.done(sl, ad)
    b.barrier()


def emit_l1_ffn(c, io, precast=True):
    b = c.b
    toks = []
    for nt in range(2):
        pnm = "pE" if nt == 0 else "pF"
        ps = getattr(c, pnm)
        mm = None
        for kc in range(16):
            st, sp = kc == 0, kc == 15
            aft = [puse(c, pnm)] if st else []
            mm = b.do("tensor", lambda eo, kc=kc, st=st, sp=sp, nt=nt, ps=ps:
                      eo.matmul(ps[0:1, 0:512], io.onescol_b,
                                c.bfa[:, kc * 1024 + nt * 512:kc * 1024 + nt * 512 + 512],
                                start=st, stop=sp), after=aft, inc=sp)
        toks.append(mm)
    m128x = c.fs[0:1, 0:1024]
    hix = c.fs[:].bitcast(BF16)[0:1, 2048:3072]
    lox = c.fs[:].bitcast(BF16)[0:1, 3072:4096]
    for nt in range(2):
        psr = (c.pE if nt == 0 else c.pF)[0:1, 0:512]
        t = b.do("vector", lambda eo, nt=nt, psr=psr: eo.tensor_scalar(
            m128x[0:1, nt * 512:nt * 512 + 512], psr, -128.0, None, ALU.mult),
            after=toks, inc=True)
        prel(c, "pE" if nt == 0 else "pF", t)
    t1 = b.do("vector", lambda eo: eo.tensor_copy(hix, m128x), inc=True)
    xc = b.do("vector", lambda eo: eo.tensor_tensor(lox, m128x, hix,
              ALU.subtract), inc=True)
    if precast:
        qg_r = io.qgB[:].rearrange("(kc p) m -> p kc m", p=128)
        qu_r = io.quB[:].rearrange("(kc p) m -> p kc m", p=128)
        qd_r = io.qdB[:].rearrange("(r p) m -> p r m", p=128)
        wq_ = "sync"
        pct = getattr(c, "pc_tok", None)
    else:
        qg_r = io.qgT[:].rearrange("(kc p) m -> p kc m", p=128)
        qu_r = io.quT[:].rearrange("(kc p) m -> p kc m", p=128)
        qd_r = io.qdT[:].rearrange("(r p) m -> p r m", p=128)
        wq_ = "gpsimd"
        pct = None
    wg = DBuf(c, "fg", queue=wq_)
    wu = DBuf(c, "fu", queue=wq_)
    wd = DBuf(c, "fd", queue=wq_)
    if pct is not None:
        for db in (wg, wu, wd):
            db.cons = [pct] * db.nslots
    bfdF = c.bfd[:].bitcast(F32)
    prev_sl = None
    last_down = None
    for iq in range(4):
        act_toks = [None] * 16
        for ic in range(16):
            i64 = iq * 16 + ic
            gld, uld = [], []
            for hf in range(2):
                slg = wg.next()
                gld.append((slg, wg.load(
                    slg, wslab_dst(c, (c.wb, slg * 1024), 8),
                    qg_r[:, hf * 8:(hf + 1) * 8, i64 * 128:(i64 + 1) * 128])))
                slu = wu.next()
                uld.append((slu, wu.load(
                    slu, wslab_dst(c, (c.wb, 2048 + slu * 1024), 8),
                    qu_r[:, hf * 8:(hf + 1) * 8, i64 * 128:(i64 + 1) * 128])))
            gts, uts = [], []
            for (psn, base, lds, res) in (("pA", 0, gld, gts),
                                          ("pB", 2048, uld, uts)):
                ps = getattr(c, psn)
                for kc in range(16):
                    for nt in range(2):
                        hf = kc // 8
                        sl_, ld = lds[hf]
                        st = kc == 0
                        aft = ([puse(c, psn), xc] if st and nt == 0 else []) + [ld]
                        b.do("tensor", lambda eo, kc=kc, st=st, nt=nt,
                             sl_=sl_, base=base, ps=ps: eo.matmul(
                                 ps[:, nt * 512:nt * 512 + 512],
                                 c.wb[:, base + sl_ * 1024 + (kc % 8) * 128:
                                      base + sl_ * 1024 + (kc % 8) * 128 + 128],
                                 c.bfa[:, kc * 1024 + nt * 512:
                                       kc * 1024 + nt * 512 + 512],
                                 start=st, stop=False), after=aft, inc=False)
                for nt in range(2):
                    b.do("tensor", lambda eo, nt=nt, ps=ps: eo.matmul(
                        ps[:, nt * 512:nt * 512 + 512], io.onesrow_b,
                        hix[0:1, nt * 512:nt * 512 + 512], start=False,
                        stop=False))
                    mm = b.do("tensor", lambda eo, nt=nt, ps=ps: eo.matmul(
                        ps[:, nt * 512:nt * 512 + 512], io.onesrow_b,
                        lox[0:1, nt * 512:nt * 512 + 512], start=False,
                        stop=True), inc=True)
                    res.append(mm)
            gsb = c.bfd[:, 0:1024]
            usb = c.bfd[:, 1024:2048]
            ssb = c.bfd[:, 2048:3072]
            ge = b.do("vector", lambda eo, i64=i64, gsb=gsb: eo.tensor_scalar(
                gsb, c.pA[:], io.gscol[:, i64:i64 + 1], None, ALU.mult),
                after=gts + [prev_sl], inc=True)
            prel(c, "pA", ge)
            for sl_, _ in gld:
                wg.done(sl_, ge)
            ue = b.do("vector", lambda eo, i64=i64, usb=usb: eo.tensor_scalar(
                usb, c.pB[:], io.uscol[:, i64:i64 + 1], None, ALU.mult),
                after=uts, inc=True)
            prel(c, "pB", ue)
            for sl_, _ in uld:
                wu.done(sl_, ue)
            sv = b.do("scalar", lambda eo, gsb=gsb, ssb=ssb: eo.activation(
                ssb, gsb, AF.Silu), after=(ge,), inc=True)
            prev_sl = sv
            hv = b.do("vector", lambda eo, ic=ic, ssb=ssb, usb=usb: eo.tensor_mul(
                c.bfc[:, ic * 1024:(ic + 1) * 1024], ssb, usb),
                after=(sv, last_down), inc=True)
            act_toks[ic] = hv
        toks2 = []
        for nt in range(2):
            pnm = "pE" if nt == 0 else "pF"
            ps = getattr(c, pnm)
            mm = None
            for kc in range(16):
                st, sp = kc == 0, kc == 15
                aft = [puse(c, pnm), act_toks[kc]] if st else [act_toks[kc]]
                mm = b.do("tensor", lambda eo, kc=kc, st=st, sp=sp, nt=nt, ps=ps:
                          eo.matmul(ps[0:1, 0:512], io.onescol_b,
                                    c.bfc[:, kc * 1024 + nt * 512:kc * 1024 + nt * 512 + 512],
                                    start=st, stop=sp), after=aft, inc=sp)
            toks2.append(mm)
        m128h = bfdF[0:1, 2048:3072]
        hih = c.bfd[0:1, 6144:7168]
        loh = c.bfd[0:1, 7168:8192]
        for nt in range(2):
            psr = (c.pE if nt == 0 else c.pF)[0:1, 0:512]
            t = b.do("vector", lambda eo, nt=nt, psr=psr: eo.tensor_scalar(
                m128h[0:1, nt * 512:nt * 512 + 512], psr, -128.0, None,
                ALU.mult), after=toks2 + [puse(c, "hrow")], inc=True)
            prel(c, "pE" if nt == 0 else "pF", t)
        b.do("vector", lambda eo: eo.tensor_copy(hih, m128h))
        hc_ = b.do("vector", lambda eo: eo.tensor_tensor(loh, m128h, hih,
                   ALU.subtract), inc=True)
        for hc in range(16):
            sl2 = wd.next()
            ld2 = wd.load(sl2, wslab_dst(c, (c.wb, 4096 + sl2 * 2048), 16),
                          qd_r[:, iq * 16:(iq + 1) * 16, hc * 128:(hc + 1) * 128])
            toks3 = []
            ads = []
            for kc in range(16):
                for nt in range(2):
                    pn = "pC" if nt == 0 else "pD"
                    ps = getattr(c, pn)
                    st = kc == 0
                    aft = ([ld2, puse(c, pn), hc_] if st else []) + [act_toks[kc]]
                    b.do("tensor", lambda eo, kc=kc, st=st, nt=nt, sl2=sl2,
                         ps=ps: eo.matmul(
                             ps[:, 0:512],
                             c.wb[:, 4096 + sl2 * 2048 + kc * 128:
                                  4096 + sl2 * 2048 + (kc + 1) * 128],
                             c.bfc[:, kc * 1024 + nt * 512:
                                   kc * 1024 + nt * 512 + 512],
                             start=st, stop=False), after=aft, inc=False)
            for nt in range(2):
                pn = "pC" if nt == 0 else "pD"
                ps = getattr(c, pn)
                b.do("tensor", lambda eo, nt=nt, ps=ps: eo.matmul(
                    ps[:, 0:512], io.onesrow_b,
                    hih[0:1, nt * 512:nt * 512 + 512], start=False, stop=False))
                mm = b.do("tensor", lambda eo, nt=nt, ps=ps: eo.matmul(
                    ps[:, 0:512], io.onesrow_b,
                    loh[0:1, nt * 512:nt * 512 + 512], start=False, stop=True),
                    inc=True)
                toks3.append(mm)
                ad = b.do("vector", lambda eo, ps=ps, hc=hc, nt=nt:
                          eo.scalar_tensor_tensor(
                              c.x[:, hc * 1024 + nt * 512:hc * 1024 + nt * 512 + 512],
                              ps[:, 0:512], io.dscol[:, hc:hc + 1],
                              c.x[:, hc * 1024 + nt * 512:hc * 1024 + nt * 512 + 512],
                              ALU.mult, ALU.add), after=(mm,), inc=True)
                prel(c, pn, ad)
                ads.append(ad)
            wd.done(sl2, ads[-1])
            if hc == 15:
                last_down = toks3[-1]
                prel(c, "hrow", toks3[-1])
    b.barrier()


def emit_out(c, io, o_out):
    b = c.b
    stage = c.bfc[:].bitcast(F32)
    prev_st = [None, None]
    for tt in range(8):
        sl = tt % 2
        stg = stage[:, sl * 2048:(sl + 1) * 2048]
        evs = None
        for kc in range(16):
            pn = "pE" if kc % 2 == 0 else "pF"
            pf = getattr(c, pn)[:, 0:128]
            tr = b.do("tensor", lambda eo, pf=pf, kc=kc, tt=tt: eo.transpose(
                pf, c.x[:, kc * 1024 + tt * 128:kc * 1024 + (tt + 1) * 128],
                io.identf), after=(puse(c, pn),), inc=True)
            evs = b.do("vector", lambda eo, pf=pf, kc=kc, stg=stg: eo.tensor_copy(
                stg[:, kc * 128:(kc + 1) * 128], pf),
                after=(tr, prev_st[sl]), inc=True)
            prel(c, pn, evs)
        st_ = b.do("sync", lambda eo, tt=tt, stg=stg: eo.dma_start(
            o_out[tt * 128:(tt + 1) * 128, :], stg), after=(evs,),
            dma=b.S("oo%d" % sl))
        prev_st[sl] = st_
    b.do("sync", lambda eo: eo.nop(),
         after=[(b.S("oo0"), b.S("oo0").n), (b.S("oo1"), b.S("oo1").n)])


def build_launch1():
    nc = bass.Bass("TRN2", target_bir_lowering=False, debug=False)
    bld = Bld(nc)
    io = IO()
    io.h_in = nc.dram_tensor("h_in", [LOC, D], F32, kind="ExternalInput")
    decl_common_inputs(nc, io)
    decl_l0_inputs(nc, io)
    decl_moe_inputs(nc, io)
    DT = nc.dram_tensor
    io.wqkv = DT("wqkv", [D, 3 * D], F32, kind="ExternalInput")
    io.bqkv = _sb_const(nc, "bqkv", [128, 48], F32)
    io.onescol_f = _sb_const(nc, "onescol_f", [128, 1], F32)
    kvout = DT("kvout", [4096, OWN], BF16, kind="ExternalOutput")
    x1_d = DT("x1_d", [D, OWN], F32, kind="ExternalOutput")
    xn1_d = DT("xn1_d", [D, OWN], BF16, kind="ExternalOutput")
    q_d = DT("q_d", [D, OWN], BF16, kind="ExternalOutput")
    c = alloc_common(nc, bld)
    c.io = io
    for v in (1e-5, 1.0 / 128.0, 1e-6, 1.28e-4):
        reg_const(c, v)
    emit_const_init(c)
    emit_const_loads(c)
    bld.barrier()
    emit_p0(c, io)
    emit_l0_kvq(c, io)
    emit_l0_scores(c, io)
    emit_l0_out(c, io)
    emit_moe(c, io)
    emit_ln1(c, io)
    emit_l1_qkv(c, io, kvout)
    b = bld
    for kc in range(16):
        b.do("sync", lambda eo, kc=kc: eo.dma_start(
            x1_d[:].rearrange("(k p) t -> k p t", p=128)[kc],
            c.x[:, kc * 1024:(kc + 1) * 1024]), dma=b.S("o0"))
        b.do("sync", lambda eo, kc=kc: eo.dma_start(
            xn1_d[:].rearrange("(k p) t -> k p t", p=128)[kc],
            c.bfa[:, kc * 1024:(kc + 1) * 1024]), dma=b.S("o0"))
        t = b.do("sync", lambda eo, kc=kc: eo.dma_start(
            q_d[:].rearrange("(k p) t -> k p t", p=128)[kc],
            c.bfb[:, kc * 1024:(kc + 1) * 1024]), dma=b.S("o0"))
    b.do("sync", lambda eo, t=t: eo.nop(), after=(t,))
    bld.finalize()
    return nc


def build_launch2():
    nc = bass.Bass("TRN2", target_bir_lowering=False, debug=False)
    bld = Bld(nc)
    io = IO()
    decl_common_inputs(nc, io)
    decl_l1_inputs(nc, io, "l2")
    DT = nc.dram_tensor
    io.onesrow_b = _sb_const(nc, "onesrow_b", [1, 128], BF16)
    io.onesrow_f = _sb_const(nc, "onesrow_f", [1, 128], F32)
    io.onescol_b = _sb_const(nc, "onescol_b", [128, 1], BF16)
    x1_d = DT("x1_d", [D, OWN], F32, kind="ExternalInput")
    xn1_d = DT("xn1_d", [D, OWN], BF16, kind="ExternalInput")
    q_d = DT("q_d", [D, OWN], BF16, kind="ExternalInput")
    kvin = DT("kvin", [4096, OWN], BF16, kind="ExternalInput")
    agx = DT("agx", [8 * 4096, OWN], BF16, kind="ExternalInput")
    o_out = DT("o_out", [OWN, D], F32, kind="ExternalOutput")
    c = alloc_common(nc, bld)
    c.io = io
    reg_const(c, 2.0 ** -32)
    reg_const(c, -1.0)
    reg_const(c, -22.18070977791825)
    io.qgB = DT("qgB", [D, IFF], BF16)
    io.quB = DT("quB", [D, IFF], BF16)
    io.qdB = DT("qdB", [IFF, D], BF16)
    emit_const_init(c)
    emit_const_loads(c)
    b = bld
    pc = None
    for (src, dst, nch) in ((io.qgT, io.qgB, 8), (io.quT, io.quB, 8),
                            (io.qdT, io.qdB, 8)):
        rows = src.shape[0] // nch
        for ch in range(nch):
            pc = b.do("gpsimd", lambda eo, src=src, dst=dst, ch=ch, rows=rows:
                      eo.dma_start(dst[ch * rows:(ch + 1) * rows, :],
                                   src[ch * rows:(ch + 1) * rows, :]),
                      dma=b.S("bgc"))
    c.pc_tok = pc
    t1 = b.do("sync", lambda eo: eo.dma_start(
        c.x[:].rearrange("p (k t) -> p k t", t=1024),
        x1_d[:].rearrange("(k p) t -> p k t", p=128)), dma=b.S("i0"))
    t2 = b.do("sync", lambda eo: eo.dma_start(
        c.bfa[:].rearrange("p (k t) -> p k t", t=1024),
        xn1_d[:].rearrange("(k p) t -> p k t", p=128)), dma=b.S("i1"))
    t3 = b.do("sync", lambda eo: eo.dma_start(
        c.bfb[:].rearrange("p (k t) -> p k t", t=1024),
        q_d[:].rearrange("(k p) t -> p k t", p=128)), dma=b.S("i2"))
    b.do("sync", lambda eo: eo.nop(), after=(t1, t2, t3))
    bld.barrier()
    emit_l1_ffn(c, io)
    emit_l1_attn(c, io, agx, kvin)
    emit_l1_oproj(c, io)
    emit_out(c, io, o_out)
    bld.finalize()
    return nc


def build_ln1_test():
    nc = bass.Bass("TRN2", target_bir_lowering=False, debug=False)
    bld = Bld(nc)
    io = IO()
    decl_common_inputs(nc, io)
    io.onescol_f = _sb_const(nc, "onescol_f", [128, 1], F32)
    io.onesrow_b = _sb_const(nc, "onesrow_b", [1, 128], BF16)
    DT = nc.dram_tensor
    x_in = DT("x_in", [D, OWN], F32, kind="ExternalInput")
    o_xn = DT("o_xn", [D, OWN], BF16, kind="ExternalOutput")
    c = alloc_common(nc, bld)
    c.io = io
    reg_const(c, 1e-5)
    emit_const_init(c)
    emit_const_loads(c)
    b = bld
    t1 = b.do("sync", lambda eo: eo.dma_start(
        c.x[:].rearrange("p (k t) -> p k t", t=1024),
        x_in[:].rearrange("(k p) t -> p k t", p=128)), dma=b.S("i0"))
    b.do("sync", lambda eo: eo.nop(), after=(t1,))
    bld.barrier()
    emit_ln1(c, io)
    for kc in range(16):
        t = b.do("sync", lambda eo, kc=kc: eo.dma_start(
            o_xn[:].rearrange("(k p) t -> k p t", p=128)[kc],
            c.bfa[:, kc * 1024:(kc + 1) * 1024]), dma=b.S("o0"))
    b.do("sync", lambda eo, t=t: eo.nop(), after=(t,))
    bld.finalize()
    return nc


def build_fused():
    nc = bass.Bass("TRN2", target_bir_lowering=False, debug=False)
    bld = Bld(nc)
    io = IO()
    io.h_in = nc.dram_tensor("h_in", [LOC, D], F32, kind="ExternalInput")
    decl_common_inputs(nc, io)
    decl_l0_inputs(nc, io)
    decl_moe_inputs(nc, io)
    decl_l1_inputs(nc, io, "fused")
    DT = nc.dram_tensor
    io.wqkv = DT("wqkv", [D, 3 * D], F32, kind="ExternalInput")
    io.bqkv = _sb_const(nc, "bqkv", [128, 48], F32)
    io.qgB = DT("qgB", [D, IFF], BF16)
    io.quB = DT("quB", [D, IFF], BF16)
    io.qdB = DT("qdB", [IFF, D], BF16)
    ag_in = DT("ag_in", [4096, OWN], BF16)
    ag_out = DT("ag_out", [8 * 4096, OWN], BF16)
    o_out = DT("o_out", [OWN, D], F32, kind="ExternalOutput")
    c = alloc_common(nc, bld)
    c.io = io
    for v in (1e-5, 1.0 / 128.0, 1e-6, 1.28e-4):
        reg_const(c, v)
    reg_const(c, 2.0 ** -32)
    reg_const(c, -1.0)
    reg_const(c, -22.18070977791825)
    emit_const_init(c)
    emit_const_loads(c)
    bld.barrier()
    b = bld
    for (src, dst, nch) in ((io.qgT, io.qgB, 8), (io.quT, io.quB, 8),
                            (io.qdT, io.qdB, 8)):
        rows = src.shape[0] // nch
        for ch in range(nch):
            b.bg(lambda eo, src=src, dst=dst, ch=ch, rows=rows: eo.dma_start(
                dst[ch * rows:(ch + 1) * rows, :],
                src[ch * rows:(ch + 1) * rows, :]))
    emit_p0(c, io)
    emit_l0_kvq(c, io)
    emit_l0_scores(c, io)
    emit_l0_out(c, io)
    emit_moe(c, io)
    while bld.bg_dmas:
        bld.drip()
    c.pc_tok = (bld.S("bgc"), bld.S("bgc").n)
    emit_ln1(c, io)

    def kick(aft):
        b.do("gpsimd", lambda eo: eo.collective_compute(
            "AllGather", ALU.bypass, replica_groups=[list(range(8))],
            ins=[ag_in.ap().opt()], outs=[ag_out.ap().opt()]),
            after=aft, inc=True)
    emit_l1_qkv(c, io, ag_in, kick=kick)
    emit_l1_ffn(c, io)
    bld.barrier()
    emit_l1_attn(c, io, ag_out, ag_in)
    emit_l1_oproj(c, io)
    emit_out(c, io, o_out)
    bld.finalize()
    return nc




def _chunks_of(core):
    return core, 7 - core


def _h_rows(core, h_full):
    a, bb = _chunks_of(core)
    rows = np.zeros((LOC, D), np.float32)
    for i, (b_, s0) in enumerate([(0, a * CH), (1, bb * CH)]):
        halo = np.zeros((CH, D), np.float32)
        if s0 >= CH:
            halo = h_full[b_, s0 - CH:s0]
        rows[i * 1024:(i + 1) * 1024] = np.concatenate(
            [halo, h_full[b_, s0:s0 + CH]], 0)
    return np.ascontiguousarray(rows)


def _positions(core):
    a, bb = _chunks_of(core)
    own = np.concatenate([np.arange(a * CH, (a + 1) * CH),
                          np.arange(bb * CH, (bb + 1) * CH)])
    loc = np.concatenate([np.arange(a * CH - CH, (a + 1) * CH),
                          np.arange(bb * CH - CH, (bb + 1) * CH)])
    return own, loc


def _rope_tables(pos):
    n = len(pos)
    inv = 1.0 / (10000.0 ** (np.arange(0, 64, 2) / 64.0))
    ang = np.maximum(pos, 0)[None, :] * inv[:, None]
    C = np.ones((128, n), np.float64)
    S = np.zeros((128, n), np.float64)
    C[0:32] = np.cos(ang)
    C[32:64] = np.cos(ang)
    S[0:32] = -np.sin(ang)
    S[32:64] = np.sin(ang)
    return C.astype(BF), S.astype(BF)


def _rperm():
    R = np.zeros((128, 128), np.float32)
    for j in range(32):
        R[j, 32 + j] = 1.0
        R[32 + j, j] = 1.0
    return R.astype(BF)


def _l0_erow(core):
    a, bb = _chunks_of(core)
    out = np.zeros((1, 5120), np.float64)
    kj = np.arange(640)
    for cq, st in ((0, a * CH), (1, bb * CH)):
        for j in range(4):
            exists = (st - CH + j * 128 + kj) >= 0
            out[0, (cq * 4 + j) * 640:(cq * 4 + j) * 640 + 640] = np.where(
                exists, 0.0, NEG)
    return out.astype(BF)


def _l0_winm():
    qi = np.arange(128)[:, None]
    kj = np.arange(128)[None, :]
    first = np.where(kj > qi, 0.0, NEG)
    last = np.where(kj <= qi, 0.0, NEG)
    return np.concatenate([first, last], 1).astype(BF)


def _aprime(core):
    a, bb = _chunks_of(core)
    out = np.zeros((128, 1024), np.float64)
    p = np.arange(128)
    for cq, own_chunk in ((0, a), (1, bb)):
        for h in range(16):
            s = ALIBI_SLOPES[h]
            for r in range(8):
                for jm in range(4):
                    J = r * 4 + jm
                    jg = (r * 4 + jm) if cq == 0 else ((7 - r) * 4 + jm)
                    if jg >= 4 * own_chunk:
                        continue
                    out[:, (cq * 16 + h) * 32 + J] = np.exp(
                        s * (128.0 * jg + p - 64.0 - 512.0 * own_chunk))
    return out.astype(np.float32)


def _l1_consts():
    f1 = np.zeros((128, 16), np.float64)
    bmat = np.zeros((128, 64), np.float64)
    p = np.arange(128)
    for h in range(16):
        s = ALIBI_SLOPES[h]
        f1[:, h] = np.exp(s * (p - 64.0))
        for jt in range(4):
            bmat[:, h * 4 + jt] = np.exp(-s * 128.0 * jt)
    tri = (np.arange(128)[:, None] <= np.arange(128)[None, :]).astype(BF)
    f1g = np.ones((128, 64), np.float64)
    for h in range(5, 16):
        s_ = ALIBI_SLOPES[h]
        for jl in range(4):
            f1g[:, h * 4 + jl] = np.exp(s_ * (np.arange(128) - 64.0 + 128.0 * jl))
    return {"f1m": f1.astype(np.float32), "bmat": bmat.astype(np.float32),
            "tri": tri, "ones128": np.ones((128, 128), BF),
            "onescol_f": np.ones((128, 1), np.float32),
            "f1g": f1g.astype(np.float32)}


_NC_CACHE = {}


def _get_nc(which):
    if which not in _NC_CACHE:
        _NC_CACHE[which] = (build_launch1 if which == 1 else build_launch2)()
    return _NC_CACHE[which]


def kernel_unfused(L0_attn_h, L0_attn_wq_comp, L0_attn_wq_up, L0_attn_wk, L0_attn_wv,
           L0_attn_q_norm_w, L0_attn_k_norm_w, L0_attn_sink_logit,
           L0_attn_out_w1, L0_attn_out_w2, L0_ffn_router_weight,
           L0_ffn_gate_up_proj, L0_ffn_down_proj, L0_ffn_sh_gate_w,
           L0_ffn_sh_up_w, L0_ffn_sh_down_w, L0_ffn_shared_gate_w,
           L1_attn_w_qkv, L1_attn_b_qkv, L1_attn_w_proj, L1_attn_b_proj,
           L1_ffn_gate_w, L1_ffn_gate_scale, L1_ffn_up_w, L1_ffn_up_scale,
           L1_ffn_down_w, L1_ffn_down_scale):
    h_full = np.asarray(L0_attn_h, np.float32)
    C = np.ascontiguousarray
    shared1 = {
        "wqc": C(np.asarray(L0_attn_wq_comp, np.float32)),
        "wqu": C(np.asarray(L0_attn_wq_up, np.float32)),
        "wk": C(np.asarray(L0_attn_wk, np.float32)),
        "wv": C(np.asarray(L0_attn_wv, np.float32)),
        "w1": C(np.asarray(L0_attn_out_w1, np.float32)),
        "w2": C(np.asarray(L0_attn_out_w2, np.float32)),
        "qnw_row": C(np.asarray(L0_attn_q_norm_w, np.float32).reshape(1, 128)),
        "knw_row": C(np.asarray(L0_attn_k_norm_w, np.float32).reshape(1, 128)),
        "sink_row": C(np.asarray(L0_attn_sink_logit, np.float32).reshape(1, 16)),
        "routerT": C(np.asarray(L0_ffn_router_weight, np.float32).T),
        "gupT": C(np.asarray(L0_ffn_gate_up_proj, np.float32).transpose(0, 2, 1)),
        "dpT": C(np.asarray(L0_ffn_down_proj, np.float32).transpose(0, 2, 1)),
        "shgT": C(np.asarray(L0_ffn_sh_gate_w, np.float32).T),
        "shuT": C(np.asarray(L0_ffn_sh_up_w, np.float32).T),
        "shdT": C(np.asarray(L0_ffn_sh_down_w, np.float32).T),
        "sgT": C(np.asarray(L0_ffn_shared_gate_w, np.float32).T),
        "wqkv": C(np.asarray(L1_attn_w_qkv, np.float32)),
        "bqkv": C(np.asarray(L1_attn_b_qkv, np.float32).reshape(48, 128).T),
        "winm": _l0_winm(),
        "identb": np.eye(128).astype(BF),
        "identf": np.eye(128, dtype=np.float32),
        "rperm": _rperm(),
        "onescol_b": np.ones((128, 1), BF),
        "onesrow_b": np.ones((1, 128), BF),
        "onesrow_f": np.ones((1, 128), np.float32),
        "onescol_f": np.ones((128, 1), np.float32),
    }
    in1 = []
    for cc in range(NCORE):
        own, loc = _positions(cc)
        cq_, sq_ = _rope_tables(own)
        ck_, sk_ = _rope_tables(loc)
        d = {"h_in": _h_rows(cc, h_full), "cqd": cq_, "sqd": sq_,
             "ckd": ck_, "skd": sk_, "erow": _l0_erow(cc)}
        d.update(shared1)
        in1.append(d)
    nc1 = _get_nc(1)
    res1 = run_bass_kernel_spmd(nc1, in1, core_ids=list(range(NCORE)))
    agx = np.concatenate([res1.results[cc]["kvout"] for cc in range(NCORE)], 0)
    lc = _l1_consts()
    shared2 = {
        "wproj": C(np.asarray(L1_attn_w_proj, np.float32)),
        "bproj": C(np.asarray(L1_attn_b_proj, np.float32).reshape(16, 128).T),
        "qgT": C(np.asarray(L1_ffn_gate_w, np.int32).T),
        "quT": C(np.asarray(L1_ffn_up_w, np.int32).T),
        "qdT": C(np.asarray(L1_ffn_down_w, np.int32).T),
        "gscol": C(np.asarray(L1_ffn_gate_scale, np.float32).reshape(64, 128).T),
        "uscol": C(np.asarray(L1_ffn_up_scale, np.float32).reshape(64, 128).T),
        "dscol": C(np.asarray(L1_ffn_down_scale, np.float32).reshape(16, 128).T),
        "agx": agx,
        "identb": shared1["identb"], "identf": shared1["identf"],
        "onesrow_b": shared1["onesrow_b"], "onesrow_f": shared1["onesrow_f"],
        "onescol_b": shared1["onescol_b"],
    }
    shared2.update(lc)
    in2 = []
    for cc in range(NCORE):
        d = {"x1_d": res1.results[cc]["x1_d"],
             "xn1_d": res1.results[cc]["xn1_d"],
             "q_d": res1.results[cc]["q_d"],
             "kvin": res1.results[cc]["kvout"],
             "aprd": _aprime(cc)}
        d.update(shared2)
        in2.append(d)
    nc2 = _get_nc(2)
    res2 = run_bass_kernel_spmd(nc2, in2, core_ids=list(range(NCORE)))
    out = np.zeros((B, T, D), np.float32)
    for cc in range(NCORE):
        a, bb = _chunks_of(cc)
        o = res2.results[cc]["o_out"]
        out[0, a * CH:(a + 1) * CH] = o[0:512]
        out[1, bb * CH:(bb + 1) * CH] = o[512:1024]
    return out




def kernel_fused(**inputs):
    h_full = np.asarray(inputs["L0_attn_h"], np.float32)
    C = np.ascontiguousarray
    lc = _l1_consts()
    shared = {
        "wqc": C(np.asarray(inputs["L0_attn_wq_comp"], np.float32)),
        "wqu": C(np.asarray(inputs["L0_attn_wq_up"], np.float32)),
        "wk": C(np.asarray(inputs["L0_attn_wk"], np.float32)),
        "wv": C(np.asarray(inputs["L0_attn_wv"], np.float32)),
        "w1": C(np.asarray(inputs["L0_attn_out_w1"], np.float32)),
        "w2": C(np.asarray(inputs["L0_attn_out_w2"], np.float32)),
        "qnw_row": C(np.asarray(inputs["L0_attn_q_norm_w"], np.float32).reshape(1, 128)),
        "knw_row": C(np.asarray(inputs["L0_attn_k_norm_w"], np.float32).reshape(1, 128)),
        "sink_row": C(np.asarray(inputs["L0_attn_sink_logit"], np.float32).reshape(1, 16)),
        "routerT": C(np.asarray(inputs["L0_ffn_router_weight"], np.float32).T),
        "gupT": C(np.asarray(inputs["L0_ffn_gate_up_proj"], np.float32).transpose(0, 2, 1)),
        "dpT": C(np.asarray(inputs["L0_ffn_down_proj"], np.float32).transpose(0, 2, 1)),
        "shgT": C(np.asarray(inputs["L0_ffn_sh_gate_w"], np.float32).T),
        "shuT": C(np.asarray(inputs["L0_ffn_sh_up_w"], np.float32).T),
        "shdT": C(np.asarray(inputs["L0_ffn_sh_down_w"], np.float32).T),
        "sgT": C(np.asarray(inputs["L0_ffn_shared_gate_w"], np.float32).T),
        "wqkv": C(np.asarray(inputs["L1_attn_w_qkv"], np.float32)),
        "bqkv": C(np.asarray(inputs["L1_attn_b_qkv"], np.float32).reshape(48, 128).T),
        "wproj": C(np.asarray(inputs["L1_attn_w_proj"], np.float32)),
        "bproj": C(np.asarray(inputs["L1_attn_b_proj"], np.float32).reshape(16, 128).T),
        "qgT": C(np.asarray(inputs["L1_ffn_gate_w"], np.int32).T),
        "quT": C(np.asarray(inputs["L1_ffn_up_w"], np.int32).T),
        "qdT": C(np.asarray(inputs["L1_ffn_down_w"], np.int32).T),
        "gscol": C(np.asarray(inputs["L1_ffn_gate_scale"], np.float32).reshape(64, 128).T),
        "uscol": C(np.asarray(inputs["L1_ffn_up_scale"], np.float32).reshape(64, 128).T),
        "dscol": C(np.asarray(inputs["L1_ffn_down_scale"], np.float32).reshape(16, 128).T),
        "winm": _l0_winm(),
        "identb": np.eye(128).astype(BF),
        "identf": np.eye(128, dtype=np.float32),
        "rperm": _rperm(),
        "onescol_b": np.ones((128, 1), BF),
        "onesrow_b": np.ones((1, 128), BF),
        "onesrow_f": np.ones((1, 128), np.float32),
    }
    shared.update(lc)
    ins = []
    for cc in range(NCORE):
        own, loc = _positions(cc)
        cq_, sq_ = _rope_tables(own)
        ck_, sk_ = _rope_tables(loc)
        d = {"h_in": _h_rows(cc, h_full), "cqd": cq_, "sqd": sq_,
             "ckd": ck_, "skd": sk_, "erow": _l0_erow(cc),
             "aprd": _aprime(cc)}
        d.update(shared)
        ins.append(d)
    if "fused" not in _NC_CACHE:
        _NC_CACHE["fused"] = build_fused()
    res = run_bass_kernel_spmd(_NC_CACHE["fused"], ins, core_ids=list(range(NCORE)))
    out = np.zeros((B, T, D), np.float32)
    for cc in range(NCORE):
        a, bb = _chunks_of(cc)
        o = res.results[cc]["o_out"]
        out[0, a * CH:(a + 1) * CH] = o[0:512]
        out[1, bb * CH:(bb + 1) * CH] = o[512:1024]
    return out


def kernel(**inputs):
    return kernel_fused(**inputs)


# version 17 (final)
# speedup vs baseline: 1.0172x; 1.0036x over previous
import math
import numpy as np
import ml_dtypes
import concourse.bass as bass
import concourse.mybir as mybir
from concourse.bass_utils import run_bass_kernel_spmd

F32 = mybir.dt.float32
BF16 = mybir.dt.bfloat16
I32 = mybir.dt.int32
AF = mybir.ActivationFunctionType
ALU = mybir.AluOpType
AX = mybir.AxisListType
BF = ml_dtypes.bfloat16

NCORE = 8
B, T, D = 2, 4096, 2048
CH = 512
OWN = 1024
LOC = 2048
KCD = 16
H, HD = 16, 128
NEG = -1e30
NE, IMOE, ISH = 8, 1408, 5632
IFF = 8192


class Sem:
    __slots__ = ("h", "n", "name")

    def __init__(self, nc, name):
        self.h = nc.alloc_semaphore(name)
        self.n = 0
        self.name = name


class Bld:
    ENG = ("tensor", "vector", "scalar", "gpsimd", "sync")

    def __init__(self, nc):
        self.nc = nc
        self.q = {e: [] for e in self.ENG}
        self.sem = {e: Sem(nc, "s_" + e) for e in self.ENG}
        self.dsems = {}
        self.eng_dma = {e: {} for e in self.ENG}
        self.waited = {e: {} for e in self.ENG}
        self.nops = 0
        self.bank = {}
        self.bg_dmas = []

    def bg(self, mk):
        self.bg_dmas.append(mk)

    def drip(self):
        if self.bg_dmas:
            mk = self.bg_dmas.pop(0)
            return self.do("gpsimd", mk, dma=self.S("bgc"))
        return None

    def S(self, name):
        if name not in self.dsems:
            self.dsems[name] = Sem(self.nc, "d_" + name)
        return self.dsems[name]

    CHAIN = ("vector", "scalar", "gpsimd")

    def do(self, eng, mk, after=(), inc=False, dma=None, force_own=False):
        s = dma if dma is not None else self.sem[eng]
        es = self.sem[eng]
        waits = []
        wmap = self.waited[eng]
        after = list(after)
        if dma is None and (eng in self.CHAIN or force_own) and es.n > 0:
            after.insert(0, (es, es.n))
        for t in after:
            if t is None:
                continue
            ps, v = t
            if ps is es and not (dma is None and (eng in self.CHAIN or force_own)):
                continue
            w = wmap.get(ps.name)
            if w is None or w < v:
                waits.append((ps, v))
                wmap[ps.name] = v
        amt = 16 if dma is not None else 1
        if dma is None and eng in self.CHAIN:
            inc = True
        tok = None
        if inc or dma is not None:
            s.n += amt
            tok = (s, s.n)
        if dma is not None:
            self.eng_dma[eng][s.name] = (s, s.n)
        self.nops += 1 + len(waits)
        doinc = inc or dma is not None

        def f(eo, waits=waits, mk=mk, doinc=doinc, s=s, amt=amt):
            for ps, v in waits:
                eo.wait_ge(ps.h, v)
            ins = mk(eo)
            if doinc:
                ins.then_inc(s.h, amt)
        self.q[eng].append(f)
        return tok

    def barrier(self):
        toks = {}
        for e in self.ENG:
            aft = list(self.eng_dma[e].values())
            toks[e] = self.do(e, lambda eo: eo.nop(), after=aft, inc=True,
                              force_own=True)
            self.eng_dma[e] = {}
        for e in self.ENG:
            self.do(e, lambda eo: eo.nop(),
                    after=[toks[e2] for e2 in self.ENG if e2 != e])
        self.bank = {}
        return toks

    def finalize(self):
        with self.nc.Block() as block:
            for e in self.ENG:
                lst = self.q[e]
                getattr(block, e)(lambda eo, lst=lst: [f(eo) for f in lst])


def dap(t, offset, dims):
    return bass.AP(t, offset, [list(d) for d in dims])


class Ctx:
    pass


def alloc_common(nc, bld):
    c = Ctx()
    c.nc, c.b = nc, bld
    A = nc.alloc_sbuf_tensor
    c.x = A("x_fm", [128, 16 * 1024], F32)
    c.bfa = A("bfa", [128, 16 * 1024], BF16)
    c.bfb = A("bfb", [128, 16 * 1024], BF16)
    c.bfc = A("bfc", [128, 16 * 1024], BF16)
    c.bfd = A("bfd", [128, 8 * 1024], BF16)
    c.wb = A("wb", [128, 8 * 1024], BF16)
    c.fs = A("fs", [128, 2 * 1024], F32)
    c.cst = A("cst", [128, 16], F32)
    c.sc = A("sc", [128, 192], F32)
    P = nc.alloc_psum_tensor
    c.pA = P("pA", [128, 1024], F32)
    c.pB = P("pB", [128, 1024], F32)
    c.pC = P("pC", [128, 512], F32)
    c.pD = P("pD", [128, 512], F32)
    c.pE = P("pE", [128, 512], F32)
    c.pF = P("pF", [128, 512], F32)
    c.consts = {}
    return c


def reg_const(c, val):
    if val not in c.consts:
        i = len(c.consts)
        assert i < 16
        c.consts[val] = i
        c.nc.const_aps.register(F32, float(val), c.cst[:, i:i + 1])


def emit_const_init(c):
    b = c.b
    for val, i in list(c.consts.items()):
        b.do("vector", lambda eo, v=val, i=i: eo.memset(c.cst[:, i:i + 1], v))
    return b.do("vector", lambda eo: eo.engine_nop(), inc=True)


def puse(c, slot):
    return c.b.bank.get(slot)


def prel(c, slot, tok):
    c.b.bank[slot] = tok


def emit_p0(c, io):
    b = c.b
    stage = c.bfc[:].bitcast(F32)
    h_r = io.h_in[:].rearrange("(i p) f -> i p f", p=128)
    reg_const(c, 1e-5)
    slot_done = [None, None]
    prev = {}
    P0SC = [(c.sc[:, 64 + 16 * par + k:65 + 16 * par + k] for k in range(7))
            for par in range(2)]
    for i in range(16):
        par = i % 2
        sl = stage[:, par * 2048:par * 2048 + 2048]
        ld = b.do("sync", lambda eo, sl=sl, i=i: eo.dma_start(sl, h_r[i]),
                  after=[slot_done[par]], dma=b.S("st%d" % par))
        base = 64 + 16 * par
        s_col = c.sc[:, base:base + 1]
        q_col = c.sc[:, base + 1:base + 2]
        mu = c.sc[:, base + 2:base + 3]
        ex2 = c.sc[:, base + 3:base + 4]
        var = c.sc[:, base + 4:base + 5]
        r = c.sc[:, base + 5:base + 6]
        bb = c.sc[:, base + 6:base + 7]
        xsq = c.bfd[:, par * 4096:par * 4096 + 2048]
        xn_tm = c.bfd[:, par * 4096 + 2048:par * 4096 + 4096]
        t1 = b.do("scalar", lambda eo, sl=sl, xsq=xsq, q_col=q_col:
                  eo.activation(xsq, sl, AF.Square, accum_out=q_col),
                  after=(ld, prev.get("t3%d" % par)), inc=True)
        t3 = b.do("scalar", lambda eo, sl=sl, xsq=xsq, s_col=s_col:
                  eo.activation(xsq, sl, AF.Identity, accum_out=s_col),
                  after=(t1,), inc=True)
        prev["t3%d" % par] = t3
        b.do("vector", lambda eo, mu=mu, s_col=s_col: eo.tensor_scalar(
            mu, s_col, 1.0 / 2048.0, None, ALU.mult), after=(t3,))
        b.do("vector", lambda eo, ex2=ex2, q_col=q_col: eo.tensor_scalar(
            ex2, q_col, 1.0 / 2048.0, None, ALU.mult), after=(t1,))
        b.do("vector", lambda eo, var=var, mu=mu: eo.scalar_tensor_tensor(
            var, mu, -1.0, mu, ALU.mult, ALU.mult),
            after=[prev.get("t8%d" % par)])
        t7 = b.do("vector", lambda eo, var=var, ex2=ex2: eo.tensor_tensor(
            var, var, ex2, ALU.add), inc=True)
        t8 = b.do("scalar", lambda eo, r=r, var=var: eo.activation(
            r, var, AF.Sqrt, bias=1e-5), after=(t7, prev.get("t11%d" % par)),
            inc=True)
        prev["t8%d" % par] = t8
        t9 = b.do("vector", lambda eo, r=r: eo.reciprocal(r, r), after=(t8,),
                  inc=True)
        b.do("vector", lambda eo, bb=bb, mu=mu, r=r: eo.scalar_tensor_tensor(
            bb, mu, -1.0, r, ALU.mult, ALU.mult))
        t11 = b.do("vector", lambda eo, sl=sl, xn_tm=xn_tm, r=r, bb=bb:
                   eo.tensor_scalar(xn_tm, sl, r, bb, ALU.mult, ALU.add),
                   after=[prev.get("lastT%d" % par)], inc=True)
        prev["t11%d" % par] = t11
        tb = i // 4
        own = tb in (1, 3)
        if own:
            col = (i - 4) * 128 if tb == 1 else 512 + (i - 12) * 128
            dst = c.bfa
        else:
            col = i * 128 if tb == 0 else 512 + (i - 8) * 128
            dst = c.bfb
        lastT = None
        evs = []
        for kc in range(16):
            if par == 0:
                bslot = ("pC", "pD")[kc % 2]
                pt = getattr(c, bslot)[:].bitcast(BF16)[:, 0:128]
            else:
                bslot = "pAb" + str(kc % 2)
                pt = c.pA[:].bitcast(BF16)[:, (kc % 2) * 1024:
                                           (kc % 2) * 1024 + 128]
            tt = b.do("tensor", lambda eo, pt=pt, kc=kc, xn_tm=xn_tm: eo.transpose(
                pt, xn_tm[:, kc * 128:(kc + 1) * 128], io.identb),
                after=(t11, puse(c, bslot)), inc=True)
            lastT = tt
            ev = b.do("vector", lambda eo, pt=pt, dst=dst, kc=kc, col=col:
                      eo.tensor_copy(dst[:, kc * 1024 + col:kc * 1024 + col + 128], pt),
                      after=(tt,), inc=True)
            prel(c, bslot, ev)
            evs.append(ev)
            if own:
                if par == 0:
                    fslot = ("pE", "pF")[kc % 2]
                    pf = getattr(c, fslot)[:, 0:128]
                else:
                    fslot = "pBb" + str(kc % 2)
                    pf = c.pB[:, (kc % 2) * 512:(kc % 2) * 512 + 128]
                tf = b.do("tensor", lambda eo, pf=pf, kc=kc, sl=sl: eo.transpose(
                    pf, sl[:, kc * 128:(kc + 1) * 128], io.identf),
                    after=(t11, puse(c, fslot)), inc=True)
                lastT = tf
                ev2 = b.do("vector", lambda eo, pf=pf, kc=kc, col=col:
                           eo.tensor_copy(c.x[:, kc * 1024 + col:kc * 1024 + col + 128], pf),
                           after=(tf,), inc=True)
                prel(c, fslot, ev2)
                evs.append(ev2)
        prev["lastT%d" % par] = lastT
        slot_done[par] = evs[-1]
    b.barrier()


class IO:
    pass


def decl_common_inputs(nc, io):
    DT = nc.dram_tensor
    io.identb = _sb_const(nc, "identb", [128, 128], BF16)
    io.identf = _sb_const(nc, "identf", [128, 128], F32)


_SB_CONST_INPUTS = []


def _sb_const(nc, name, shape, dtype, sbuf_dtype=None):
    sd = sbuf_dtype or dtype
    d = nc.dram_tensor(name, shape, dtype, kind="ExternalInput")
    s = nc.alloc_sbuf_tensor("sb_" + name, shape, sd)
    _SB_CONST_INPUTS.append((d, s, sd != dtype))
    return s[:]


def emit_const_loads(c):
    b = c.b
    t = t2 = None
    for d, s, cast in _SB_CONST_INPUTS:
        if cast:
            t2 = b.do("gpsimd", lambda eo, d=d, s=s: eo.dma_start(s[:], d[:]),
                      dma=b.S("cl2"))
        else:
            t = b.do("sync", lambda eo, d=d, s=s: eo.dma_start(s[:], d[:]),
                     dma=b.S("cl"))
    if t is not None:
        b.do("sync", lambda eo: eo.nop(), after=(t,))
    if t2 is not None:
        b.do("gpsimd", lambda eo: eo.nop(), after=(t2,))
    _SB_CONST_INPUTS.clear()


def build_p0_test():
    nc = bass.Bass("TRN2", target_bir_lowering=False, debug=False)
    bld = Bld(nc)
    io = IO()
    io.h_in = nc.dram_tensor("h_in", [LOC, D], F32, kind="ExternalInput")
    decl_common_inputs(nc, io)
    o_x = nc.dram_tensor("o_x", [D, OWN], F32, kind="ExternalOutput")
    o_xn = nc.dram_tensor("o_xn", [D, OWN], BF16, kind="ExternalOutput")
    o_xh = nc.dram_tensor("o_xh", [D, OWN], BF16, kind="ExternalOutput")
    c = alloc_common(nc, bld)
    reg_const(c, 1e-5)
    emit_const_init(c)
    emit_const_loads(c)
    bld.barrier()
    emit_p0(c, io)
    so = bld.sem["sync"]
    b = bld
    for kc in range(16):
        b.do("sync", lambda eo, kc=kc: eo.dma_start(
            o_x[:].rearrange("(k p) t -> k p t", p=128)[kc],
            c.x[:, kc * 1024:(kc + 1) * 1024]), dma=b.S("o0"))
        b.do("sync", lambda eo, kc=kc: eo.dma_start(
            o_xn[:].rearrange("(k p) t -> k p t", p=128)[kc],
            c.bfa[:, kc * 1024:(kc + 1) * 1024]), dma=b.S("o0"))
        t = b.do("sync", lambda eo, kc=kc: eo.dma_start(
            o_xh[:].rearrange("(k p) t -> k p t", p=128)[kc],
            c.bfb[:, kc * 1024:(kc + 1) * 1024]), dma=b.S("o0"))
    b.do("sync", lambda eo, t=t: eo.nop(), after=(t,))
    bld.finalize()
    return nc


class DBuf:

    def __init__(self, c, name, nslots=2, queue="gpsimd"):
        self.c = c
        self.sems = [c.b.S("%s%d" % (name, i)) for i in range(nslots)]
        self.cons = [None] * nslots
        self.nslots = nslots
        self.q = queue
        self.i = 0

    def next(self):
        k = self.i % self.nslots
        self.i += 1
        return k

    def load(self, k, dst_ap, src_ap, after=()):
        return self.c.b.do(self.q, lambda eo: eo.dma_start(dst_ap, src_ap),
                           after=[self.cons[k]] + list(after), dma=self.sems[k])

    def done(self, k, tok):
        self.cons[k] = tok


def wslab_src(wdram, mc, ncols=128):
    r = wdram[:].rearrange("(kc p) m -> p kc m", p=128)
    return r[:, :, mc * ncols:(mc + 1) * ncols]


def wslab_dst(c, region_cols, kcn, ncols=128):
    a, bcol = region_cols
    return a[:, bcol:bcol + kcn * ncols].rearrange("p (kc m) -> p kc m", m=ncols)


def emit_l0_kvq(c, io):
    b = c.b
    reg_const(c, 1.0 / 128.0)
    reg_const(c, 1e-6)
    reg_const(c, 1.28e-4)
    wdb = DBuf(c, "wkv")
    KP = 4096
    VP = 6144
    lck = b.do("sync", lambda eo: eo.dma_start(c.bfd[:, 0:2048], io.ckd[:]),
               dma=b.S("rope"))
    lsk = b.do("sync", lambda eo: eo.dma_start(c.bfd[:, 2048:4096], io.skd[:]),
               dma=b.S("rope2"))
    lk = b.do("gpsimd", lambda eo: eo.dma_start(
        wslab_dst(c, (c.wb, 4096), 16), wslab_src(io.wk, 0)), dma=b.S("wk"))
    lv = b.do("gpsimd", lambda eo: eo.dma_start(
        wslab_dst(c, (c.wb, 6144), 16), wslab_src(io.wv, 0)), dma=b.S("wv"))

    def xn_g4(g4):
        src = c.bfb if g4 in (0, 2) else c.bfa
        cb = 0 if g4 < 2 else 512
        return lambda kc: src[:, kc * 1024 + cb:kc * 1024 + cb + 512]

    for g4 in range(4):
        xf = xn_g4(g4)
        ps = c.pA if g4 % 2 == 0 else c.pB
        pn = "pA" if g4 % 2 == 0 else "pB"
        wsl = c.wb[:, 6144:8192]
        mm = None
        for kc in range(16):
            st, sp = kc == 0, kc == 15
            aft = [lv, puse(c, pn)] if st else []
            mm = b.do("tensor", lambda eo, kc=kc, st=st, sp=sp, ps=ps, xf=xf, wsl=wsl:
                      eo.matmul(ps[:, 0:512], wsl[:, kc * 128:(kc + 1) * 128],
                                xf(kc), start=st, stop=sp),
                      after=aft, inc=sp)
        vg = c.bfc[:, 0:512]
        ev = b.do("vector", lambda eo, ps=ps, vg=vg: eo.tensor_copy(vg, ps[:, 0:512]),
                  after=(mm,), inc=True)
        prel(c, pn, ev)
        for t in range(4):
            pn2 = "pC" if t % 2 == 0 else "pD"
            pt = getattr(c, pn2)[:].bitcast(BF16)[:, 0:128]
            tt = b.do("tensor", lambda eo, pt=pt, vg=vg, t=t: eo.transpose(
                pt, vg[:, t * 128:(t + 1) * 128], io.identb),
                after=(ev, puse(c, pn2)), inc=True)
            ev2 = b.do("vector", lambda eo, pt=pt, g4=g4, t=t: eo.tensor_copy(
                c.bfd[:, VP + (g4 * 4 + t) * 128:VP + (g4 * 4 + t) * 128 + 128], pt),
                after=(tt,), inc=True)
            prel(c, pn2, ev2)
    prev_mk = None
    for g4 in range(4):
        xf = xn_g4(g4)
        ps = c.pA if g4 % 2 == 0 else c.pB
        pn = "pA" if g4 % 2 == 0 else "pB"
        wsl = c.wb[:, 4096:6144]
        mm = None
        for kc in range(16):
            st, sp = kc == 0, kc == 15
            aft = [lk, puse(c, pn)] if st else []
            mm = b.do("tensor", lambda eo, kc=kc, st=st, sp=sp, ps=ps, xf=xf, wsl=wsl:
                      eo.matmul(ps[:, 0:512], wsl[:, kc * 128:(kc + 1) * 128],
                                xf(kc), start=st, stop=sp),
                      after=aft, inc=sp)
        k0 = c.bfc[:, 512:1024]
        ev = b.do("vector", lambda eo, ps=ps, k0=k0: eo.tensor_copy(k0, ps[:, 0:512]),
                  after=(mm,), inc=True)
        prel(c, pn, ev)
        pn2 = "pE"
        rt = b.do("tensor", lambda eo, k0=k0: eo.matmul(
            c.pE[:, 0:512], io.rperm, k0, start=True, stop=True),
            after=(ev, puse(c, pn2)), inc=True)
        cksl = c.bfd[:, g4 * 512:g4 * 512 + 512]
        sksl = c.bfd[:, 2048 + g4 * 512:2048 + g4 * 512 + 512]
        k1 = c.bfc[:, 1024:1536]
        tmp = c.bfc[:, 1536:2048]
        b.do("vector", lambda eo, k0=k0, cksl=cksl, k1=k1: eo.tensor_mul(k1, k0, cksl),
             after=(lck,))
        t2 = b.do("vector", lambda eo, sksl=sksl, tmp=tmp: eo.tensor_mul(
            tmp, c.pE[:, 0:512], sksl), after=(rt, lsk), inc=True)
        prel(c, pn2, t2)
        t3 = b.do("vector", lambda eo, k1=k1, tmp=tmp: eo.tensor_add(k1, k1, tmp),
                  inc=True)
        ksq = c.bfc[:, 1536:2048]
        t4 = b.do("scalar", lambda eo, k1=k1, ksq=ksq: eo.activation(
            ksq, k1, AF.Square), after=(t3,), inc=True)
        pn3 = "pF"
        sq = b.do("tensor", lambda eo, ksq=ksq: eo.matmul(
            c.pF[0:1, 0:512], io.onescol_b, ksq, start=True, stop=True),
            after=(t4, puse(c, pn3)), inc=True)
        u = c.fs[0:1, 1024:1536]
        t5 = b.do("scalar", lambda eo, u=u: eo.activation(
            u, c.pF[0:1, 0:512], AF.Sqrt, bias=1e-6, scale=1.0 / 128.0),
            after=(sq,), inc=True)
        prel(c, pn3, t5)
        t6 = b.do("vector", lambda eo, u=u: eo.reciprocal(u, u), after=(t5,), inc=True)
        ub = c.fs[:].bitcast(BF16)[0:1, 1024:1536]
        t7 = b.do("vector", lambda eo, u=u, ub=ub: eo.tensor_copy(ub, u),
                  after=[prev_mk], inc=True)
        pn4 = "pC"
        mk = b.do("tensor", lambda eo, ub=ub: eo.matmul(
            c.pC[:, 0:512], io.knw_row, ub, start=True, stop=True),
            after=(t7, puse(c, pn4)), inc=True)
        prev_mk = mk
        kf = b.do("vector", lambda eo, k1=k1, g4=g4: eo.tensor_mul(
            c.bfd[:, KP + g4 * 512:KP + g4 * 512 + 512], k1, c.pC[:, 0:512]),
            after=(mk,), inc=True)
        prel(c, pn4, kf)
    b.barrier()
    lcq = b.do("sync", lambda eo: eo.dma_start(c.bfc[:, 0:1024], io.cqd[:]),
               dma=b.S("rope"))
    lsq = b.do("sync", lambda eo: eo.dma_start(c.bfc[:, 1024:2048], io.sqd[:]),
               dma=b.S("rope2"))
    io.cq = c.bfc[:, 0:1024]
    io.sq = c.bfc[:, 1024:2048]
    qc_evs = []
    prev_mq = None
    reg_const(c, 1.28e-4)
    for mc in range(4):
        sl = wdb.next()
        dst = wslab_dst(c, (c.wb, sl * 2048), 16)
        ld = wdb.load(sl, dst, wslab_src(io.wqc, mc))
        ps = c.pA if mc % 2 == 0 else c.pB
        pn = "pA" if mc % 2 == 0 else "pB"
        toks = []
        for kc in range(16):
            for nt in range(2):
                st, sp = kc == 0, kc == 15
                aft = [ld, puse(c, pn)] if st and nt == 0 else []
                mm = b.do("tensor", lambda eo, kc=kc, st=st, sp=sp, ps=ps, nt=nt, sl=sl:
                          eo.matmul(ps[:, nt * 512:nt * 512 + 512],
                                    c.wb[:, sl * 2048 + kc * 128:sl * 2048 + (kc + 1) * 128],
                                    c.bfa[:, kc * 1024 + nt * 512:kc * 1024 + nt * 512 + 512],
                                    start=st, stop=sp),
                          after=aft, inc=sp)
                if sp:
                    toks.append(mm)
        ev = b.do("vector", lambda eo, ps=ps, mc=mc: eo.tensor_copy(
            c.bfc[:, 2048 + mc * 1024:2048 + (mc + 1) * 1024], ps[:]),
            after=toks, inc=True)
        prel(c, pn, ev)
        wdb.done(sl, ev)
        qc_evs.append(ev)
    def q_post(hd):
        h_, q0_, ev_ = hd
        qh = c.bfb[:, h_ * 1024:(h_ + 1) * 1024]
        par = h_ % 2
        tmp = c.bfd[:, par * 2048:par * 2048 + 1024]
        qsq = c.bfd[:, par * 2048 + 1024:par * 2048 + 2048]
        for nt in range(2):
            pn2 = "pE" if nt == 0 else "pF"
            p2 = getattr(c, pn2)
            rt = b.do("tensor", lambda eo, q0_=q0_, nt=nt, p2=p2: eo.matmul(
                p2[:, 0:512], io.rperm, q0_[:, nt * 512:nt * 512 + 512],
                start=True, stop=True), after=(ev_, puse(c, pn2)), inc=True)
            b.do("vector", lambda eo, q0_=q0_, nt=nt, qh=qh: eo.tensor_mul(
                qh[:, nt * 512:nt * 512 + 512], q0_[:, nt * 512:nt * 512 + 512],
                io.cq[:, nt * 512:nt * 512 + 512]), after=(lcq,))
            t2 = b.do("vector", lambda eo, nt=nt, tmp=tmp, p2=p2: eo.tensor_mul(
                tmp[:, nt * 512:nt * 512 + 512], p2[:, 0:512],
                io.sq[:, nt * 512:nt * 512 + 512]), after=(rt, lsq), inc=True)
            prel(c, pn2, t2)
        t3 = b.do("vector", lambda eo, qh=qh, tmp=tmp: eo.tensor_add(qh, qh, tmp),
                  inc=True)
        t4 = b.do("scalar", lambda eo, qh=qh, qsq=qsq: eo.activation(
            qsq, qh, AF.Square), after=(t3,), inc=True)
        return (h_, qh, t4, par)

    def q_post2(hd2):
        h_, qh, t4, par = hd2
        qsq = c.bfd[:, par * 2048 + 1024:par * 2048 + 2048]
        u = c.fs[0:1, par * 1024:par * 1024 + 1024]
        ub = c.fs[:].bitcast(BF16)[0:1, 2048 + par * 1024:3072 + par * 1024]
        t5s = []
        for nt in range(2):
            pn3 = "pC" if nt == 0 else "pD"
            p3 = getattr(c, pn3)
            sq = b.do("tensor", lambda eo, qsq=qsq, nt=nt, p3=p3: eo.matmul(
                p3[0:1, 0:512], io.onescol_b, qsq[:, nt * 512:nt * 512 + 512],
                start=True, stop=True), after=(t4, puse(c, pn3)), inc=True)
            t5 = b.do("scalar", lambda eo, nt=nt, p3=p3, u=u: eo.activation(
                u[0:1, nt * 512:nt * 512 + 512], p3[0:1, 0:512],
                AF.Sqrt, bias=1.28e-4), after=(sq,), inc=True)
            prel(c, pn3, t5)
            t5s.append(t5)
        t6 = b.do("vector", lambda eo, u=u: eo.reciprocal(u, u), after=t5s,
                  inc=True)
        t7 = b.do("vector", lambda eo, u=u, ub=ub: eo.tensor_copy(ub, u),
                  after=[prev_mq[par]], inc=True)
        for nt in range(2):
            pn4 = "pE" if nt == 0 else "pF"
            p4 = getattr(c, pn4)
            mq = b.do("tensor", lambda eo, nt=nt, p4=p4, ub=ub: eo.matmul(
                p4[:, 0:512], io.qnw_row, ub[0:1, nt * 512:nt * 512 + 512],
                start=True, stop=True), after=(t7, puse(c, pn4)), inc=True)
            qf = b.do("vector", lambda eo, qh=qh, nt=nt, p4=p4: eo.tensor_mul(
                qh[:, nt * 512:nt * 512 + 512], qh[:, nt * 512:nt * 512 + 512],
                p4[:, 0:512]), after=(mq,), inc=True)
            prel(c, pn4, qf)
            prev_mq[par] = mq

    prev_mq = [None, None]
    prev_q0rd = [None, None]
    hd = hd2 = None
    for h in range(16):
        sl = wdb.next()
        dst = wslab_dst(c, (c.wb, sl * 2048), 4)
        ld = wdb.load(sl, dst, wslab_src(io.wqu, h))
        ps = c.pA if h % 2 == 0 else c.pB
        pn = "pA" if h % 2 == 0 else "pB"
        toks = []
        for kc in range(4):
            for nt in range(2):
                st, sp = kc == 0, kc == 3
                aft = ([ld, puse(c, pn)] if st and nt == 0 else []) + [qc_evs[kc]]
                mm = b.do("tensor", lambda eo, kc=kc, st=st, sp=sp, ps=ps, nt=nt,
                          sl=sl: eo.matmul(
                              ps[:, nt * 512:nt * 512 + 512],
                              c.wb[:, sl * 2048 + kc * 128:sl * 2048 + (kc + 1) * 128],
                              c.bfc[:, 2048 + kc * 1024 + nt * 512:2048 + kc * 1024 + nt * 512 + 512],
                              start=st, stop=sp),
                          after=aft, inc=sp)
                if sp:
                    toks.append(mm)
        q0 = c.bfc[:, 6144 + (h % 2) * 1024:6144 + (h % 2) * 1024 + 1024]
        ev = b.do("vector", lambda eo, ps=ps, q0=q0: eo.tensor_copy(q0, ps[:]),
                  after=toks + [prev_q0rd[h % 2]], inc=True)
        prel(c, pn, ev)
        wdb.done(sl, ev)
        if hd2 is not None:
            q_post2(hd2)
            hd2 = None
        if hd is not None:
            hd2 = q_post(hd)
            prev_q0rd[hd[0] % 2] = hd2[2]
        hd = (h, q0, ev)
    if hd2 is not None:
        q_post2(hd2)
    hd2 = q_post(hd)
    q_post2(hd2)
    b.barrier()


def emit_l0_scores(c, io):
    b = c.b
    KP, VP = 4096, 6144
    lm = b.do("sync", lambda eo: eo.dma_start(c.wb[0:1, 0:5120], io.erow[:]),
              dma=b.S("mask"))
    sb = b.do("tensor", lambda eo: eo.matmul(
        c.pE[:, 0:16], io.onesrow_f, io.sink_row, start=True, stop=True),
        after=(puse(c, "pE"),), inc=True)
    e1 = b.do("vector", lambda eo: eo.tensor_copy(c.sc[:, 16:32], c.pE[:, 0:16]),
              after=(sb,), inc=True)
    prel(c, "pE", e1)
    b.do("vector", lambda eo: eo.tensor_scalar(
        c.sc[:, 32:48], c.sc[:, 16:32], -1.0, None, ALU.mult))
    tiles = [(cq, h, j) for cq in range(2) for h in range(16) for j in range(4)]
    state = {"cp": [None, None], "pv": [None, None], "T": [None, None],
             "norm": [None, None]}
    flushT = _mk_flushT(c, io, state)
    SC = ("pA", "pB")
    pend = [None, None]
    v2p = None

    def do_v2(v2):
        ti_, cq_, h_, j_, qt_, probs_, probn_, rsum_, ske_, rden_, ex_, sx_, \
            par_ = v2
        b.do("vector", lambda eo: eo.tensor_add(rden_, rsum_, ske_),
             after=(ex_, sx_))
        b.do("vector", lambda eo: eo.reciprocal(rden_, rden_))
        nt = b.do("vector", lambda eo: eo.tensor_scalar(
            probn_, probs_, rden_, None, ALU.mult),
            after=(state["T"][par_],), inc=True)
        state["norm"][par_] = nt
        pend[par_] = (ti_, cq_, h_, j_, qt_, nt, probn_)

    for ti, (cq, h, j) in enumerate(tiles):
        if ti % 4 == 0:
            b.drip()
        par = ti % 2
        pn = SC[par]
        ps = getattr(c, pn)
        probs = c.bfc[:, par * 2048:par * 2048 + 640]
        probn = c.bfc[:, par * 2048 + 640:par * 2048 + 1280]
        negm = c.sc[:, 48 + par:49 + par]
        rsum = c.sc[:, 52 + par:53 + par]
        ske = c.sc[:, 56 + par:57 + par]
        rden = c.sc[:, 60 + par:61 + par]
        qt = c.bfb[:, h * 1024 + cq * 512 + j * 128:
                   h * 1024 + cq * 512 + j * 128 + 128]
        kb = KP + cq * 1024 + j * 128
        b.do("tensor", lambda eo, ps=ps, qt=qt, kb=kb: eo.matmul(
            ps[:, 0:512], qt, c.bfd[:, kb:kb + 512], start=True, stop=False),
            after=(puse(c, pn),), inc=False)
        b.do("tensor", lambda eo, ps=ps, qt=qt, kb=kb: eo.matmul(
            ps[:, 512:640], qt, c.bfd[:, kb + 512:kb + 640],
            start=True, stop=True), inc=False)
        ebase = (cq * 4 + j) * 640
        m2 = b.do("tensor", lambda eo, ps=ps, ebase=ebase: eo.matmul(
            ps[:, 0:512], io.onesrow_b, c.wb[0:1, ebase:ebase + 512],
            start=False, stop=True), after=(lm,), inc=True)
        if pend[par] is not None:
            flushT(pend[par])
            pend[par] = None
        b.do("vector", lambda eo, ps=ps: eo.tensor_add(
            ps[:, 0:128], ps[:, 0:128], io.winm[:, 0:128]), after=(m2,))
        b.do("vector", lambda eo, ps=ps: eo.tensor_add(
            ps[:, 512:640], ps[:, 512:640], io.winm[:, 128:256]))
        b.do("vector", lambda eo, ps=ps, negm=negm: eo.reduce_max(
            negm, ps[:, 0:640], axis=AX.X, negate=True))
        tm = b.do("vector", lambda eo, h=h, negm=negm: eo.tensor_single_scalar(
            negm, negm, c.sc[:, 32 + h:33 + h], ALU.min), inc=True)
        ex = b.do("scalar", lambda eo, ps=ps, probs=probs, negm=negm, rsum=rsum:
                  eo.activation(probs, ps[:, 0:640], AF.Exp, bias=negm,
                                accum_out=rsum),
                  after=(tm, state["norm"][par]), inc=True)
        prel(c, pn, ex)
        sx = b.do("scalar", lambda eo, h=h, ske=ske, negm=negm: eo.activation(
            ske, negm, AF.Exp, bias=c.sc[:, 16 + h:17 + h]), inc=True)
        if v2p is not None:
            do_v2(v2p)
        v2p = (ti, cq, h, j, qt, probs, probn, rsum, ske, rden, ex, sx, par)
    do_v2(v2p)
    for par in range(2):
        if pend[par] is not None:
            flushT(pend[par])
    b.barrier()


def _mk_flushT(c, io, state):
    b = c.b
    VP = 6144

    def flushT(pend):
        ti, cq, h, j, qt, nt, probn = pend
        tsl = ti % 2
        pcn = ("pC", "pD")[tsl]
        pvn = ("pE", "pF")[tsl]
        pvp = getattr(c, pvn)
        ptsb = c.bfc[:, 4096 + tsl * 640:4096 + tsl * 640 + 640]
        tts = None
        for st in range(5):
            pt = getattr(c, pcn)[:].bitcast(BF16)
            tts = b.do("tensor", lambda eo, st=st, pt=pt, probn=probn:
                       eo.transpose(pt[:, st * 128:(st + 1) * 128],
                                    probn[:, st * 128:(st + 1) * 128],
                                    io.identb),
                       after=(nt, state["cp"][tsl]) if st == 0 else (),
                       inc=(st == 4))
        state["T"][tsl] = tts
        cp = b.do("scalar", lambda eo, pcn=pcn, ptsb=ptsb: eo.copy(
            ptsb, getattr(c, pcn)[:].bitcast(BF16)[:, 0:640]),
            after=(tts, state["pv"][tsl]), inc=True)
        state["cp"][tsl] = cp
        prel(c, pcn, cp)
        pv = None
        for st in range(5):
            tile = cq * 8 + j + st
            stt_, spp = st == 0, st == 4
            aft = [cp, puse(c, pvn)] if stt_ else []
            pv = b.do("tensor", lambda eo, st=st, tile=tile, stt_=stt_, spp=spp,
                      pvp=pvp, ptsb=ptsb: eo.matmul(
                          pvp[:, 0:128],
                          c.bfd[:, VP + tile * 128:VP + tile * 128 + 128],
                          ptsb[:, st * 128:(st + 1) * 128],
                          start=stt_, stop=spp), after=aft, inc=spp)
        av = b.do("vector", lambda eo, qt=qt, pvp=pvp: eo.tensor_copy(
            qt, pvp[:, 0:128]), after=(pv,), inc=True)
        state["pv"][tsl] = av
        prel(c, pvn, av)
    return flushT


def emit_l0_out(c, io):
    b = c.b
    wdb = DBuf(c, "wo")
    w1r = [io.w1[:][g].rearrange("(kc p) m -> p kc m", p=128) for g in range(4)]
    w2r = io.w2[:].rearrange("(r p) m -> p r m", p=128)
    prev_w2 = None
    for g in range(4):
        mid_evs = [None] * 16
        for so in range(4):
            sl = wdb.next()
            dst = c.wb[:, sl * 2048:sl * 2048 + 2048].rearrange(
                "p (kc m) -> p kc m", m=512)
            ld = wdb.load(sl, dst, w1r[g][:, :, so * 512:(so + 1) * 512])
            lastmm = None
            for oc4 in range(4):
                oc = so * 4 + oc4
                pn = "pA" if oc % 2 == 0 else "pB"
                ps = getattr(c, pn)
                toks = []
                for kc in range(4):
                    for nt in range(2):
                        st, sp = kc == 0, kc == 3
                        aft = [ld, puse(c, pn)] if st and nt == 0 else []
                        mm = b.do("tensor", lambda eo, sl=sl, kc=kc, oc4=oc4, g=g,
                                  nt=nt, st=st, sp=sp, ps=ps: eo.matmul(
                                      ps[:, nt * 512:nt * 512 + 512],
                                      c.wb[:, sl * 2048 + kc * 512 + oc4 * 128:
                                           sl * 2048 + kc * 512 + oc4 * 128 + 128],
                                      c.bfb[:, (4 * g + kc) * 1024 + nt * 512:
                                            (4 * g + kc) * 1024 + nt * 512 + 512],
                                      start=st, stop=sp), after=aft, inc=sp)
                        if sp:
                            toks.append(mm)
                lastmm = toks[-1]
                ev = b.do("vector", lambda eo, ps=ps, oc=oc: eo.tensor_copy(
                    c.bfc[:, oc * 1024:(oc + 1) * 1024], ps[:]),
                    after=toks + [prev_w2], inc=True)
                prel(c, pn, ev)
                mid_evs[oc] = ev
            wdb.done(sl, lastmm)
        for mc in range(16):
            sl = wdb.next()
            dst = c.wb[:, sl * 2048:sl * 2048 + 2048].rearrange(
                "p (kc m) -> p kc m", m=128)
            ld = wdb.load(sl, dst, w2r[:, g * 16:(g + 1) * 16,
                                       mc * 128:(mc + 1) * 128])
            pn = "pA" if mc % 2 == 0 else "pB"
            ps = getattr(c, pn)
            toks = []
            for kc in range(16):
                for nt in range(2):
                    st, sp = kc == 0, kc == 15
                    aft = ([ld, puse(c, pn)] if st and nt == 0 else []) + \
                        [mid_evs[kc]]
                    mm = b.do("tensor", lambda eo, sl=sl, kc=kc, nt=nt, st=st,
                              sp=sp, ps=ps: eo.matmul(
                                  ps[:, nt * 512:nt * 512 + 512],
                                  c.wb[:, sl * 2048 + kc * 128:sl * 2048 + (kc + 1) * 128],
                                  c.bfc[:, kc * 1024 + nt * 512:kc * 1024 + nt * 512 + 512],
                                  start=st, stop=sp), after=aft, inc=sp)
                    if sp:
                        toks.append(mm)
            prev_w2 = toks[-1]
            ad = b.do("vector", lambda eo, ps=ps, mc=mc: eo.tensor_add(
                c.x[:, mc * 1024:(mc + 1) * 1024],
                c.x[:, mc * 1024:(mc + 1) * 1024], ps[:]), after=toks, inc=True)
            prel(c, pn, ad)
            wdb.done(sl, ad)
    b.barrier()


def decl_l0_inputs(nc, io):
    DT = nc.dram_tensor
    io.wqc = DT("wqc", [D, 512], F32, kind="ExternalInput")
    io.wqu = DT("wqu", [512, D], F32, kind="ExternalInput")
    io.wk = DT("wk", [D, HD], F32, kind="ExternalInput")
    io.wv = DT("wv", [D, HD], F32, kind="ExternalInput")
    io.w1 = DT("w1", [4, 512, D], F32, kind="ExternalInput")
    io.w2 = DT("w2", [4 * D, D], F32, kind="ExternalInput")
    io.ckd = DT("ckd", [128, LOC], BF16, kind="ExternalInput")
    io.skd = DT("skd", [128, LOC], BF16, kind="ExternalInput")
    io.cqd = DT("cqd", [128, OWN], BF16, kind="ExternalInput")
    io.sqd = DT("sqd", [128, OWN], BF16, kind="ExternalInput")
    io.erow = DT("erow", [1, 5120], BF16, kind="ExternalInput")
    io.winm = _sb_const(nc, "winm", [128, 256], BF16)
    io.rperm = _sb_const(nc, "rperm", [128, 128], BF16)
    io.onescol_b = _sb_const(nc, "onescol_b", [128, 1], BF16)
    io.onesrow_b = _sb_const(nc, "onesrow_b", [1, 128], BF16)
    io.onesrow_f = _sb_const(nc, "onesrow_f", [1, 128], F32)
    io.qnw_row = _sb_const(nc, "qnw_row", [1, 128], F32, BF16)
    io.knw_row = _sb_const(nc, "knw_row", [1, 128], F32, BF16)
    io.sink_row = _sb_const(nc, "sink_row", [1, 16], F32)


def build_l0_test():
    nc = bass.Bass("TRN2", target_bir_lowering=False, debug=False)
    bld = Bld(nc)
    io = IO()
    io.h_in = nc.dram_tensor("h_in", [LOC, D], F32, kind="ExternalInput")
    decl_common_inputs(nc, io)
    decl_l0_inputs(nc, io)
    o_x = nc.dram_tensor("o_x", [D, OWN], F32, kind="ExternalOutput")
    c = alloc_common(nc, bld)
    for v in (1e-5, 1.0 / 128.0, 1e-6, 1.28e-4):
        reg_const(c, v)
    emit_const_init(c)
    emit_const_loads(c)
    bld.barrier()
    emit_p0(c, io)
    emit_l0_kvq(c, io)
    emit_l0_scores(c, io)
    emit_l0_out(c, io)
    b = bld
    for kc in range(16):
        t = b.do("sync", lambda eo, kc=kc: eo.dma_start(
            o_x[:].rearrange("(k p) t -> k p t", p=128)[kc],
            c.x[:, kc * 1024:(kc + 1) * 1024]), dma=b.S("o0"))
    b.do("sync", lambda eo, t=t: eo.nop(), after=(t,))
    bld.finalize()
    return nc


def emit_moe(c, io):
    b = c.b
    lrt = b.do("gpsimd", lambda eo: eo.dma_start(
        c.bfd[:, 0:128].rearrange("p (kc m) -> p kc m", m=8),
        io.routerT[:].rearrange("(kc p) m -> p kc m", p=128)), dma=b.S("wrt"))
    comb_toks = []
    e_ = c.sc[:, 64:72]
    negmx = c.sc[:, 72:73]
    m1 = c.sc[:, 73:74]
    t2c = c.sc[:, 74:82]
    m2 = c.sc[:, 82:83]
    wsum = c.sc[:, 83:84]
    for tt in range(8):
        mm = None
        for kc in range(16):
            st, sp = kc == 0, kc == 15
            aft = [lrt, puse(c, "pC")] if st else []
            mm = b.do("tensor", lambda eo, kc=kc, st=st, sp=sp, tt=tt: eo.matmul(
                c.pC[:, 0:8],
                c.bfa[:, kc * 1024 + tt * 128:kc * 1024 + tt * 128 + 128],
                c.bfd[:, kc * 8:(kc + 1) * 8], start=st, stop=sp),
                after=aft, inc=sp)
        rm = b.do("vector", lambda eo: eo.reduce_max(negmx, c.pC[:, 0:8], axis=AX.X,
             negate=True), after=(mm,), inc=True)
        ex = b.do("scalar", lambda eo: eo.activation(e_, c.pC[:, 0:8], AF.Exp,
                  bias=negmx), after=(rm,), inc=True)
        prel(c, "pC", ex)
        b.do("vector", lambda eo: eo.reduce_max(m1, e_, axis=AX.X), after=(ex,))
        b.do("vector", lambda eo: eo.tensor_single_scalar(t2c, e_, m1, ALU.is_ge))
        b.do("vector", lambda eo: eo.scalar_tensor_tensor(
            t2c, t2c, -2.0, e_, ALU.mult, ALU.add))
        b.do("vector", lambda eo: eo.reduce_max(m2, t2c, axis=AX.X))
        b.do("vector", lambda eo: eo.tensor_single_scalar(t2c, e_, m2, ALU.is_ge))
        b.do("vector", lambda eo: eo.tensor_mul(t2c, t2c, e_))
        b.do("vector", lambda eo: eo.reduce_sum(wsum, t2c, axis=AX.X))
        b.do("vector", lambda eo: eo.reciprocal(wsum, wsum))
        comb_toks.append(b.do("vector", lambda eo, tt=tt: eo.tensor_scalar(
            c.fs[:, tt * 8:tt * 8 + 8], t2c, wsum, None, ALU.mult), inc=True))
    lsg = b.do("gpsimd", lambda eo: eo.dma_start(
        c.bfd[:, 128:144].rearrange("p (kc one) -> p kc one", one=1),
        io.sgT[:].rearrange("(kc p) one -> p kc one", p=128),
        allow_slow_non_contiguous=True), dma=b.S("wsg"))
    sgrow = c.fs[0:1, 1024:2048]
    sg_toks = []
    for nt in range(2):
        pn = "pE" if nt == 0 else "pF"
        ps = getattr(c, pn)
        mm = None
        for kc in range(16):
            st, sp = kc == 0, kc == 15
            aft = [lsg, puse(c, pn)] if st else []
            mm = b.do("tensor", lambda eo, kc=kc, st=st, sp=sp, nt=nt, ps=ps:
                      eo.matmul(ps[0:1, 0:512], c.bfd[:, 128 + kc:129 + kc],
                                c.bfa[:, kc * 1024 + nt * 512:kc * 1024 + nt * 512 + 512],
                                start=st, stop=sp), after=aft, inc=sp)
        sg = b.do("scalar", lambda eo, nt=nt, ps=ps: eo.activation(
            sgrow[0:1, nt * 512:nt * 512 + 512], ps[0:1, 0:512], AF.Sigmoid),
            after=(mm,), inc=True)
        prel(c, pn, sg)
        sg_toks.append(sg)
    sgb = c.fs[:].bitcast(BF16)[0:1, 1024:2048]
    scb = b.do("vector", lambda eo: eo.tensor_copy(sgb, sgrow), after=sg_toks,
               inc=True)
    for nt in range(2):
        pn = "pE" if nt == 0 else "pF"
        ps = getattr(c, pn)
        mm = b.do("tensor", lambda eo, nt=nt, ps=ps: eo.matmul(
            ps[:, 0:512], io.onesrow_b, sgb[0:1, nt * 512:nt * 512 + 512],
            start=True, stop=True), after=(scb, puse(c, pn)), inc=True)
        row_war = mm
        ev = b.do("vector", lambda eo, nt=nt, ps=ps: eo.tensor_copy(
            c.bfd[:, 3072 + nt * 512:3072 + nt * 512 + 512], ps[:, 0:512]),
            after=(mm,), inc=True)
        prel(c, pn, ev)

    wg = DBuf(c, "wg", nslots=4)
    wd = DBuf(c, "wd")

    def gu_mm(slot_ld, sl, oc_in_slab, psn, xcols=1024):
        ps = getattr(c, psn)
        toks = []
        for kc in range(16):
            for nt in range(xcols // 512):
                st, sp = kc == 0, kc == 15
                aft = [slot_ld, puse(c, psn)] if st and nt == 0 else []
                mm = b.do("tensor", lambda eo, kc=kc, st=st, sp=sp, nt=nt, sl=sl,
                          oc_in_slab=oc_in_slab, ps=ps: eo.matmul(
                              ps[:, nt * 512:nt * 512 + 512],
                              c.wb[:, sl * 2048 + kc * 128:sl * 2048 + (kc + 1) * 128],
                              c.bfa[:, kc * 1024 + nt * 512:kc * 1024 + nt * 512 + 512],
                              start=st, stop=sp), after=aft, inc=sp)
                if sp:
                    toks.append(mm)
        return toks

    prev_av = None
    shg_r = io.shgT[:].rearrange("(kc p) m -> p kc m", p=128)
    shu_r = io.shuT[:].rearrange("(kc p) m -> p kc m", p=128)
    shd_r = io.shdT[:].rearrange("(r p) m -> p r m", p=128)
    last_down = None
    def gu_load(wsrc_r, gc):
        sl_ = wg.next()
        ld_ = wg.load(sl_, wslab_dst(c, (c.wb, sl_ * 2048), 16),
                      wsrc_r[:, :, gc * 128:(gc + 1) * 128])
        return sl_, ld_

    for iq in range(4):
        noc = 11
        act_toks = [None] * 11
        pre = {0: (gu_load(shg_r, iq * 11), gu_load(shu_r, iq * 11))}
        for oc in range(noc):
            gc = iq * 11 + oc
            (slg, ldg), (slu, ldu) = pre.pop(oc)
            if oc + 1 < noc:
                pre[oc + 1] = (gu_load(shg_r, gc + 1), gu_load(shu_r, gc + 1))
            gt = gu_mm(ldg, slg, 0, "pA")
            ut = gu_mm(ldu, slu, 0, "pB")
            sl_ = c.bfd[:, 1024:2048]
            sv = b.do("scalar", lambda eo, sl_=sl_: eo.activation(
                sl_, c.pA[:], AF.Silu), after=gt + [prev_av], inc=True)
            prel(c, "pA", sv)
            wg.done(slg, sv)
            av = b.do("vector", lambda eo, sl_=sl_, oc=oc: eo.tensor_mul(
                c.bfc[:, oc * 1024:(oc + 1) * 1024], sl_, c.pB[:]),
                after=ut + [sv, last_down], inc=True)
            prel(c, "pB", av)
            wg.done(slu, av)
            act_toks[oc] = av
            prev_av = av
        for hc in range(16):
            sl2 = wd.next()
            ld2 = wd.load(sl2, c.bfd[:, 4096 + sl2 * 1408:4096 + sl2 * 1408 + 1408]
                          .rearrange("p (kc m) -> p kc m", m=128),
                          shd_r[:, iq * 11:(iq + 1) * 11, hc * 128:(hc + 1) * 128])
            dtoks = {}
            for kc in range(11):
                for nt in range(2):
                    pn = "pC" if nt == 0 else "pD"
                    ps = getattr(c, pn)
                    st, sp = kc == 0, kc == 10
                    aft = ([ld2, puse(c, pn)] if st else []) + [act_toks[kc]]
                    mm = b.do("tensor", lambda eo, kc=kc, st=st, sp=sp, nt=nt,
                              sl2=sl2, ps=ps: eo.matmul(
                                  ps[:, 0:512],
                                  c.bfd[:, 4096 + sl2 * 1408 + kc * 128:
                                       4096 + sl2 * 1408 + (kc + 1) * 128],
                                  c.bfc[:, kc * 1024 + nt * 512:kc * 1024 + nt * 512 + 512],
                                  start=st, stop=sp), after=aft, inc=sp)
                    if sp:
                        dtoks[nt] = mm
            for nt in range(2):
                pn = "pC" if nt == 0 else "pD"
                ps = getattr(c, pn)
                mm = dtoks[nt]
                tv = b.do("vector", lambda eo, nt=nt, hc=hc, ps=ps: eo.tensor_mul(
                    c.bfd[:].bitcast(F32)[:, 3456:3968],
                    ps[:, 0:512], c.bfd[:, 3072 + nt * 512:3072 + nt * 512 + 512]),
                    after=(mm,), inc=True)
                prel(c, pn, tv)
                if nt == 1:
                    wd.done(sl2, tv)
                b.do("vector", lambda eo, nt=nt, hc=hc: eo.tensor_add(
                    c.x[:, hc * 1024 + nt * 512:hc * 1024 + nt * 512 + 512],
                    c.x[:, hc * 1024 + nt * 512:hc * 1024 + nt * 512 + 512],
                    c.bfd[:].bitcast(F32)[:, 3456:3968]))
                if nt == 1 and hc == 15:
                    last_down = mm
    gup_r = [io.gupT[:][e].rearrange("(kc p) m -> p kc m", p=128) for e in range(NE)]
    dp_r = [io.dpT[:][e].rearrange("(r p) m -> p r m", p=128) for e in range(NE)]
    last_pool = None
    row_war = locals().get("row_war")
    for e in range(NE):
        comb_evs = []
        pool_toks = [None] * 11
        rowp = c.fs[0:1, 1024:2048]
        rowb = c.fs[:].bitcast(BF16)[0:1, 1024:2048]
        tts = []
        for tt in range(8):
            tp = b.do("tensor", lambda eo, tt=tt, e=e: eo.transpose(
                c.pE[0:1, 0:128], c.fs[:, tt * 8 + e:tt * 8 + e + 1],
                io.identf), after=[puse(c, "pE"), comb_toks[tt]], inc=True)
            ev = b.do("vector", lambda eo, tt=tt: eo.tensor_copy(
                rowp[0:1, tt * 128:(tt + 1) * 128], c.pE[0:1, 0:128]),
                after=(tp,), inc=True)
            prel(c, "pE", ev)
        cb = b.do("vector", lambda eo: eo.tensor_copy(rowb, rowp),
                  after=[row_war], inc=True)
        for nt in range(2):
            pn = "pE" if nt == 0 else "pF"
            ps = getattr(c, pn)
            mm = b.do("tensor", lambda eo, nt=nt, ps=ps: eo.matmul(
                ps[:, 0:512], io.onesrow_b, rowb[0:1, nt * 512:nt * 512 + 512],
                start=True, stop=True), after=(cb, puse(c, pn)), inc=True)
            row_war = mm
            ev = b.do("vector", lambda eo, nt=nt, ps=ps: eo.tensor_copy(
                c.bfd[:, 2048 + nt * 512:2048 + nt * 512 + 512], ps[:, 0:512]),
                after=(mm, last_pool), inc=True)
            prel(c, pn, ev)
            comb_evs.append(ev)
        pre = {0: (gu_load(gup_r[e], 0), gu_load(gup_r[e], 11))}
        for oc in range(11):
            (slg, ldg), (slu, ldu) = pre.pop(oc)
            if oc + 1 < 11:
                pre[oc + 1] = (gu_load(gup_r[e], oc + 1),
                               gu_load(gup_r[e], 12 + oc))
            gt = gu_mm(ldg, slg, 0, "pA")
            ut = gu_mm(ldu, slu, 0, "pB")
            sl_ = c.bfd[:, 1024:2048]
            sv = b.do("scalar", lambda eo, sl_=sl_: eo.activation(
                sl_, c.pA[:], AF.Silu), after=gt + [prev_av], inc=True)
            prel(c, "pA", sv)
            wg.done(slg, sv)
            av = b.do("vector", lambda eo, sl_=sl_, oc=oc: eo.tensor_mul(
                c.bfc[:, oc * 1024:(oc + 1) * 1024], sl_, c.pB[:]),
                after=ut + [sv, last_down], inc=True)
            prel(c, "pB", av)
            wg.done(slu, av)
            prev_av = av
            pool_toks[oc] = b.do("vector", lambda eo, oc=oc: eo.tensor_mul(
                c.bfc[:, oc * 1024:(oc + 1) * 1024],
                c.bfc[:, oc * 1024:(oc + 1) * 1024],
                c.bfd[:, 2048:3072]), after=[av] + comb_evs, inc=True)
            last_pool = pool_toks[oc]
        for hc in range(16):
            sl2 = wd.next()
            ld2 = wd.load(sl2, c.bfd[:, 4096 + sl2 * 1408:4096 + sl2 * 1408 + 1408]
                          .rearrange("p (kc m) -> p kc m", m=128),
                          dp_r[e][:, :, hc * 128:(hc + 1) * 128])
            dtoks = {}
            for kc in range(11):
                for nt in range(2):
                    pn = "pC" if nt == 0 else "pD"
                    ps = getattr(c, pn)
                    st, sp = kc == 0, kc == 10
                    aft = ([ld2, puse(c, pn)] if st else []) + [pool_toks[kc]]
                    mm = b.do("tensor", lambda eo, kc=kc, st=st, sp=sp, nt=nt,
                              sl2=sl2, ps=ps: eo.matmul(
                                  ps[:, 0:512],
                                  c.bfd[:, 4096 + sl2 * 1408 + kc * 128:
                                       4096 + sl2 * 1408 + (kc + 1) * 128],
                                  c.bfc[:, kc * 1024 + nt * 512:kc * 1024 + nt * 512 + 512],
                                  start=st, stop=sp), after=aft, inc=sp)
                    if sp:
                        dtoks[nt] = mm
            for nt in range(2):
                pn = "pC" if nt == 0 else "pD"
                ps = getattr(c, pn)
                mm = dtoks[nt]
                ad = b.do("vector", lambda eo, nt=nt, hc=hc, ps=ps: eo.tensor_add(
                    c.x[:, hc * 1024 + nt * 512:hc * 1024 + nt * 512 + 512],
                    c.x[:, hc * 1024 + nt * 512:hc * 1024 + nt * 512 + 512],
                    ps[:, 0:512]), after=(mm,), inc=True)
                prel(c, pn, ad)
                if nt == 1:
                    wd.done(sl2, ad)
                    if hc == 15:
                        last_down = mm
    b.barrier()


def decl_moe_inputs(nc, io):
    DT = nc.dram_tensor
    io.routerT = DT("routerT", [D, 8], F32, kind="ExternalInput")
    io.gupT = DT("gupT", [NE, D, 2 * IMOE], F32, kind="ExternalInput")
    io.dpT = DT("dpT", [NE, IMOE, D], F32, kind="ExternalInput")
    io.shgT = DT("shgT", [D, ISH], F32, kind="ExternalInput")
    io.shuT = DT("shuT", [D, ISH], F32, kind="ExternalInput")
    io.shdT = DT("shdT", [ISH, D], F32, kind="ExternalInput")
    io.sgT = DT("sgT", [D, 1], F32, kind="ExternalInput")


def build_layer0_test():
    nc = bass.Bass("TRN2", target_bir_lowering=False, debug=False)
    bld = Bld(nc)
    io = IO()
    io.h_in = nc.dram_tensor("h_in", [LOC, D], F32, kind="ExternalInput")
    decl_common_inputs(nc, io)
    decl_l0_inputs(nc, io)
    decl_moe_inputs(nc, io)
    o_x = nc.dram_tensor("o_x", [D, OWN], F32, kind="ExternalOutput")
    c = alloc_common(nc, bld)
    for v in (1e-5, 1.0 / 128.0, 1e-6, 1.28e-4):
        reg_const(c, v)
    emit_const_init(c)
    emit_const_loads(c)
    bld.barrier()
    emit_p0(c, io)
    emit_l0_kvq(c, io)
    emit_l0_scores(c, io)
    emit_l0_out(c, io)
    emit_moe(c, io)
    b = bld
    for kc in range(16):
        t = b.do("sync", lambda eo, kc=kc: eo.dma_start(
            o_x[:].rearrange("(k p) t -> k p t", p=128)[kc],
            c.x[:, kc * 1024:(kc + 1) * 1024]), dma=b.S("o0"))
    b.do("sync", lambda eo, t=t: eo.nop(), after=(t,))
    bld.finalize()
    return nc


def row_bcast_exact(c, psum_ap, row_f32, hi_b, lo_b, after=(), pn=None):
    b = c.b
    io = c.io
    t1 = b.do("vector", lambda eo: eo.tensor_copy(hi_b, row_f32), after=after,
              inc=True)
    t2 = b.do("vector", lambda eo: eo.tensor_tensor(lo_b, row_f32, hi_b,
              ALU.subtract), inc=True)
    n = row_f32.shape[1]
    m1 = b.do("tensor", lambda eo: eo.matmul(psum_ap, io.onesrow_b, hi_b,
              start=True, stop=False), after=(t1, t2, puse(c, pn)), inc=True)
    m2 = b.do("tensor", lambda eo: eo.matmul(psum_ap, io.onesrow_b, lo_b,
              start=False, stop=True), inc=True)
    return m2


def emit_ln1(c, io):
    b = c.b
    reg_const(c, 1e-5)
    bfcF = c.bfc[:].bitcast(F32)
    xsq = bfcF[:, 0:1024]
    prev_sq = None
    toks = []
    for kc in range(16):
        sq = b.do("scalar", lambda eo, kc=kc: eo.activation(
            xsq, c.x[:, kc * 1024:(kc + 1) * 1024], AF.Square),
            after=[prev_sq], inc=True)
        for nt in range(2):
            pnm = "pC" if nt == 0 else "pD"
            ps = getattr(c, pnm)
            aft = [puse(c, pnm)] if kc == 0 else []
            t = b.do("tensor", lambda eo, kc=kc, nt=nt, ps=ps: eo.matmul(
                ps[0:1, 0:512], io.onescol_f,
                c.x[:, kc * 1024 + nt * 512:kc * 1024 + nt * 512 + 512],
                start=kc == 0, stop=kc == 15), after=aft, inc=kc == 15)
            if t:
                toks.append(t)
            pnm2 = "pE" if nt == 0 else "pF"
            ps2 = getattr(c, pnm2)
            aft = ([puse(c, pnm2)] if kc == 0 else []) + [sq]
            t2_ = b.do("tensor", lambda eo, kc=kc, nt=nt, ps2=ps2: eo.matmul(
                ps2[0:1, 0:512], io.onescol_f,
                xsq[:, nt * 512:nt * 512 + 512],
                start=kc == 0, stop=kc == 15), after=aft, inc=True)
            if kc == 15:
                toks.append(t2_)
            if nt == 1:
                prev_sq = t2_
    mean = c.fs[0:1, 0:1024]
    v2 = c.fs[0:1, 1024:2048]
    tmp = bfcF[0:1, 1024:2048]
    for nt in range(2):
        psm = (c.pC if nt == 0 else c.pD)[0:1, 0:512]
        psq = (c.pE if nt == 0 else c.pF)[0:1, 0:512]
        b.do("vector", lambda eo, nt=nt, psm=psm: eo.tensor_scalar(
            mean[0:1, nt * 512:nt * 512 + 512], psm, 1.0 / 2048.0, None,
            ALU.mult), after=toks)
        b.do("vector", lambda eo, nt=nt, psq=psq: eo.tensor_scalar(
            v2[0:1, nt * 512:nt * 512 + 512], psq, 1.0 / 2048.0, None,
            ALU.mult), after=toks)
    t1 = b.do("vector", lambda eo: eo.tensor_mul(tmp, mean, mean), inc=True)
    t2 = b.do("vector", lambda eo: eo.tensor_tensor(v2, v2, tmp, ALU.subtract),
              inc=True)
    t3 = b.do("scalar", lambda eo: eo.activation(tmp, v2, AF.Sqrt, bias=1e-5),
              after=(t2,), inc=True)
    t4 = b.do("vector", lambda eo: eo.reciprocal(v2, tmp), after=(t3,), inc=True)
    t5 = b.do("vector", lambda eo: eo.scalar_tensor_tensor(
        tmp, mean, -1.0, v2, ALU.mult, ALU.mult), inc=True)
    hi_b = c.bfc[0:1, 4096:5120]
    lo_b = c.bfc[0:1, 5120:6144]
    mma = []
    for nt in range(2):
        mma.append(row_bcast_exact(
            c, c.pA[:, nt * 512:nt * 512 + 512],
            v2[0:1, nt * 512:nt * 512 + 512],
            hi_b[0:1, nt * 512:nt * 512 + 512],
            lo_b[0:1, nt * 512:nt * 512 + 512],
            after=(t4,), pn="pA" if nt == 0 else None))
    hi_b2 = c.bfc[0:1, 6144:7168]
    lo_b2 = c.bfc[0:1, 7168:8192]
    for nt in range(2):
        mma.append(row_bcast_exact(
            c, c.pB[:, nt * 512:nt * 512 + 512],
            tmp[0:1, nt * 512:nt * 512 + 512],
            hi_b2[0:1, nt * 512:nt * 512 + 512],
            lo_b2[0:1, nt * 512:nt * 512 + 512],
            after=(t5,), pn="pB" if nt == 0 else None))
    t32 = bfcF[:, 2048 - 1024:2048]
    prev_t = None
    for kc in range(16):
        tt1 = b.do("vector", lambda eo, kc=kc: eo.tensor_mul(
            t32, c.x[:, kc * 1024:(kc + 1) * 1024], c.pA[:]),
            after=mma + [prev_t], inc=True)
        tt2 = b.do("vector", lambda eo, kc=kc: eo.tensor_tensor(
            c.bfa[:, kc * 1024:(kc + 1) * 1024], t32, c.pB[:], ALU.add),
            inc=True)
        prev_t = tt2
    prel(c, "pA", prev_t)
    prel(c, "pB", prev_t)
    b.barrier()


def decl_l1_inputs(nc, io, mode):
    DT = nc.dram_tensor
    io.wproj = DT("wproj", [D, D], F32, kind="ExternalInput")
    io.bproj = _sb_const(nc, "bproj", [128, 16], F32)
    io.qgT = DT("qgT", [D, IFF], I32, kind="ExternalInput")
    io.quT = DT("quT", [D, IFF], I32, kind="ExternalInput")
    io.qdT = DT("qdT", [IFF, D], I32, kind="ExternalInput")
    io.gscol = _sb_const(nc, "gscol", [128, 64], F32)
    io.uscol = _sb_const(nc, "uscol", [128, 64], F32)
    io.dscol = _sb_const(nc, "dscol", [128, 16], F32)
    io.onescol_f = _sb_const(nc, "onescol_f", [128, 1], F32)
    io.ones128 = _sb_const(nc, "ones128", [128, 128], BF16)
    io.tri = _sb_const(nc, "tri", [128, 128], BF16)
    io.bmat = _sb_const(nc, "bmat", [128, 64], F32)
    io.f1m = _sb_const(nc, "f1m", [128, 16], F32)
    io.f1g = _sb_const(nc, "f1g", [128, 64], F32)
    io.aprd = DT("aprd", [128, 1024], F32, kind="ExternalInput")


def emit_l1_qkv(c, io, kv_dram, kick=None, kick_k=None):
    b = c.b
    wdb = DBuf(c, "wq")
    sdb = DBuf(c, "kvs", queue="sync")
    for mc in list(range(16, 48)) + list(range(16)):
        sl = wdb.next()
        dst = wslab_dst(c, (c.wb, sl * 2048), 16)
        ld = wdb.load(sl, dst, wslab_src(io.wqkv, mc))
        pn = "pA" if mc % 2 == 0 else "pB"
        ps = getattr(c, pn)
        toks = []
        for kc in range(16):
            for nt in range(2):
                st, sp = kc == 0, kc == 15
                aft = [ld, puse(c, pn)] if st and nt == 0 else []
                mm = b.do("tensor", lambda eo, kc=kc, st=st, sp=sp, ps=ps, nt=nt,
                          sl=sl: eo.matmul(
                              ps[:, nt * 512:nt * 512 + 512],
                              c.wb[:, sl * 2048 + kc * 128:sl * 2048 + (kc + 1) * 128],
                              c.bfa[:, kc * 1024 + nt * 512:kc * 1024 + nt * 512 + 512],
                              start=st, stop=sp), after=aft, inc=sp)
                if sp:
                    toks.append(mm)
        bcol = io.bqkv[:, mc:mc + 1]
        if mc < 16:
            ev = b.do("vector", lambda eo, ps=ps, mc=mc, bcol=bcol: eo.tensor_scalar(
                c.bfb[:, mc * 1024:(mc + 1) * 1024], ps[:], bcol, 1.0 / 128.0,
                ALU.add, ALU.mult), after=toks, inc=True)
            prel(c, pn, ev)
            wdb.done(sl, ev)
        elif mc < 32:
            kc_ = mc - 16
            ssl = sdb.next()
            stg = c.bfd[:, ssl * 1024:(ssl + 1) * 1024]
            ev = b.do("vector", lambda eo, ps=ps, bcol=bcol, stg=stg:
                      eo.tensor_scalar(stg, ps[:], bcol, None, ALU.add),
                      after=toks + [sdb.cons[ssl]], inc=True)
            prel(c, pn, ev)
            wdb.done(sl, ev)
            st_ = sdb.load(ssl, kv_dram[kc_ * 128:(kc_ + 1) * 128, :], stg,
                           after=(ev,))
            sdb.done(ssl, st_)
            if mc == 31 and kick_k is not None:
                kick_k([(sm, sm.n) for sm in sdb.sems])
        else:
            mcv = mc - 32
            ssl = sdb.next()
            stg = c.bfd[:, ssl * 1024:(ssl + 1) * 1024]
            ev = b.do("vector", lambda eo, ps=ps, bcol=bcol, stg=stg:
                      eo.tensor_scalar(stg, ps[:], bcol, None, ALU.add),
                      after=toks + [sdb.cons[ssl]], inc=True)
            prel(c, pn, ev)
            wdb.done(sl, ev)
            vstg = c.bfd[:, 2048 + ssl * 1024:2048 + (ssl + 1) * 1024]
            evs = None
            for tt in range(8):
                pn2 = "pE" if tt % 2 == 0 else "pF"
                pt = getattr(c, pn2)[:].bitcast(BF16)
                tr = b.do("tensor", lambda eo, tt=tt, stg=stg, pt=pt: eo.transpose(
                    pt[:, 0:128], stg[:, tt * 128:(tt + 1) * 128], io.identb),
                    after=(ev, puse(c, pn2)), inc=True)
                evs = b.do("vector", lambda eo, tt=tt, pt=pt, vstg=vstg:
                           eo.tensor_copy(vstg[:, tt * 128:(tt + 1) * 128],
                                          pt[:, 0:128]), after=(tr,), inc=True)
                prel(c, pn2, evs)
            dst2 = dap(kv_dram, 2048 * 1024 + mcv * 128,
                       [(2048, 128), (128 * 2048, 8), (1, 128)])
            st_ = sdb.load(ssl, dst2,
                           vstg[:].rearrange("p (t f) -> p t f", f=128),
                           after=(evs,))
            sdb.done(ssl, st_)
        if mc == 47 and kick is not None:
            kick([(sm, sm.n) for sm in sdb.sems])
    b.barrier()


ALIBI_SLOPES = [2.0 ** (-0.5 * (i + 1)) for i in range(16)]


def emit_l1_attn(c, io, ag_dram, own_dram):
    if isinstance(ag_dram, tuple):
        agk, agv = ag_dram
        ksrc_fn = lambda h, cq: agk[:].rearrange("(r x) cc -> x r cc", x=2048)[
            h * 128:(h + 1) * 128, :, cq * 512:(cq + 1) * 512]
        voff_fn = lambda r, cq, h: r * 2048 * 1024 + cq * 512 * 2048 + h * 128
        vhandle = agv
    else:
        agk = agv = ag_dram
        ksrc_fn = lambda h, cq: ag_dram[:].rearrange("(r x) cc -> x r cc",
                                                     x=4096)[
            h * 128:(h + 1) * 128, :, cq * 512:(cq + 1) * 512]
        voff_fn = lambda r, cq, h: (r * 4096 + 2048) * 1024 + \
            cq * 512 * 2048 + h * 128
        vhandle = ag_dram
    """Transposed no-max causal+ALiBi attention, software-pipelined:
    PV/den of step i-1 are emitted after QKT of step i so TensorE never
    stalls behind the Exp/scale chain. attn overwrites q in bfb."""
    b = c.b
    la = b.do("sync", lambda eo: eo.dma_start(c.fs[:, 0:1024], io.aprd[:]),
              dma=b.S("apr"))
    kdb = DBuf(c, "l1kv", nslots=2, queue="sync")
    wpar = [None, None, None, None]
    SPS = (("pA", 0, "pA0"), ("pA", 512, "pA1"), ("pB", 0, "pB0"), ("pB", 512, "pB1"))
    for cq in range(2):
        for h in range(16):
            hi5 = h >= 5
            sl = kdb.next()
            kbuf = c.bfc[:, sl * 8192:sl * 8192 + 4096]
            vbuf = c.bfc[:, sl * 8192 + 4096:sl * 8192 + 8192]
            kown = c.bfd[:, 4096 + sl * 1024:4096 + sl * 1024 + 512]
            vown = c.bfd[:, 4096 + sl * 1024 + 512:4096 + sl * 1024 + 1024]
            ksrc = ksrc_fn(h, cq)
            kdb.load(sl, kbuf[:].rearrange("p (r cc) -> p r cc", cc=512), ksrc)
            for r in range(8):
                vsrc = dap(vhandle, voff_fn(r, cq, h),
                           [(2048, 128), (128 * 2048, 4), (1, 128)])
                kdb.load(sl, vbuf[:, r * 512:(r + 1) * 512].rearrange(
                    "p (t f) -> p t f", f=128), vsrc)
            kdb.load(sl, kown, own_dram[h * 128:(h + 1) * 128,
                                        cq * 512:(cq + 1) * 512])
            vsrc2 = dap(own_dram, 2048 * 1024 + cq * 512 * 2048 + h * 128,
                        [(2048, 128), (128 * 2048, 4), (1, 128)])
            ldt = kdb.load(sl, vown[:].rearrange("p (t f) -> p t f", f=128), vsrc2)
            qh = c.bfb[:, h * 1024 + cq * 512:h * 1024 + cq * 512 + 512]
            steps = [("ag", J) for J in range(32)] + \
                    [("lo", Jl) for Jl in (3, 2, 1, 0)]
            pend = None
            pend2 = None
            first = [True]
            pvlast = [None, None]

            def flush(last, pend_=None):
                kind, idx, w, n, tss = (pend_ or pend)[:5]
                st_ = first[0]
                if kind == "ag":
                    lh = vbuf[:, idx * 128:(idx + 1) * 128]
                    co = 0
                else:
                    lh = vown[:, idx * 128:(idx + 1) * 128]
                    co = idx * 128
                aft = [tss, (puse(c, "pE") if st_ else None)]
                pv = b.do("tensor", lambda eo, lh=lh, w=w, n=n, co=co, st_=st_,
                          last=last: eo.matmul(
                              c.pE[:, co:co + n], lh, w[:, 0:n],
                              start=st_, stop=last), after=aft, inc=True)
                dn = b.do("tensor", lambda eo, w=w, n=n, co=co, st_=st_,
                          last=last: eo.matmul(
                              c.pF[:, co:co + n], io.ones128, w[:, 0:n],
                              start=st_, stop=last),
                          after=[puse(c, "pF") if st_ else None], inc=True)
                first[0] = False
                pvlast[0], pvlast[1] = pv, dn
                return dn

            si = 0
            for kind, idx in steps:
                pn, poff, pkey = SPS[si % 4]
                ps = getattr(c, pn)[:, poff:poff + 512]
                wsl = si % 4
                w = c.bfd[:, 6144 + wsl * 512:6144 + wsl * 512 + 512]
                if kind == "ag":
                    n = 512
                    mm = b.do("tensor", lambda eo, idx=idx, ps=ps, kbuf=kbuf,
                              qh=qh: eo.matmul(
                                  ps[:, 0:512], kbuf[:, idx * 128:(idx + 1) * 128],
                                  qh, start=True, stop=True),
                              after=(ldt, puse(c, pkey)), inc=True)
                else:
                    n = 512 - idx * 128
                    mm = b.do("tensor", lambda eo, idx=idx, n=n, ps=ps,
                              kown=kown, qh=qh: eo.matmul(
                                  ps[:, 0:n], kown[:, idx * 128:(idx + 1) * 128],
                                  qh[:, idx * 128:512], start=True, stop=True),
                              after=(ldt, puse(c, pkey)), inc=True)
                if kind == "ag" and si % 2 == 0:
                    held = (kind, idx, w, n, mm, pkey, wsl)
                    si += 1
                    continue
                if pend is not None:
                    wpar[pend[6]] = flush(False)
                    pend = pend2
                    pend2 = None
                if pend is not None:
                    wpar[pend[6]] = flush(False)
                    pend = None
                if kind == "ag":
                    k0, i0, w0, n0, mm0, pk0, ws0 = held
                    wpair = c.bfd[:, 6144 + ws0 * 512:6144 + ws0 * 512 + 1024]
                    pspair = getattr(c, pn)[:]
                    ex = b.do("scalar", lambda eo, wpair=wpair, pspair=pspair:
                              eo.activation(wpair, pspair, AF.Exp),
                              after=(mm, wpar[ws0], wpar[wsl], la), inc=True)
                    prel(c, pk0, ex)
                    prel(c, pkey, ex)
                else:
                    ex = b.do("scalar", lambda eo, ps=ps, w=w, n=n: eo.activation(
                        w[:, 0:n], ps[:, 0:n], AF.Exp),
                        after=(mm, wpar[wsl], la), inc=True)
                    prel(c, pkey, ex)
                if kind == "ag":
                    def do_ts(w_, idx_):
                        if hi5:
                            return b.do("vector", lambda eo, w_=w_, idx_=idx_:
                                        eo.tensor_scalar(
                                            w_, w_,
                                            c.fs[:, (cq * 16 + h) * 32 + idx_:
                                                 (cq * 16 + h) * 32 + idx_ + 1],
                                            None, ALU.mult), after=(ex,),
                                        inc=True)
                        t_ = None
                        for jt in range(4):
                            t_ = b.do("vector", lambda eo, w_=w_, jt=jt,
                                      idx_=idx_: eo.tensor_scalar(
                                          w_[:, jt * 128:(jt + 1) * 128],
                                          w_[:, jt * 128:(jt + 1) * 128],
                                          c.fs[:, (cq * 16 + h) * 32 + idx_:
                                               (cq * 16 + h) * 32 + idx_ + 1],
                                          io.bmat[:, h * 4 + jt:h * 4 + jt + 1],
                                          ALU.mult, ALU.mult),
                                      after=(ex,), inc=True)
                        return t_
                    t0_ = do_ts(w0, i0)
                    t1_ = do_ts(w, idx)
                    pend = (k0, i0, w0, n0, t0_, pk0, ws0)
                    pend2 = ("ag", idx, w, n, t1_, pkey, wsl)
                    si += 1
                    continue
                elif False:
                    pass
                else:
                    if hi5:
                        tss = b.do("vector", lambda eo, w=w, n=n, h=h, idx=idx:
                                   eo.tensor_scalar(
                                       w[:, 0:n], w[:, 0:n],
                                       io.f1g[:, h * 4 + idx:h * 4 + idx + 1],
                                       None, ALU.mult), after=(ex,), inc=True)
                    else:
                        tss = None
                        for jt in range(idx, 4):
                            g = math.exp(min(ALIBI_SLOPES[h] * 128.0 * (idx - jt),
                                             0.0))
                            g = float(np.float32(g))
                            tss = b.do("vector", lambda eo, w=w, jt=jt, idx=idx,
                                       h=h, g=g: eo.tensor_scalar(
                                           w[:, (jt - idx) * 128:(jt - idx + 1) * 128],
                                           w[:, (jt - idx) * 128:(jt - idx + 1) * 128],
                                           io.f1m[:, h:h + 1], g,
                                           ALU.mult, ALU.mult),
                                       after=(ex,), inc=True)
                    tss = b.do("vector", lambda eo, w=w: eo.tensor_mul(
                        w[:, 0:128], w[:, 0:128], io.tri), inc=True)
                pend = (kind, idx, w, n, tss, pkey, wsl)
                si += 1
            if pend2 is not None:
                wpar[pend[6]] = flush(False)
                pend = pend2
            wpar[pend[6]] = flush(True)
            kdb.done(sl, pvlast[1])
            crow = c.fs[0:1, 1024:1536]
            crb = c.fs[:].bitcast(BF16)[0:1, 3072:3584]
            r1 = b.do("scalar", lambda eo: eo.activation(
                crow, c.pF[0:1, 0:512], AF.Ln, scale=2.0 ** -32),
                after=[pvlast[1], puse(c, "crow")], inc=True)
            r1b = b.do("scalar", lambda eo: eo.activation(
                crow, crow, AF.Exp, scale=-1.0, bias=-22.18070977791825),
                after=(r1,), inc=True)
            prel(c, "pF", r1b)
            r2 = b.do("vector", lambda eo: eo.tensor_copy(crb, crow),
                      after=[r1b, puse(c, "crb")], inc=True)
            pnc = "pC"
            cbm = b.do("tensor", lambda eo: eo.matmul(
                c.pC[:, 0:512], io.onesrow_b, crb, start=True, stop=True),
                after=(r2, puse(c, pnc)), inc=True)
            prel(c, "crb", cbm)
            prel(c, "crow", cbm)
            cbs = c.bfd[:].bitcast(F32)[:, 1024:1536]
            r3 = b.do("vector", lambda eo: eo.tensor_copy(cbs, c.pC[:, 0:512]),
                      after=(cbm,), inc=True)
            prel(c, pnc, r3)
            av = b.do("vector", lambda eo, qh=qh: eo.tensor_mul(
                qh, c.pE[:, 0:512], cbs), after=[pvlast[0], r3], inc=True)
            prel(c, "pE", av)
    b.barrier()


def emit_l1_oproj(c, io):
    b = c.b
    wdb = DBuf(c, "wp")
    for mc in range(16):
        sl = wdb.next()
        dst = wslab_dst(c, (c.wb, sl * 2048), 16)
        ld = wdb.load(sl, dst, wslab_src(io.wproj, mc))
        pn = "pA" if mc % 2 == 0 else "pB"
        ps = getattr(c, pn)
        toks = []
        for kc in range(16):
            for nt in range(2):
                st, sp = kc == 0, kc == 15
                aft = [ld, puse(c, pn)] if st and nt == 0 else []
                mm = b.do("tensor", lambda eo, kc=kc, st=st, sp=sp, ps=ps, nt=nt,
                          sl=sl: eo.matmul(
                              ps[:, nt * 512:nt * 512 + 512],
                              c.wb[:, sl * 2048 + kc * 128:sl * 2048 + (kc + 1) * 128],
                              c.bfb[:, kc * 1024 + nt * 512:kc * 1024 + nt * 512 + 512],
                              start=st, stop=sp), after=aft, inc=sp)
                if sp:
                    toks.append(mm)
        ad = b.do("vector", lambda eo, ps=ps, mc=mc: eo.scalar_tensor_tensor(
            c.x[:, mc * 1024:(mc + 1) * 1024], ps[:], io.bproj[:, mc:mc + 1],
            c.x[:, mc * 1024:(mc + 1) * 1024], ALU.add, ALU.add),
            after=toks, inc=True)
        prel(c, pn, ad)
        wdb.done(sl, ad)
    b.barrier()


def emit_l1_ffn(c, io, precast=True):
    b = c.b
    toks = []
    for nt in range(2):
        pnm = "pE" if nt == 0 else "pF"
        ps = getattr(c, pnm)
        mm = None
        for kc in range(16):
            st, sp = kc == 0, kc == 15
            aft = [puse(c, pnm)] if st else []
            mm = b.do("tensor", lambda eo, kc=kc, st=st, sp=sp, nt=nt, ps=ps:
                      eo.matmul(ps[0:1, 0:512], io.onescol_b,
                                c.bfa[:, kc * 1024 + nt * 512:kc * 1024 + nt * 512 + 512],
                                start=st, stop=sp), after=aft, inc=sp)
        toks.append(mm)
    m128x = c.fs[0:1, 0:1024]
    hix = c.fs[:].bitcast(BF16)[0:1, 2048:3072]
    lox = c.fs[:].bitcast(BF16)[0:1, 3072:4096]
    for nt in range(2):
        psr = (c.pE if nt == 0 else c.pF)[0:1, 0:512]
        t = b.do("vector", lambda eo, nt=nt, psr=psr: eo.tensor_scalar(
            m128x[0:1, nt * 512:nt * 512 + 512], psr, -128.0, None, ALU.mult),
            after=toks, inc=True)
        prel(c, "pE" if nt == 0 else "pF", t)
    t1 = b.do("vector", lambda eo: eo.tensor_copy(hix, m128x), inc=True)
    xc = b.do("vector", lambda eo: eo.tensor_tensor(lox, m128x, hix,
              ALU.subtract), inc=True)
    if precast:
        qg_r = io.qgB[:].rearrange("(kc p) m -> p kc m", p=128)
        qu_r = io.quB[:].rearrange("(kc p) m -> p kc m", p=128)
        qd_r = io.qdB[:].rearrange("(r p) m -> p r m", p=128)
        wq_ = "sync"
        pct = getattr(c, "pc_tok", None)
    else:
        qg_r = io.qgT[:].rearrange("(kc p) m -> p kc m", p=128)
        qu_r = io.quT[:].rearrange("(kc p) m -> p kc m", p=128)
        qd_r = io.qdT[:].rearrange("(r p) m -> p r m", p=128)
        wq_ = "gpsimd"
        pct = None
    wg = DBuf(c, "fg", queue=wq_)
    wu = DBuf(c, "fu", queue=wq_)
    wd = DBuf(c, "fd", queue=wq_)
    if pct is not None:
        for db in (wg, wu, wd):
            db.cons = [pct] * db.nslots
    bfdF = c.bfd[:].bitcast(F32)
    prev_sl = None
    last_down = None
    for iq in range(4):
        act_toks = [None] * 16
        for ic in range(16):
            i64 = iq * 16 + ic
            gld, uld = [], []
            for hf in range(2):
                slg = wg.next()
                gld.append((slg, wg.load(
                    slg, wslab_dst(c, (c.wb, slg * 1024), 8),
                    qg_r[:, hf * 8:(hf + 1) * 8, i64 * 128:(i64 + 1) * 128])))
                slu = wu.next()
                uld.append((slu, wu.load(
                    slu, wslab_dst(c, (c.wb, 2048 + slu * 1024), 8),
                    qu_r[:, hf * 8:(hf + 1) * 8, i64 * 128:(i64 + 1) * 128])))
            gts, uts = [], []
            for (psn, base, lds, res) in (("pA", 0, gld, gts),
                                          ("pB", 2048, uld, uts)):
                ps = getattr(c, psn)
                for kc in range(16):
                    for nt in range(2):
                        hf = kc // 8
                        sl_, ld = lds[hf]
                        st = kc == 0
                        aft = ([puse(c, psn), xc] if st and nt == 0 else []) + [ld]
                        b.do("tensor", lambda eo, kc=kc, st=st, nt=nt,
                             sl_=sl_, base=base, ps=ps: eo.matmul(
                                 ps[:, nt * 512:nt * 512 + 512],
                                 c.wb[:, base + sl_ * 1024 + (kc % 8) * 128:
                                      base + sl_ * 1024 + (kc % 8) * 128 + 128],
                                 c.bfa[:, kc * 1024 + nt * 512:
                                       kc * 1024 + nt * 512 + 512],
                                 start=st, stop=False), after=aft, inc=False)
                for nt in range(2):
                    b.do("tensor", lambda eo, nt=nt, ps=ps: eo.matmul(
                        ps[:, nt * 512:nt * 512 + 512], io.onesrow_b,
                        hix[0:1, nt * 512:nt * 512 + 512], start=False,
                        stop=False))
                    mm = b.do("tensor", lambda eo, nt=nt, ps=ps: eo.matmul(
                        ps[:, nt * 512:nt * 512 + 512], io.onesrow_b,
                        lox[0:1, nt * 512:nt * 512 + 512], start=False,
                        stop=True), inc=True)
                    res.append(mm)
            gsb = c.bfd[:, 0:1024]
            usb = c.bfd[:, 1024:2048]
            ssb = c.bfd[:, 2048:3072]
            ge = b.do("vector", lambda eo, i64=i64, gsb=gsb: eo.tensor_scalar(
                gsb, c.pA[:], io.gscol[:, i64:i64 + 1], None, ALU.mult),
                after=gts + [prev_sl], inc=True)
            prel(c, "pA", ge)
            for sl_, _ in gld:
                wg.done(sl_, ge)
            ue = b.do("vector", lambda eo, i64=i64, usb=usb: eo.tensor_scalar(
                usb, c.pB[:], io.uscol[:, i64:i64 + 1], None, ALU.mult),
                after=uts, inc=True)
            prel(c, "pB", ue)
            for sl_, _ in uld:
                wu.done(sl_, ue)
            sv = b.do("scalar", lambda eo, gsb=gsb, ssb=ssb: eo.activation(
                ssb, gsb, AF.Silu), after=(ge,), inc=True)
            prev_sl = sv
            hv = b.do("vector", lambda eo, ic=ic, ssb=ssb, usb=usb: eo.tensor_mul(
                c.bfc[:, ic * 1024:(ic + 1) * 1024], ssb, usb),
                after=(sv, last_down), inc=True)
            act_toks[ic] = hv
        toks2 = []
        for nt in range(2):
            pnm = "pE" if nt == 0 else "pF"
            ps = getattr(c, pnm)
            mm = None
            for kc in range(16):
                st, sp = kc == 0, kc == 15
                aft = [puse(c, pnm), act_toks[kc]] if st else [act_toks[kc]]
                mm = b.do("tensor", lambda eo, kc=kc, st=st, sp=sp, nt=nt, ps=ps:
                          eo.matmul(ps[0:1, 0:512], io.onescol_b,
                                    c.bfc[:, kc * 1024 + nt * 512:kc * 1024 + nt * 512 + 512],
                                    start=st, stop=sp), after=aft, inc=sp)
            toks2.append(mm)
        m128h = bfdF[0:1, 2048:3072]
        hih = c.bfd[0:1, 6144:7168]
        loh = c.bfd[0:1, 7168:8192]
        for nt in range(2):
            psr = (c.pE if nt == 0 else c.pF)[0:1, 0:512]
            t = b.do("vector", lambda eo, nt=nt, psr=psr: eo.tensor_scalar(
                m128h[0:1, nt * 512:nt * 512 + 512], psr, -128.0, None,
                ALU.mult), after=toks2 + [puse(c, "hrow")], inc=True)
            prel(c, "pE" if nt == 0 else "pF", t)
        b.do("vector", lambda eo: eo.tensor_copy(hih, m128h))
        hc_ = b.do("vector", lambda eo: eo.tensor_tensor(loh, m128h, hih,
                   ALU.subtract), inc=True)
        for hc in range(16):
            sl2 = wd.next()
            ld2 = wd.load(sl2, wslab_dst(c, (c.wb, 4096 + sl2 * 2048), 16),
                          qd_r[:, iq * 16:(iq + 1) * 16, hc * 128:(hc + 1) * 128])
            toks3 = []
            ads = []
            for kc in range(16):
                for nt in range(2):
                    pn = "pC" if nt == 0 else "pD"
                    ps = getattr(c, pn)
                    st = kc == 0
                    aft = ([ld2, puse(c, pn), hc_] if st else []) + [act_toks[kc]]
                    b.do("tensor", lambda eo, kc=kc, st=st, nt=nt, sl2=sl2,
                         ps=ps: eo.matmul(
                             ps[:, 0:512],
                             c.wb[:, 4096 + sl2 * 2048 + kc * 128:
                                  4096 + sl2 * 2048 + (kc + 1) * 128],
                             c.bfc[:, kc * 1024 + nt * 512:
                                   kc * 1024 + nt * 512 + 512],
                             start=st, stop=False), after=aft, inc=False)
            for nt in range(2):
                pn = "pC" if nt == 0 else "pD"
                ps = getattr(c, pn)
                b.do("tensor", lambda eo, nt=nt, ps=ps: eo.matmul(
                    ps[:, 0:512], io.onesrow_b,
                    hih[0:1, nt * 512:nt * 512 + 512], start=False, stop=False))
                mm = b.do("tensor", lambda eo, nt=nt, ps=ps: eo.matmul(
                    ps[:, 0:512], io.onesrow_b,
                    loh[0:1, nt * 512:nt * 512 + 512], start=False, stop=True),
                    inc=True)
                toks3.append(mm)
                ad = b.do("vector", lambda eo, ps=ps, hc=hc, nt=nt:
                          eo.scalar_tensor_tensor(
                              c.x[:, hc * 1024 + nt * 512:hc * 1024 + nt * 512 + 512],
                              ps[:, 0:512], io.dscol[:, hc:hc + 1],
                              c.x[:, hc * 1024 + nt * 512:hc * 1024 + nt * 512 + 512],
                              ALU.mult, ALU.add), after=(mm,), inc=True)
                prel(c, pn, ad)
                ads.append(ad)
            wd.done(sl2, ads[-1])
            if hc == 15:
                last_down = toks3[-1]
                prel(c, "hrow", toks3[-1])
    b.barrier()


def emit_out(c, io, o_out):
    b = c.b
    stage = c.bfc[:].bitcast(F32)
    prev_st = [None, None]
    for tt in range(8):
        sl = tt % 2
        stg = stage[:, sl * 2048:(sl + 1) * 2048]
        evs = None
        for kc in range(16):
            pn = "pE" if kc % 2 == 0 else "pF"
            pf = getattr(c, pn)[:, 0:128]
            tr = b.do("tensor", lambda eo, pf=pf, kc=kc, tt=tt: eo.transpose(
                pf, c.x[:, kc * 1024 + tt * 128:kc * 1024 + (tt + 1) * 128],
                io.identf), after=(puse(c, pn),), inc=True)
            evs = b.do("vector", lambda eo, pf=pf, kc=kc, stg=stg: eo.tensor_copy(
                stg[:, kc * 128:(kc + 1) * 128], pf),
                after=(tr, prev_st[sl]), inc=True)
            prel(c, pn, evs)
        st_ = b.do("sync", lambda eo, tt=tt, stg=stg: eo.dma_start(
            o_out[tt * 128:(tt + 1) * 128, :], stg), after=(evs,),
            dma=b.S("oo%d" % sl))
        prev_st[sl] = st_
    b.do("sync", lambda eo: eo.nop(),
         after=[(b.S("oo0"), b.S("oo0").n), (b.S("oo1"), b.S("oo1").n)])


def build_launch1():
    nc = bass.Bass("TRN2", target_bir_lowering=False, debug=False)
    bld = Bld(nc)
    io = IO()
    io.h_in = nc.dram_tensor("h_in", [LOC, D], F32, kind="ExternalInput")
    decl_common_inputs(nc, io)
    decl_l0_inputs(nc, io)
    decl_moe_inputs(nc, io)
    DT = nc.dram_tensor
    io.wqkv = DT("wqkv", [D, 3 * D], F32, kind="ExternalInput")
    io.bqkv = _sb_const(nc, "bqkv", [128, 48], F32)
    io.onescol_f = _sb_const(nc, "onescol_f", [128, 1], F32)
    kvout = DT("kvout", [4096, OWN], BF16, kind="ExternalOutput")
    x1_d = DT("x1_d", [D, OWN], F32, kind="ExternalOutput")
    xn1_d = DT("xn1_d", [D, OWN], BF16, kind="ExternalOutput")
    q_d = DT("q_d", [D, OWN], BF16, kind="ExternalOutput")
    c = alloc_common(nc, bld)
    c.io = io
    for v in (1e-5, 1.0 / 128.0, 1e-6, 1.28e-4):
        reg_const(c, v)
    emit_const_init(c)
    emit_const_loads(c)
    bld.barrier()
    emit_p0(c, io)
    emit_l0_kvq(c, io)
    emit_l0_scores(c, io)
    emit_l0_out(c, io)
    emit_moe(c, io)
    emit_ln1(c, io)
    emit_l1_qkv(c, io, kvout)
    b = bld
    for kc in range(16):
        b.do("sync", lambda eo, kc=kc: eo.dma_start(
            x1_d[:].rearrange("(k p) t -> k p t", p=128)[kc],
            c.x[:, kc * 1024:(kc + 1) * 1024]), dma=b.S("o0"))
        b.do("sync", lambda eo, kc=kc: eo.dma_start(
            xn1_d[:].rearrange("(k p) t -> k p t", p=128)[kc],
            c.bfa[:, kc * 1024:(kc + 1) * 1024]), dma=b.S("o0"))
        t = b.do("sync", lambda eo, kc=kc: eo.dma_start(
            q_d[:].rearrange("(k p) t -> k p t", p=128)[kc],
            c.bfb[:, kc * 1024:(kc + 1) * 1024]), dma=b.S("o0"))
    b.do("sync", lambda eo, t=t: eo.nop(), after=(t,))
    bld.finalize()
    return nc


def build_launch2():
    nc = bass.Bass("TRN2", target_bir_lowering=False, debug=False)
    bld = Bld(nc)
    io = IO()
    decl_common_inputs(nc, io)
    decl_l1_inputs(nc, io, "l2")
    DT = nc.dram_tensor
    io.onesrow_b = _sb_const(nc, "onesrow_b", [1, 128], BF16)
    io.onesrow_f = _sb_const(nc, "onesrow_f", [1, 128], F32)
    io.onescol_b = _sb_const(nc, "onescol_b", [128, 1], BF16)
    x1_d = DT("x1_d", [D, OWN], F32, kind="ExternalInput")
    xn1_d = DT("xn1_d", [D, OWN], BF16, kind="ExternalInput")
    q_d = DT("q_d", [D, OWN], BF16, kind="ExternalInput")
    kvin = DT("kvin", [4096, OWN], BF16, kind="ExternalInput")
    agx = DT("agx", [8 * 4096, OWN], BF16, kind="ExternalInput")
    o_out = DT("o_out", [OWN, D], F32, kind="ExternalOutput")
    c = alloc_common(nc, bld)
    c.io = io
    reg_const(c, 2.0 ** -32)
    reg_const(c, -1.0)
    reg_const(c, -22.18070977791825)
    io.qgB = DT("qgB", [D, IFF], BF16)
    io.quB = DT("quB", [D, IFF], BF16)
    io.qdB = DT("qdB", [IFF, D], BF16)
    emit_const_init(c)
    emit_const_loads(c)
    b = bld
    pc = None
    for (src, dst, nch) in ((io.qgT, io.qgB, 8), (io.quT, io.quB, 8),
                            (io.qdT, io.qdB, 8)):
        rows = src.shape[0] // nch
        for ch in range(nch):
            pc = b.do("gpsimd", lambda eo, src=src, dst=dst, ch=ch, rows=rows:
                      eo.dma_start(dst[ch * rows:(ch + 1) * rows, :],
                                   src[ch * rows:(ch + 1) * rows, :]),
                      dma=b.S("bgc"))
    c.pc_tok = pc
    t1 = b.do("sync", lambda eo: eo.dma_start(
        c.x[:].rearrange("p (k t) -> p k t", t=1024),
        x1_d[:].rearrange("(k p) t -> p k t", p=128)), dma=b.S("i0"))
    t2 = b.do("sync", lambda eo: eo.dma_start(
        c.bfa[:].rearrange("p (k t) -> p k t", t=1024),
        xn1_d[:].rearrange("(k p) t -> p k t", p=128)), dma=b.S("i1"))
    t3 = b.do("sync", lambda eo: eo.dma_start(
        c.bfb[:].rearrange("p (k t) -> p k t", t=1024),
        q_d[:].rearrange("(k p) t -> p k t", p=128)), dma=b.S("i2"))
    b.do("sync", lambda eo: eo.nop(), after=(t1, t2, t3))
    bld.barrier()
    emit_l1_ffn(c, io)
    emit_l1_attn(c, io, agx, kvin)
    emit_l1_oproj(c, io)
    emit_out(c, io, o_out)
    bld.finalize()
    return nc


def build_ln1_test():
    nc = bass.Bass("TRN2", target_bir_lowering=False, debug=False)
    bld = Bld(nc)
    io = IO()
    decl_common_inputs(nc, io)
    io.onescol_f = _sb_const(nc, "onescol_f", [128, 1], F32)
    io.onesrow_b = _sb_const(nc, "onesrow_b", [1, 128], BF16)
    DT = nc.dram_tensor
    x_in = DT("x_in", [D, OWN], F32, kind="ExternalInput")
    o_xn = DT("o_xn", [D, OWN], BF16, kind="ExternalOutput")
    c = alloc_common(nc, bld)
    c.io = io
    reg_const(c, 1e-5)
    emit_const_init(c)
    emit_const_loads(c)
    b = bld
    t1 = b.do("sync", lambda eo: eo.dma_start(
        c.x[:].rearrange("p (k t) -> p k t", t=1024),
        x_in[:].rearrange("(k p) t -> p k t", p=128)), dma=b.S("i0"))
    b.do("sync", lambda eo: eo.nop(), after=(t1,))
    bld.barrier()
    emit_ln1(c, io)
    for kc in range(16):
        t = b.do("sync", lambda eo, kc=kc: eo.dma_start(
            o_xn[:].rearrange("(k p) t -> k p t", p=128)[kc],
            c.bfa[:, kc * 1024:(kc + 1) * 1024]), dma=b.S("o0"))
    b.do("sync", lambda eo, t=t: eo.nop(), after=(t,))
    bld.finalize()
    return nc


def build_fused():
    nc = bass.Bass("TRN2", target_bir_lowering=False, debug=False)
    bld = Bld(nc)
    io = IO()
    io.h_in = nc.dram_tensor("h_in", [LOC, D], F32, kind="ExternalInput")
    decl_common_inputs(nc, io)
    decl_l0_inputs(nc, io)
    decl_moe_inputs(nc, io)
    decl_l1_inputs(nc, io, "fused")
    DT = nc.dram_tensor
    io.wqkv = DT("wqkv", [D, 3 * D], F32, kind="ExternalInput")
    io.bqkv = _sb_const(nc, "bqkv", [128, 48], F32)
    io.qgB = DT("qgB", [D, IFF], BF16)
    io.quB = DT("quB", [D, IFF], BF16)
    io.qdB = DT("qdB", [IFF, D], BF16)
    ag_in = DT("ag_in", [4096, OWN], BF16)
    agk_out = DT("agk_out", [8 * 2048, OWN], BF16)
    agv_out = DT("agv_out", [8 * 2048, OWN], BF16)
    o_out = DT("o_out", [OWN, D], F32, kind="ExternalOutput")
    c = alloc_common(nc, bld)
    c.io = io
    for v in (1e-5, 1.0 / 128.0, 1e-6, 1.28e-4):
        reg_const(c, v)
    reg_const(c, 2.0 ** -32)
    reg_const(c, -1.0)
    reg_const(c, -22.18070977791825)
    emit_const_init(c)
    emit_const_loads(c)
    bld.barrier()
    b = bld
    for (src, dst, nch) in ((io.qgT, io.qgB, 8), (io.quT, io.quB, 8),
                            (io.qdT, io.qdB, 8)):
        rows = src.shape[0] // nch
        for ch in range(nch):
            b.bg(lambda eo, src=src, dst=dst, ch=ch, rows=rows: eo.dma_start(
                dst[ch * rows:(ch + 1) * rows, :],
                src[ch * rows:(ch + 1) * rows, :]))
    emit_p0(c, io)
    emit_l0_kvq(c, io)
    emit_l0_scores(c, io)
    emit_l0_out(c, io)
    emit_moe(c, io)
    while bld.bg_dmas:
        bld.drip()
    c.pc_tok = (bld.S("bgc"), bld.S("bgc").n)
    emit_ln1(c, io)

    def kick_k(aft):
        b.do("gpsimd", lambda eo: eo.collective_compute(
            "AllGather", ALU.bypass, replica_groups=[list(range(8))],
            ins=[ag_in[0:2048, :].opt()], outs=[agk_out.ap().opt()]),
            after=aft, inc=True)

    def kick(aft):
        b.do("gpsimd", lambda eo: eo.collective_compute(
            "AllGather", ALU.bypass, replica_groups=[list(range(8))],
            ins=[ag_in[2048:4096, :].opt()], outs=[agv_out.ap().opt()]),
            after=aft, inc=True)
    emit_l1_qkv(c, io, ag_in, kick=kick, kick_k=kick_k)
    emit_l1_ffn(c, io)
    bld.barrier()
    emit_l1_attn(c, io, (agk_out, agv_out), ag_in)
    emit_l1_oproj(c, io)
    emit_out(c, io, o_out)
    bld.finalize()
    return nc




def _chunks_of(core):
    return core, 7 - core


def _h_rows(core, h_full):
    a, bb = _chunks_of(core)
    rows = np.zeros((LOC, D), np.float32)
    for i, (b_, s0) in enumerate([(0, a * CH), (1, bb * CH)]):
        halo = np.zeros((CH, D), np.float32)
        if s0 >= CH:
            halo = h_full[b_, s0 - CH:s0]
        rows[i * 1024:(i + 1) * 1024] = np.concatenate(
            [halo, h_full[b_, s0:s0 + CH]], 0)
    return np.ascontiguousarray(rows)


def _positions(core):
    a, bb = _chunks_of(core)
    own = np.concatenate([np.arange(a * CH, (a + 1) * CH),
                          np.arange(bb * CH, (bb + 1) * CH)])
    loc = np.concatenate([np.arange(a * CH - CH, (a + 1) * CH),
                          np.arange(bb * CH - CH, (bb + 1) * CH)])
    return own, loc


def _rope_tables(pos):
    n = len(pos)
    inv = 1.0 / (10000.0 ** (np.arange(0, 64, 2) / 64.0))
    ang = np.maximum(pos, 0)[None, :] * inv[:, None]
    C = np.ones((128, n), np.float64)
    S = np.zeros((128, n), np.float64)
    C[0:32] = np.cos(ang)
    C[32:64] = np.cos(ang)
    S[0:32] = -np.sin(ang)
    S[32:64] = np.sin(ang)
    return C.astype(BF), S.astype(BF)


def _rperm():
    R = np.zeros((128, 128), np.float32)
    for j in range(32):
        R[j, 32 + j] = 1.0
        R[32 + j, j] = 1.0
    return R.astype(BF)


def _l0_erow(core):
    a, bb = _chunks_of(core)
    out = np.zeros((1, 5120), np.float64)
    kj = np.arange(640)
    for cq, st in ((0, a * CH), (1, bb * CH)):
        for j in range(4):
            exists = (st - CH + j * 128 + kj) >= 0
            out[0, (cq * 4 + j) * 640:(cq * 4 + j) * 640 + 640] = np.where(
                exists, 0.0, NEG)
    return out.astype(BF)


def _l0_winm():
    qi = np.arange(128)[:, None]
    kj = np.arange(128)[None, :]
    first = np.where(kj > qi, 0.0, NEG)
    last = np.where(kj <= qi, 0.0, NEG)
    return np.concatenate([first, last], 1).astype(BF)


def _aprime(core):
    a, bb = _chunks_of(core)
    out = np.zeros((128, 1024), np.float64)
    p = np.arange(128)
    for cq, own_chunk in ((0, a), (1, bb)):
        for h in range(16):
            s = ALIBI_SLOPES[h]
            for r in range(8):
                for jm in range(4):
                    J = r * 4 + jm
                    jg = (r * 4 + jm) if cq == 0 else ((7 - r) * 4 + jm)
                    if jg >= 4 * own_chunk:
                        continue
                    out[:, (cq * 16 + h) * 32 + J] = np.exp(
                        s * (128.0 * jg + p - 64.0 - 512.0 * own_chunk))
    return out.astype(np.float32)


def _l1_consts():
    f1 = np.zeros((128, 16), np.float64)
    bmat = np.zeros((128, 64), np.float64)
    p = np.arange(128)
    for h in range(16):
        s = ALIBI_SLOPES[h]
        f1[:, h] = np.exp(s * (p - 64.0))
        for jt in range(4):
            bmat[:, h * 4 + jt] = np.exp(-s * 128.0 * jt)
    tri = (np.arange(128)[:, None] <= np.arange(128)[None, :]).astype(BF)
    f1g = np.ones((128, 64), np.float64)
    for h in range(5, 16):
        s_ = ALIBI_SLOPES[h]
        for jl in range(4):
            f1g[:, h * 4 + jl] = np.exp(s_ * (np.arange(128) - 64.0 + 128.0 * jl))
    return {"f1m": f1.astype(np.float32), "bmat": bmat.astype(np.float32),
            "tri": tri, "ones128": np.ones((128, 128), BF),
            "onescol_f": np.ones((128, 1), np.float32),
            "f1g": f1g.astype(np.float32)}


_NC_CACHE = {}


def _get_nc(which):
    if which not in _NC_CACHE:
        _NC_CACHE[which] = (build_launch1 if which == 1 else build_launch2)()
    return _NC_CACHE[which]


def kernel_unfused(L0_attn_h, L0_attn_wq_comp, L0_attn_wq_up, L0_attn_wk, L0_attn_wv,
           L0_attn_q_norm_w, L0_attn_k_norm_w, L0_attn_sink_logit,
           L0_attn_out_w1, L0_attn_out_w2, L0_ffn_router_weight,
           L0_ffn_gate_up_proj, L0_ffn_down_proj, L0_ffn_sh_gate_w,
           L0_ffn_sh_up_w, L0_ffn_sh_down_w, L0_ffn_shared_gate_w,
           L1_attn_w_qkv, L1_attn_b_qkv, L1_attn_w_proj, L1_attn_b_proj,
           L1_ffn_gate_w, L1_ffn_gate_scale, L1_ffn_up_w, L1_ffn_up_scale,
           L1_ffn_down_w, L1_ffn_down_scale):
    h_full = np.asarray(L0_attn_h, np.float32)
    C = np.ascontiguousarray
    shared1 = {
        "wqc": C(np.asarray(L0_attn_wq_comp, np.float32)),
        "wqu": C(np.asarray(L0_attn_wq_up, np.float32)),
        "wk": C(np.asarray(L0_attn_wk, np.float32)),
        "wv": C(np.asarray(L0_attn_wv, np.float32)),
        "w1": C(np.asarray(L0_attn_out_w1, np.float32)),
        "w2": C(np.asarray(L0_attn_out_w2, np.float32)),
        "qnw_row": C(np.asarray(L0_attn_q_norm_w, np.float32).reshape(1, 128)),
        "knw_row": C(np.asarray(L0_attn_k_norm_w, np.float32).reshape(1, 128)),
        "sink_row": C(np.asarray(L0_attn_sink_logit, np.float32).reshape(1, 16)),
        "routerT": C(np.asarray(L0_ffn_router_weight, np.float32).T),
        "gupT": C(np.asarray(L0_ffn_gate_up_proj, np.float32).transpose(0, 2, 1)),
        "dpT": C(np.asarray(L0_ffn_down_proj, np.float32).transpose(0, 2, 1)),
        "shgT": C(np.asarray(L0_ffn_sh_gate_w, np.float32).T),
        "shuT": C(np.asarray(L0_ffn_sh_up_w, np.float32).T),
        "shdT": C(np.asarray(L0_ffn_sh_down_w, np.float32).T),
        "sgT": C(np.asarray(L0_ffn_shared_gate_w, np.float32).T),
        "wqkv": C(np.asarray(L1_attn_w_qkv, np.float32)),
        "bqkv": C(np.asarray(L1_attn_b_qkv, np.float32).reshape(48, 128).T),
        "winm": _l0_winm(),
        "identb": np.eye(128).astype(BF),
        "identf": np.eye(128, dtype=np.float32),
        "rperm": _rperm(),
        "onescol_b": np.ones((128, 1), BF),
        "onesrow_b": np.ones((1, 128), BF),
        "onesrow_f": np.ones((1, 128), np.float32),
        "onescol_f": np.ones((128, 1), np.float32),
    }
    in1 = []
    for cc in range(NCORE):
        own, loc = _positions(cc)
        cq_, sq_ = _rope_tables(own)
        ck_, sk_ = _rope_tables(loc)
        d = {"h_in": _h_rows(cc, h_full), "cqd": cq_, "sqd": sq_,
             "ckd": ck_, "skd": sk_, "erow": _l0_erow(cc)}
        d.update(shared1)
        in1.append(d)
    nc1 = _get_nc(1)
    res1 = run_bass_kernel_spmd(nc1, in1, core_ids=list(range(NCORE)))
    agx = np.concatenate([res1.results[cc]["kvout"] for cc in range(NCORE)], 0)
    lc = _l1_consts()
    shared2 = {
        "wproj": C(np.asarray(L1_attn_w_proj, np.float32)),
        "bproj": C(np.asarray(L1_attn_b_proj, np.float32).reshape(16, 128).T),
        "qgT": C(np.asarray(L1_ffn_gate_w, np.int32).T),
        "quT": C(np.asarray(L1_ffn_up_w, np.int32).T),
        "qdT": C(np.asarray(L1_ffn_down_w, np.int32).T),
        "gscol": C(np.asarray(L1_ffn_gate_scale, np.float32).reshape(64, 128).T),
        "uscol": C(np.asarray(L1_ffn_up_scale, np.float32).reshape(64, 128).T),
        "dscol": C(np.asarray(L1_ffn_down_scale, np.float32).reshape(16, 128).T),
        "agx": agx,
        "identb": shared1["identb"], "identf": shared1["identf"],
        "onesrow_b": shared1["onesrow_b"], "onesrow_f": shared1["onesrow_f"],
        "onescol_b": shared1["onescol_b"],
    }
    shared2.update(lc)
    in2 = []
    for cc in range(NCORE):
        d = {"x1_d": res1.results[cc]["x1_d"],
             "xn1_d": res1.results[cc]["xn1_d"],
             "q_d": res1.results[cc]["q_d"],
             "kvin": res1.results[cc]["kvout"],
             "aprd": _aprime(cc)}
        d.update(shared2)
        in2.append(d)
    nc2 = _get_nc(2)
    res2 = run_bass_kernel_spmd(nc2, in2, core_ids=list(range(NCORE)))
    out = np.zeros((B, T, D), np.float32)
    for cc in range(NCORE):
        a, bb = _chunks_of(cc)
        o = res2.results[cc]["o_out"]
        out[0, a * CH:(a + 1) * CH] = o[0:512]
        out[1, bb * CH:(bb + 1) * CH] = o[512:1024]
    return out




def kernel_fused(**inputs):
    h_full = np.asarray(inputs["L0_attn_h"], np.float32)
    C = np.ascontiguousarray
    lc = _l1_consts()
    shared = {
        "wqc": C(np.asarray(inputs["L0_attn_wq_comp"], np.float32)),
        "wqu": C(np.asarray(inputs["L0_attn_wq_up"], np.float32)),
        "wk": C(np.asarray(inputs["L0_attn_wk"], np.float32)),
        "wv": C(np.asarray(inputs["L0_attn_wv"], np.float32)),
        "w1": C(np.asarray(inputs["L0_attn_out_w1"], np.float32)),
        "w2": C(np.asarray(inputs["L0_attn_out_w2"], np.float32)),
        "qnw_row": C(np.asarray(inputs["L0_attn_q_norm_w"], np.float32).reshape(1, 128)),
        "knw_row": C(np.asarray(inputs["L0_attn_k_norm_w"], np.float32).reshape(1, 128)),
        "sink_row": C(np.asarray(inputs["L0_attn_sink_logit"], np.float32).reshape(1, 16)),
        "routerT": C(np.asarray(inputs["L0_ffn_router_weight"], np.float32).T),
        "gupT": C(np.asarray(inputs["L0_ffn_gate_up_proj"], np.float32).transpose(0, 2, 1)),
        "dpT": C(np.asarray(inputs["L0_ffn_down_proj"], np.float32).transpose(0, 2, 1)),
        "shgT": C(np.asarray(inputs["L0_ffn_sh_gate_w"], np.float32).T),
        "shuT": C(np.asarray(inputs["L0_ffn_sh_up_w"], np.float32).T),
        "shdT": C(np.asarray(inputs["L0_ffn_sh_down_w"], np.float32).T),
        "sgT": C(np.asarray(inputs["L0_ffn_shared_gate_w"], np.float32).T),
        "wqkv": C(np.asarray(inputs["L1_attn_w_qkv"], np.float32)),
        "bqkv": C(np.asarray(inputs["L1_attn_b_qkv"], np.float32).reshape(48, 128).T),
        "wproj": C(np.asarray(inputs["L1_attn_w_proj"], np.float32)),
        "bproj": C(np.asarray(inputs["L1_attn_b_proj"], np.float32).reshape(16, 128).T),
        "qgT": C(np.asarray(inputs["L1_ffn_gate_w"], np.int32).T),
        "quT": C(np.asarray(inputs["L1_ffn_up_w"], np.int32).T),
        "qdT": C(np.asarray(inputs["L1_ffn_down_w"], np.int32).T),
        "gscol": C(np.asarray(inputs["L1_ffn_gate_scale"], np.float32).reshape(64, 128).T),
        "uscol": C(np.asarray(inputs["L1_ffn_up_scale"], np.float32).reshape(64, 128).T),
        "dscol": C(np.asarray(inputs["L1_ffn_down_scale"], np.float32).reshape(16, 128).T),
        "winm": _l0_winm(),
        "identb": np.eye(128).astype(BF),
        "identf": np.eye(128, dtype=np.float32),
        "rperm": _rperm(),
        "onescol_b": np.ones((128, 1), BF),
        "onesrow_b": np.ones((1, 128), BF),
        "onesrow_f": np.ones((1, 128), np.float32),
    }
    shared.update(lc)
    ins = []
    for cc in range(NCORE):
        own, loc = _positions(cc)
        cq_, sq_ = _rope_tables(own)
        ck_, sk_ = _rope_tables(loc)
        d = {"h_in": _h_rows(cc, h_full), "cqd": cq_, "sqd": sq_,
             "ckd": ck_, "skd": sk_, "erow": _l0_erow(cc),
             "aprd": _aprime(cc)}
        d.update(shared)
        ins.append(d)
    if "fused" not in _NC_CACHE:
        _NC_CACHE["fused"] = build_fused()
    res = run_bass_kernel_spmd(_NC_CACHE["fused"], ins, core_ids=list(range(NCORE)))
    out = np.zeros((B, T, D), np.float32)
    for cc in range(NCORE):
        a, bb = _chunks_of(cc)
        o = res.results[cc]["o_out"]
        out[0, a * CH:(a + 1) * CH] = o[0:512]
        out[1, bb * CH:(bb + 1) * CH] = o[512:1024]
    return out


def kernel(**inputs):
    return kernel_fused(**inputs)
